# version 22
# baseline (speedup 1.0000x reference)
import numpy as np

import concourse.bass as bass
import concourse.mybir as mybir
from concourse.bass_utils import run_bass_kernel_spmd

F32 = mybir.dt.float32
BF16 = mybir.dt.bfloat16
I32 = mybir.dt.int32
AX = mybir.AxisListType.X
Alu = mybir.AluOpType
Act = mybir.ActivationFunctionType

D = 1024
DC = D // 128
FF = 3072
IC = FF // 128
QMAX = 32767.0
EPS_LN = 1e-5
EPS_Q = 1e-12

ENGINES = ("sync", "tensor", "vector", "scalar", "gpsimd")


class Sem:

    def __init__(self, h):
        self.h = h
        self.n = 0

    def inc(self, k=1):
        self.n += k
        return self.n


class EngSem(Sem):
    pass


class VSem:

    def __init__(self, name):
        self.name = name
        self.n = 0
        self.map = {}
        self._cur = {}

    def inc_record(self, eng_sem, shared_val, k=1):
        self.n += k
        self._cur = dict(self._cur)
        self._cur[id(eng_sem)] = (eng_sem, shared_val)
        self.map[self.n] = self._cur

    def translate(self, v):
        assert v in self.map, f"VSem {self.name}: wait on unrecorded count {v} (max {self.n})"
        return list(self.map[v].values())


class Rec:
    def __init__(self, eng_sems):
        self.ops = {e: [] for e in ENGINES}
        self.eng_sems = eng_sems

    @staticmethod
    def _resolve(waits):
        tw = {}
        for (s, v) in waits:
            if v <= 0:
                continue
            if isinstance(s, VSem):
                pairs = s.translate(v)
            else:
                pairs = [(s, v)]
            for sh_sem, sv in pairs:
                tw[id(sh_sem)] = (sh_sem.h, max(tw.get(id(sh_sem), (None, 0))[1], sv))
        return list(tw.values())

    def op(self, eng, waits, emit, incs):
        waits = [(s, v) for (s, v) in waits]
        real_incs = []
        v_incs = [s for (s, k) in incs if isinstance(s, VSem)]
        for (s, k) in incs:
            if not isinstance(s, VSem):
                real_incs.append((s.h, k))
                s.inc(k)
        es = self.eng_sems[eng]
        if v_incs:
            es.inc(1)
            for s in v_incs:
                s.inc_record(es, es.n)
            real_incs.append((es.h, 1))

        def fn(h):
            for sh, v in Rec._resolve(waits):
                h.wait_ge(sh, v)
            ins = emit(h)
            for sh, k in real_incs:
                ins = ins.then_inc(sh, k)
        self.ops[eng].append(fn)

    def wait(self, eng, waits):
        waits = [(s, v) for (s, v) in waits]

        def fn(h):
            for sh, v in Rec._resolve(waits):
                h.wait_ge(sh, v)
        self.ops[eng].append(fn)


class SlotAlloc:
    def __init__(self, tensor, n_slots, slot_elems, name):
        self.t = tensor
        self.free = list(range(n_slots))
        self.se = slot_elems
        self.name = name

    def alloc(self, k=1):
        assert len(self.free) >= k, f"{self.name}: need {k} slots, have {len(self.free)}"
        ids = [self.free.pop(0) for _ in range(k)]
        return [self.t[:, s * self.se:(s + 1) * self.se] for s in ids], ids

    def alloc2(self):
        for j in range(len(self.free) - 1):
            if self.free[j + 1] == self.free[j] + 1:
                a = self.free.pop(j + 1)
                b = self.free.pop(j)
                return self.t[:, b * self.se:(a + 1) * self.se], [b, a]
        raise AssertionError(f"{self.name}: no contiguous pair")

    def release(self, ids):
        self.free = sorted(self.free + list(ids))


def build_program(S, ncores, upto=99):
    T = S // 128
    NS = S // 512 if S >= 512 else 1
    SB = min(512, S)
    NBLK = max(S // SB, 1)
    SPB = SB // 128
    assert S % 512 == 0 and S <= 2048

    nc = bass.Bass("TRN2", target_bir_lowering=False, debug=False)

    x_d = nc.dram_tensor("x", [S, D], F32, kind="ExternalInput").ap()
    wqT_d = nc.dram_tensor("wqT", [D, D], F32, kind="ExternalInput").ap()
    wkT_d = nc.dram_tensor("wkT", [D, D], F32, kind="ExternalInput").ap()
    wvT_d = nc.dram_tensor("wvT", [D, D], F32, kind="ExternalInput").ap()
    woT_d = nc.dram_tensor("woT", [D, D], F32, kind="ExternalInput").ap()
    wmod_d = nc.dram_tensor("wmod", [D, 6 * D], F32, kind="ExternalInput").ap()
    ccol_d = nc.dram_tensor("ccol", [128, DC], F32, kind="ExternalInput").ap()
    bqcol_d = nc.dram_tensor("bqcol", [128, DC], F32, kind="ExternalInput").ap()
    bkcol_d = nc.dram_tensor("bkcol", [128, DC], F32, kind="ExternalInput").ap()
    bvrow_d = nc.dram_tensor("bvrow", [1, D], F32, kind="ExternalInput").ap()
    borow_d = nc.dram_tensor("borow", [1, D], F32, kind="ExternalInput").ap()

    y_d = nc.dram_tensor("y", [S, D], F32, kind="ExternalOutput").ap()

    wup_o_d = nc.dram_tensor("wup_o", [FF, D], F32, kind="ExternalInput").ap()
    wg_o_d = nc.dram_tensor("wg_o", [FF, D], F32, kind="ExternalInput").ap()
    wdn_o_d = nc.dram_tensor("wdn_o", [D, FF], F32, kind="ExternalInput").ap()

    x1_d = nc.dram_tensor("x1buf", [S, D], F32).ap()
    qTs_d = nc.dram_tensor("qTspill", [D, S], BF16).ap()
    upraw_d = nc.dram_tensor("upraw", [IC, 128, S], BF16).ap()
    wup_bd = nc.dram_tensor("wup_b", [D, FF], BF16).ap()
    wg_bd = nc.dram_tensor("wg_b", [D, FF], BF16).ap()
    wdn_bd = nc.dram_tensor("wdn_b", [FF, D], BF16).ap()
    cc_in = [nc.dram_tensor(f"cc_in{i}", [1, 2], F32).ap() for i in range(4)]
    cc_out = [nc.dram_tensor(f"cc_out{i}", [1, 2], F32).ap() for i in range(4)]

    from contextlib import ExitStack
    st = ExitStack()
    sbt = lambda n_, sh, dt: st.enter_context(nc.sbuf_tensor(n_, sh, dt))
    eng_sems = {e: EngSem(st.enter_context(nc.semaphore("p_" + e))) for e in ENGINES}
    rec = Rec(eng_sems)
    sems = {}
    REAL = set()

    def sem(name):
        if name not in sems:
            if name in REAL:
                sems[name] = Sem(st.enter_context(nc.semaphore(name)))
            else:
                sems[name] = VSem(name)
        return sems[name]

    def rsem(name):
        if name not in sems:
            REAL.add(name)
        return sem(name)

    BSLOT = 2048
    NBS = 36
    FSLOT = 1024
    NFS = 10
    arena_b = sbt("arena_b", [128, NBS * BSLOT], BF16)
    arena_f = sbt("arena_f", [128, NFS * FSLOT], F32)
    AB = SlotAlloc(arena_b, NBS, BSLOT, "bf16")
    AF = SlotAlloc(arena_f, NFS, FSLOT, "f32")
    (trashb_s, trashb_ids) = AB.alloc(1)
    trashb = trashb_s[0][:, 0:D]

    ident = sbt("ident", [128, 128], BF16)
    onesf = sbt("onesf", [1, 128], F32)
    ccol = sbt("ccol_sb", [128, DC], F32)
    silu_b = sbt("silu_b", [128, DC], BF16)
    bqcol = sbt("bqcol_sb", [128, DC], F32)
    bkcol = sbt("bkcol_sb", [128, DC], F32)
    ca1col = sbt("ca1col", [128, DC], F32)
    sacol = sbt("sacol", [128, DC], F32)
    bvb = sbt("bvb", [128, D], BF16)
    gab = sbt("gab", [128, D], BF16)
    gabob = sbt("gabob", [128, D], BF16)
    cb1b = sbt("cb1b", [128, D], BF16)
    sbb = sbt("sbb", [128, D], BF16)
    gbb = sbt("gbb", [128, D], BF16)
    bob = sbt("bob", [128, D], BF16)
    sumc = sbt("sumc", [128, T], F32)
    sqc = sbt("sqc", [128, T], F32)
    stat1 = sbt("stat1", [128, T], F32)
    stat2 = sbt("stat2", [128, T], F32)
    rstdc = sbt("rstdc", [128, T], F32)
    nbc = sbt("nbc", [128, T], F32)
    bcolc = sbt("bcolc", [128, T], F32)
    rsc = sbt("rsc", [128, T], F32)
    rrs = sbt("rrs", [128, T], F32)
    scal = sbt("scal", [1, 64], F32)
    bsc = sbt("bsc", [128, 16], F32)
    redrow = sbt("redrow", [1, 128], F32)
    zeroc = sbt("zeroc", [128, 1], F32)
    scr1b = sbt("scr1b", [128, D], BF16)
    trash = scr1b
    qscr = sbt("qscr", [128, 1024], I32)

    (modrow_slots, modrow_ids) = AF.alloc(6)
    assert modrow_ids == [0, 1, 2, 3, 4, 5]
    modrow = arena_f[0:1, 0:6 * D]

    ps_all = st.enter_context(nc.psum_tensor("ps", [128, 4096], F32))

    def bank(b, w=512):
        return ps_all[:, b * 512: b * 512 + w]

    def _finish(waits):
        rFin = rsem("rFin")
        rec.op("sync", list(waits),
               lambda h: h.dma_start(out=y_d[0:128, :], in_=arena_f[:, 0:D]),
               [(rFin, 16)])
        rec.wait("sync", [(rFin, 16)])
        with nc.Block() as block:
            block.sync(lambda h: [f(h) for f in rec.ops["sync"]])
            block.tensor(lambda h: [f(h) for f in rec.ops["tensor"]])
            block.vector(lambda h: [f(h) for f in rec.ops["vector"]])
            block.scalar(lambda h: [f(h) for f in rec.ops["scalar"]])
            block.gpsimd(lambda h: [f(h) for f in rec.ops["gpsimd"]])
        st.close()
        return nc

    def T_(waits, emit, incs=()):
        rec.op("tensor", waits, emit, incs)

    def V_(waits, emit, incs=()):
        rec.op("vector", waits, emit, incs)

    def S_(waits, emit, incs=()):
        rec.op("scalar", waits, emit, incs)

    def G_(waits, emit, incs=()):
        rec.op("gpsimd", waits, emit, incs)

    def SY(waits, emit, incs=()):
        rec.op("sync", waits, emit, incs)

    sB7e = sem("sB7e")
    sB7p = sem("sB7p")

    def via_b7(extra_waits, emitT, evac_eng, emitE, evac_incs=()):
        prior = sB7e.n
        T_(list(extra_waits) + [(sB7e, prior), (sem("sIdent"), 2)], emitT, [(sB7p, 1)])
        rec.op(evac_eng, [(sB7p, sB7p.n)], emitE, list(evac_incs) + [(sB7e, 1)])

    sProD = rsem("sProD")
    sPro = sem("sPro")
    sBc0 = rsem("gBc0")
    sBc1 = rsem("gBc1")
    rStUp = rsem("rStUp")
    rStG = rsem("rStG")
    rStDn = rsem("rStDn")
    rA = [rsem("rA0"), rsem("rA1"), rsem("rA2")]
    rW = [rsem("rW0"), rsem("rW1"), rsem("rW2"), rsem("rW3")]
    rQ = [rsem("rQ0"), rsem("rQ1")]
    rX = [rsem("rX0"), rsem("rX1")]
    rX1s = [rsem("rX1s0"), rsem("rX1s1")]
    rUpSt = [rsem("rUpSt0"), rsem("rUpSt1")]
    rYSt = [rsem("rYSt0"), rsem("rYSt1")]
    rQSp = [rsem("rQSp0"), rsem("rQSp1")]
    gCCD = rsem("gCCD")
    gCC = rsem("gCC")

    G_([], lambda g: g.iota(qscr[:, 0:128], pattern=[[1, 128]], base=0,
                            channel_multiplier=-1), [(sem("sIota"), 1)])
    V_([(sem("sIota"), 1)],
       lambda v: v.tensor_single_scalar(ident[:, :], qscr[:, 0:128], 0.0, Alu.is_equal),
       [(sem("sIdent"), 1)])
    V_([], lambda v: v.memset(onesf[:, :], 1.0), [(sem("sIdent"), 1)])
    V_([], lambda v: v.memset(zeroc[:, :], 0.0), [(sem("sConst"), 1)])
    V_([], lambda v: v.memset(bsc[:, 7:8], 32768.0), [(sem("sConst"), 1)])

    for dst, src in ((ccol, ccol_d), (bqcol, bqcol_d), (bkcol, bkcol_d)):
        SY([], (lambda d=dst, s=src: (lambda h: h.dma_start(out=d[:, :], in_=s[:, :])))(),
           [(sProD, 16)])
    G_([], lambda g: g.dma_start(out=bvb[:, :], in_=bvrow_d[:, :].partition_broadcast(128)), [(sBc0, 16)])
    G_([], lambda g: g.dma_start(out=bob[:, :], in_=borow_d[:, :].partition_broadcast(128)), [(sBc1, 16)])

    S_([(sProD, 48)],
       lambda s: s.activation(bsc[:, 8:8 + DC], ccol[:, :], Act.Tanh, bias=0.0, scale=0.5),
       [(sPro, 1)])
    V_([(sPro, 1)],
       lambda v: v.tensor_scalar(bsc[:, 8:8 + DC], bsc[:, 8:8 + DC], 0.5, 0.5, Alu.mult, Alu.add),
       [(sPro, 1)])
    V_([(sPro, 2)],
       lambda v: v.tensor_tensor(silu_b[:, :], ccol[:, :], bsc[:, 8:8 + DC], Alu.mult),
       [(sPro, 1)])

    (xf_s, xf_ids) = AF.alloc(3)
    (hT_s, hT_ids) = AB.alloc(DC)
    hT = [hT_s[dc][:, 0:S] for dc in range(DC)]
    (xnv, xn_ids) = AB.alloc(1)
    xn_buf = [xnv[0][:, 0:D], xnv[0][:, D:2 * D]]

    sSq = sem("sSq")
    sSum = sem("sSum")
    sStat = sem("sStat")
    xa_tok = []
    for i in range(T):
        b = i % 3
        w = []
        if i >= 3:
            w.append((sSq, i - 2))
            w.append((sSum, i - 2))
        SY(w, (lambda i_=i, b_=b: (lambda h: h.dma_start(
            out=xf_s[b_][:, :], in_=x_d[i_ * 128:(i_ + 1) * 128, :])))(),
           [(rA[b], 16)])
        xa_tok.append(rA[b].n)
    for i in range(T):
        b = i % 3
        nld = xa_tok[i]
        S_([(rA[b], nld), (sSq, i)],
           (lambda i_=i, b_=b: (lambda s: s.activation(
               trashb[:, :], xf_s[b_][:, :], Act.Square, bias=0.0, scale=1.0,
               accum_out=sqc[:, i_:i_ + 1])))(),
           [(sSq, 1)])
        V_([(rA[b], nld)],
           (lambda i_=i, b_=b: (lambda v: v.reduce_sum(sumc[:, i_:i_ + 1], xf_s[b_][:, :], AX)))(),
           [(sSum, 1)])

    def ln_stats(sS, sQ, sOut):
        V_([(sS, T)], lambda v: v.tensor_scalar(stat1[:, :], sumc[:, :], 1.0 / D, None, Alu.mult),
           [(sOut, 1)])
        V_([(sQ, T), (sOut, sOut.n)],
           lambda v: v.tensor_scalar(stat2[:, :], sqc[:, :], 1.0 / D, None, Alu.mult),
           [(sOut, 1)])
        V_([(sOut, sOut.n)],
           lambda v: v.scalar_tensor_tensor(stat2[:, :], stat1[:, :], 1.0, stat2[:, :], Alu.mult, Alu.subtract),
           [(sOut, 1)])
        V_([(sOut, sOut.n)],
           lambda v: v.tensor_scalar(stat2[:, :], stat2[:, :], -1.0, EPS_LN, Alu.mult, Alu.add),
           [(sOut, 1)])
        S_([(sOut, sOut.n)],
           lambda s: s.activation(rstdc[:, :], stat2[:, :], Act.Sqrt, bias=0.0, scale=1.0),
           [(sOut, 1)])
        V_([(sOut, sOut.n)],
           lambda v: v.reciprocal(rstdc[:, :], rstdc[:, :]),
           [(sOut, 1)])
        V_([(sOut, sOut.n)],
           lambda v: v.scalar_tensor_tensor(nbc[:, :], stat1[:, :], -1.0, rstdc[:, :], Alu.mult, Alu.mult),
           [(sOut, 1)])

    ln_stats(sSum, sSq, sStat)
    nStat = sStat.n

    (wmod_v, wmod_ids) = AB.alloc(4)
    wmod_buf = [arena_b[:, wmod_ids[0] * BSLOT:(wmod_ids[0] + 2) * BSLOT],
                arena_b[:, wmod_ids[2] * BSLOT:(wmod_ids[2] + 2) * BSLOT]]
    assert wmod_ids[1] == wmod_ids[0] + 1 and wmod_ids[3] == wmod_ids[2] + 1
    sModC = sem("sModC")
    sModE = sem("sModE")
    n_slices = 6 * D // 512
    modL_tok = []
    for sl in range(n_slices):
        b = sl % 2
        w = []
        if sl >= 2:
            w.append((sModC, sl - 1))
        src = wmod_d[:, sl * 512:(sl + 1) * 512].rearrange("(c p) n -> p c n", p=128)
        dstv = wmod_buf[b].rearrange("p (c n) -> p c n", n=512)
        G_(w, (lambda s_=src, d_=dstv: (lambda g: g.dma_start(out=d_, in_=s_)))(),
           [(rQ[b], 16)])
        modL_tok.append(rQ[b].n)
    for sl in range(n_slices):
        b = sl % 2
        for kc in range(DC):
            w = [(rQ[b], modL_tok[sl])]
            if kc == 0:
                w.append((sPro, 3))
                if sl >= 1:
                    w.append((sModE, sl))
            T_(w, (lambda b_=b, kc_=kc: (lambda t: t.matmul(
                bank(6)[0:1, :], silu_b[:, kc_:kc_ + 1],
                wmod_buf[b_][:, kc_ * 512:(kc_ + 1) * 512],
                start=(kc_ == 0), stop=(kc_ == DC - 1))))(),
               [(sModC, 1)] if kc == DC - 1 else [])
        S_([(sModC, sl + 1)],
           (lambda sl_=sl: (lambda s: s.copy(modrow[:, sl_ * 512:(sl_ + 1) * 512],
                                             bank(6)[0:1, :])))(),
           [(sModE, 1)])
    AB.release(wmod_ids)

    sModB = sem("sModB")

    def bcast_vec(dst, off):
        for half in range(2):
            via_b7(
                [(sModE, n_slices)],
                (lambda h_=half, o_=off: (lambda t: t.matmul(
                    bank(7), onesf[:, :], modrow[0:1, o_ + h_ * 512: o_ + (h_ + 1) * 512],
                    start=True, stop=True)))(),
                "scalar",
                (lambda h_=half, d_=dst: (lambda s: s.copy(d_[:, h_ * 512:(h_ + 1) * 512], bank(7))))(),
                [(sModB, 1)])

    bcast_vec(gab, 2 * D)
    bcast_vec(cb1b, 4 * D)
    bcast_vec(sbb, 3 * D)
    bcast_vec(gbb, 5 * D)
    nModB = sModB.n

    sModCC = sem("sModCC")
    for dstc, off, add1 in ((ca1col, D, True), (sacol, 0, False)):
        for kc in range(DC):
            via_b7(
                [(sModE, n_slices)],
                (lambda o_=off, kc_=kc: (lambda t: t.matmul(
                    bank(7, 128)[:, 0:1], modrow[0:1, o_ + kc_ * 128: o_ + (kc_ + 1) * 128],
                    onesf[0:1, 0:1], start=True, stop=True)))(),
                "scalar",
                (lambda d_=dstc, kc_=kc, a_=add1: (lambda s: s.activation(
                    d_[:, kc_:kc_ + 1], bank(7, 128)[:, 0:1], Act.Identity,
                    bias=(1.0 if a_ else 0.0), scale=1.0)))(),
                [(sModCC, 1)])
    nModCC = sModCC.n

    sModP = sem("sModP")

    def gate_fix(dst):
        base = sModP.n
        S_([(sModB, nModB), (sModP, base)],
           lambda s, d=dst: s.activation(scr1b[:, :], d[:, :], Act.Abs, bias=0.0, scale=1.0),
           [(sModP, 1)])
        V_([(sModP, base + 1)],
           lambda v: v.tensor_scalar(scr1b[:, :], scr1b[:, :], 0.25, None, Alu.max),
           [(sModP, 1)])
        V_([(sModP, base + 2)],
           lambda v, d=dst: v.tensor_single_scalar(d[:, :], d[:, :], 0.0, Alu.is_ge),
           [(sModP, 1)])
        V_([(sModP, base + 3)],
           lambda v, d=dst: v.tensor_tensor(d[:, :], d[:, :], scr1b[:, :], Alu.mult),
           [(sModP, 1)])
        V_([(sModP, base + 4)],
           lambda v, d=dst: v.scalar_tensor_tensor(d[:, :], d[:, :], 2.0, scr1b[:, :], Alu.mult, Alu.subtract),
           [(sModP, 1)])

    gate_fix(gab)
    gate_fix(gbb)
    V_([(sModP, sModP.n), (sBc1, 16)],
       lambda v: v.tensor_tensor(gabob[:, :], gab[:, :], bob[:, :], Alu.mult),
       [(sModP, 1)])
    V_([(sModB, nModB)],
       lambda v: v.tensor_scalar(cb1b[:, :], cb1b[:, :], 1.0, None, Alu.add),
       [(sModP, 1)])
    nModP = sModP.n
    if upto <= 0:
        return _finish([(sModP, nModP), (sModCC, nModCC)])

    sXn = sem("sXn")
    sTr1 = sem("sTr1")
    sHT = sem("sHT")
    xb_tok = []
    for i in range(T):
        b = i % 3
        w = [(sSq, T), (sSum, T)]
        if i >= 3:
            w.append((sXn, i - 2))
        SY(w, (lambda i_=i, b_=b: (lambda h: h.dma_start(
            out=xf_s[b_][:, :], in_=x_d[i_ * 128:(i_ + 1) * 128, :])))(),
           [(rA[b], 16)])
        xb_tok.append(rA[b].n)
    for i in range(T):
        b = i % 3
        bx = i % 2
        nld = xb_tok[i]
        w = [(rA[b], nld), (sStat, nStat)]
        if i >= 2:
            w.append((sTr1, DC * (i - 1)))
        S_(w, (lambda i_=i, b_=b, bx_=bx: (lambda s: s.activation(
            xn_buf[bx_][:, :], xf_s[b_][:, :], Act.Identity,
            bias=nbc[:, i_:i_ + 1], scale=rstdc[:, i_:i_ + 1])))(),
           [(sXn, 1)])
        for dc in range(DC):
            pb = (DC * i + dc) % 2
            w2 = [(sXn, i + 1), (sem("sIdent"), 2)]
            prev = DC * i + dc - 2
            if prev >= 0:
                w2.append((sHT, prev + 1))
            T_(w2, (lambda dc_=dc, pb_=pb, bx_=bx: (lambda t: t.matmul(
                bank(pb_, 128), xn_buf[bx_][:, dc_ * 128:(dc_ + 1) * 128], ident[:, :],
                start=True, stop=True)))(),
               [(sTr1, 1)])
            S_([(sTr1, DC * i + dc + 1), (sModCC, nModCC)],
               (lambda i_=i, dc_=dc, pb_=pb: (lambda s: s.activation(
                   hT[dc_][:, i_ * 128:(i_ + 1) * 128], bank(pb_, 128), Act.Identity,
                   bias=sacol[:, dc_:dc_ + 1], scale=ca1col[:, dc_:dc_ + 1])))(),
               [(sHT, 1)])
    nHT = sHT.n
    if upto <= 1:
        return _finish([(sHT, nHT)])
    AF.release(xf_ids)

    (wA_s, wA_ids) = AB.alloc(4)
    (wB_s, wB_ids) = AB.alloc(4)
    AB.release(xn_ids)

    def wsl(bufs, kc, lo, hi):
        return bufs[kc // 2][:, (kc % 2) * 1024 + lo:(kc % 2) * 1024 + hi]

    wring = {"wq": rW[0], "wk": rW[1], "wv": rW[2], "wo": rW[3]}
    wtok = {}

    def load_weight(bufs, wname, src_fd, extra_wait=()):
        r = wring[wname]
        for kc in range(DC):
            G_(list(extra_wait),
               (lambda s_=src_fd[kc * 128:(kc + 1) * 128, :], d_=wsl(bufs, kc, 0, 1024):
                (lambda g: g.dma_start(out=d_, in_=s_)))(),
               [(r, 16)])
        wtok[wname] = r.n

    load_weight(wA_s, "wq", wqT_d)
    load_weight(wB_s, "wk", wkT_d)

    sMMq = sem("sMMq")
    sQT = sem("sQT")

    def proj_fm(bufs, wname, bias_col, dst_of_ec, spill):
        sp_tok = {}
        for ns in range(NS):
            for ec in range(DC):
                gidx = ns * DC + ec
                pb = 4 + gidx % 2
                for kc in range(DC):
                    w = [(wring[wname], wtok[wname])]
                    if kc == 0:
                        w.append((sHT, min(DC * (4 * ns + 4), nHT)))
                        if sQT.n >= 1:
                            w.append((sQT, sQT.n - 1))
                    T_(w, (lambda ec_=ec, ns_=ns, kc_=kc, pb_=pb: (lambda t: t.matmul(
                        bank(pb_), wsl(bufs, kc_, ec_ * 128, ec_ * 128 + 128),
                        hT[kc_][:, ns_ * 512:(ns_ + 1) * 512],
                        start=(kc_ == 0), stop=(kc_ == DC - 1))))(),
                       [(sMMq, 1)] if kc == DC - 1 else [])
                wE = [(sMMq, sMMq.n)]
                if spill and ec >= 2:
                    wE.append((rQSp[ec % 2], sp_tok[(ec - 2, ns)]))
                S_(wE,
                   (lambda ec_=ec, ns_=ns, pb_=pb: (lambda s: s.activation(
                       dst_of_ec(ec_)[:, ns_ * 512:(ns_ + 1) * 512], bank(pb_), Act.Identity,
                       bias=bias_col[:, ec_:ec_ + 1], scale=1.0)))(),
                   [(sQT, 1)])
                if spill:
                    SY([(sQT, sQT.n)],
                       (lambda ec_=ec, ns_=ns: (lambda h: h.dma_start(
                           out=qTs_d[ec_ * 128:(ec_ + 1) * 128, ns_ * 512:(ns_ + 1) * 512],
                           in_=dst_of_ec(ec_)[:, ns_ * 512:(ns_ + 1) * 512])))(),
                       [(rQSp[ec % 2], 16)])
                    sp_tok[(ec, ns)] = rQSp[ec % 2].n

    (qst2, qst2_ids) = AB.alloc(2)
    proj_fm(wA_s, "wq", bqcol, lambda ec: qst2[ec % 2][:, 0:S], True)
    nQSp0 = rQSp[0].n
    nQSp1 = rQSp[1].n
    (kT_s, kT_ids) = AB.alloc(DC)
    kT = [kT_s[dc][:, 0:S] for dc in range(DC)]
    proj_fm(wB_s, "wk", bkcol, lambda ec: kT[ec], False)
    load_weight(wA_s, "wv", wvT_d, extra_wait=[(rQSp[0], nQSp0), (rQSp[1], nQSp1)])
    nQT_afterK = sQT.n

    (v_s, v_ids) = AB.alloc(DC)

    def v_sl(sc, lo, hi):
        return v_s[sc // 2][:, (sc % 2) * 1024 + lo:(sc % 2) * 1024 + hi]

    sMMv = sem("sMMv")
    sV = sem("sV")
    for sc in range(T):
        for ne in range(2):
            gidx = sc * 2 + ne
            pb = 4 + gidx % 2
            for kc in range(DC):
                w = [(wring["wv"], wtok["wv"])]
                if kc == 0:
                    w.append((sQT, nQT_afterK))
                    if gidx >= 2:
                        w.append((sV, gidx - 1))
                T_(w, (lambda sc_=sc, ne_=ne, kc_=kc, pb_=pb: (lambda t: t.matmul(
                    bank(pb_), hT[kc_][:, sc_ * 128:(sc_ + 1) * 128],
                    wsl(wA_s, kc_, ne_ * 512, (ne_ + 1) * 512),
                    start=(kc_ == 0), stop=(kc_ == DC - 1))))(),
                   [(sMMv, 1)] if kc == DC - 1 else [])
            V_([(sMMv, sMMv.n), (sBc0, 16)],
               (lambda sc_=sc, ne_=ne, pb_=pb: (lambda v: v.tensor_tensor(
                   v_sl(sc_, ne_ * 512, (ne_ + 1) * 512), bank(pb_),
                   bvb[:, ne_ * 512:(ne_ + 1) * 512], Alu.add)))(),
               [(sV, 1)])
    nV = sV.n
    load_weight(wB_s, "wo", woT_d, extra_wait=[(sQT, nQT_afterK)])
    if upto <= 2:
        return _finish([(sV, nV)])

    AB.release(hT_ids)
    AB.release(qst2_ids)
    AB.release(wA_ids)

    AB.release(trashb_ids)
    (lq_ab, lq_ids) = AB.alloc(7)
    lw = ([lq_ab[0].bitcast(F32)[:, 512 * j:512 * (j + 1)] for j in range(2)]
          + [lq_ab[1].bitcast(F32)[:, 512 * j:512 * (j + 1)] for j in range(2)])
    lq32 = [lq_ab[2].bitcast(I32)[:, 512 * j:512 * (j + 1)] for j in range(2)]
    lwq = [lq_ab[3 + i // 4][:, (i % 4) * 512:(i % 4 + 1) * 512] for i in range(12)]
    lwqT = [lq_ab[6][:, 512 * j:512 * (j + 1)] for j in range(2)]
    lam = [sumc[:, 0:16], sqc[:, 0:16]]
    lss = [stat1[:, 0:16], stat2[:, 0:16]]
    lrr = [rstdc[:, 0:16], nbc[:, 0:16]]

    lpanels = []
    for t in range(FF // 128):
        for hh2 in range(2):
            lpanels.append((wup_o_d, wup_bd, t, hh2, rStUp))
    for t in range(FF // 128):
        for hh2 in range(2):
            lpanels.append((wg_o_d, wg_bd, t, hh2, rStG))
    for t in range(D // 128):
        for hh2 in range(2 * IC // DC):
            lpanels.append((wdn_o_d, wdn_bd, t, hh2, rStDn))
    NPAN = len(lpanels)

    sLqV = sem("sLqV")
    sLqT = sem("sLqT")
    sLqE = sem("sLqE")
    sLqDone = sem("sLqDone")
    rLq = [rsem(f"rLq{j}") for j in range(4)]
    LQ = {"r": 0, "ld_tok": [0] * NPAN, "st_tok": [None] * NPAN,
          "vt_Q": [0] * NPAN, "vt_W": [0] * NPAN}

    def lq_load(j):
        if j >= NPAN:
            return
        src_d, dst_d, t, hh2, rSt = lpanels[j]
        w = []
        if j < 4:
            w += [(sMMv, 2 * T), (rQSp[0], nQSp0), (rQSp[1], nQSp1), (sSq, T)]
        else:
            w.append((sLqV, LQ["vt_Q"][j - 4]))
        G_(w, (lambda s_=src_d[t * 128:t * 128 + 128, hh2 * 512:hh2 * 512 + 512],
               d_=lw[j % 4]: (lambda g: g.dma_start(out=d_, in_=s_)))(),
           [(rLq[j % 4], 16)])
        LQ["ld_tok"][j] = rLq[j % 4].n

    def lq_tail(r):
        if r < 0 or r >= NPAN:
            return
        par = r % 2
        src_d, dst_d, t, hh2, rSt = lpanels[r]
        pb = 4 + r % 2
        for c in range(4):
            w2 = [(sem("sIdent"), 2)]
            if c == 0:
                w2 += [(sLqV, LQ["vt_W"][r]),
                       (sem("sPTe"), sem("sPTe").n), (sem("sA1"), sem("sA1").n),
                       (sQT, sQT.n), (sV, sV.n)]
                if r >= 2:
                    w2.append((sLqE, r - 1))
            T_(w2, (lambda m_=r % 12, c_=c, pb_=pb: (lambda tt: tt.matmul(
                bank(pb_)[:, c_ * 128:(c_ + 1) * 128],
                lwq[m_][:, c_ * 128:(c_ + 1) * 128], ident[:, :],
                start=True, stop=True)))(),
               [(sLqT, 1)])
        wE_ = [(sLqT, 4 * (r + 1))]
        if r >= 2 and LQ["st_tok"][r - 2] is not None:
            wE_.append(LQ["st_tok"][r - 2])
        S_(wE_, (lambda p_=par, pb_=pb: (lambda s: s.copy(lwqT[p_], bank(pb_))))(),
           [(sLqE, 1)])
        G_([(sLqE, r + 1)],
           (lambda p_=par, d_=dst_d, t_=t, h_=hh2: (lambda g: g.dma_start(
               out=d_[h_ * 512:h_ * 512 + 512, t_ * 128:t_ * 128 + 128]
               .rearrange("(c p) n -> p c n", p=128),
               in_=lwqT[p_].rearrange("p (c n) -> p c n", n=128))))(),
           [(rSt, 16)])
        LQ["st_tok"][r] = (rSt, rSt.n)

    def lq_round():
        r = LQ["r"]
        if r >= NPAN:
            return
        LQ["r"] = r + 1
        par = r % 2
        th2 = LQ.get("th2", NPAN + 1)
        lamX = lam if r < th2 else LQ["lam2"]
        lssX = lss if r < th2 else LQ["lss2"]
        lrrX = lrr if r < th2 else LQ["lrr2"]
        wA_ = [(sLqV, sLqV.n), (rLq[r % 4], LQ["ld_tok"][r])]
        if r < 2:
            wA_ += [(sStat, nStat), (sXn, T)]
        V_(wA_, (lambda p_=par, q_=r % 4, la_=lamX: (lambda v: v.tensor_reduce(
            la_[p_], lw[q_].rearrange("p (b k) -> p b k", k=32),
            AX, Alu.max, apply_absolute_value=True)))(),
           [(sLqV, 1)])
        V_([(sLqV, sLqV.n)],
           (lambda p_=par, la_=lamX, ls_=lssX: (lambda v: v.tensor_scalar(
               ls_[p_], la_[p_], 1.0 / 7.0, EPS_Q, Alu.mult, Alu.max)))(),
           [(sLqV, 1)])
        V_([(sLqV, sLqV.n)],
           (lambda p_=par, ls_=lssX, lr_=lrrX: (lambda v: v.reciprocal(lr_[p_], ls_[p_])))(),
           [(sLqV, 1)])
        V_([(sLqV, sLqV.n)],
           (lambda p_=par, q_=r % 4, lr_=lrrX: (lambda v: v.tensor_tensor(
               lq32[p_].rearrange("p (b k) -> p b k", k=32),
               lw[q_].rearrange("p (b k) -> p b k", k=32),
               lr_[p_].unsqueeze(2).broadcast_to([128, 16, 32]), Alu.mult)))(),
           [(sLqV, 1)])
        LQ["vt_Q"][r] = sLqV.n
        wW_ = [(sLqV, sLqV.n)]
        if r >= 12:
            wW_.append((sLqT, 4 * (r - 11)))
        V_(wW_, (lambda p_=par, m_=r % 12, ls_=lssX: (lambda v: v.tensor_tensor(
            lwq[m_].rearrange("p (b k) -> p b k", k=32),
            lq32[p_].rearrange("p (b k) -> p b k", k=32),
            ls_[p_].unsqueeze(2).broadcast_to([128, 16, 32]), Alu.mult)))(),
           [(sLqV, 1)])
        LQ["vt_W"][r] = sLqV.n
        lq_tail(r - 8)
        lq_load(r + 3)

    for j in range(3):
        lq_load(j)
    for _ in range(8):
        lq_round()

    n_pt_slots = max(1, (T * SB) // 2048)
    (PT_s, PT_ids) = AB.alloc(n_pt_slots)
    PSB = 2048 // SB
    PT = [PT_s[tc // PSB][:, (tc % PSB) * SB:(tc % PSB + 1) * SB] for tc in range(T)]
    (Pst_s, Pst_ids) = AB.alloc(2)
    (oT_s, oT_ids) = AB.alloc(2 if SB == 512 else 2)

    def oT_sl(dc, lo, hi):
        return oT_s[dc // 4][:, (dc % 4) * SB + lo:(dc % 4) * SB + hi]

    (qstv, qst_ids) = AB.alloc(1)

    def qstrip(par):
        return qstv[0][:, par * 1024:(par + 1) * 1024]

    AF.release(modrow_ids)
    scA, scA_ids = AF.alloc2()
    scB, scB_ids = AF.alloc2()
    scC, scC_ids = AF.alloc2()

    sSc = sem("sSc")
    sScE = sem("sScE")
    sTh = sem("sTh")
    sP = sem("sP")
    sPT = sem("sPT")
    sPTe = sem("sPTe")
    sPV = sem("sPV")
    sOT = sem("sOT")
    sPR = sem("sPR")
    sA1 = sem("sA1")
    sRes = sem("sRes")
    sRRS = sem("sRRS")
    (a1_s, a1_ids) = AF.alloc(2)
    (xr_s, xr_ids) = AF.alloc(2)
    ql_tok = [0] * T
    xr_tok = [0] * T
    x1st_tok = [0] * T

    def emit_qload(si):
        par = si % 2
        w = [(rQSp[0], nQSp0), (rQSp[1], nQSp1), (sV, nV)]
        if si >= 2:
            w.append((sSc, si - 1))
        SY(w, (lambda si_=si, p_=par: (lambda h: h.dma_start(
            out=qstrip(p_).rearrange("p (c n) -> p c n", n=128),
            in_=qTs_d[:, si_ * 128:(si_ + 1) * 128].rearrange("(c p) n -> p c n", p=128))))(),
           [(rQ[par], 16)])
        ql_tok[si] = rQ[par].n

    def emit_scores(si):
        par = si % 2
        for nb4 in range(NS):
            for kc in range(DC):
                w = [(rQ[par], ql_tok[si])]
                if kc == 0:
                    w.append((sV, nV))
                    if si >= 1:
                        w.append((sScE, 3 * si))
                T_(w, (lambda nb_=nb4, kc_=kc, p_=par: (lambda t: t.matmul(
                    bank(nb_), qstrip(p_)[:, kc_ * 128:(kc_ + 1) * 128],
                    kT[kc_][:, nb_ * 512:(nb_ + 1) * 512],
                    start=(kc_ == 0), stop=(kc_ == DC - 1))))(),
                   [(sSc, 1)] if (kc == DC - 1 and nb4 == NS - 1) else [])

    def emit_theta(si):
        par = si % 2
        psc = ps_all[:, 0:S]
        wA_ = [(sSc, si + 1)]
        wB_ = [(sSc, si + 1)]
        wC_ = [(sSc, si + 1)]
        if si >= 1:
            wA_.append((sTh, 4 * (si - 1) + 1))
            wB_.append((sTh, 4 * (si - 1) + 2))
            wC_.append((sP, si))
        S_(wA_, lambda s: s.activation(scA[:, 0:S], psc, Act.Tanh, bias=0.0, scale=0.5),
           [(sScE, 1)])
        S_(wB_, lambda s: s.activation(scB[:, 0:S], psc, Act.Tanh, bias=0.0, scale=1.0),
           [(sScE, 1)])
        V_(wC_ + [(sScE, 3 * si + 2)],
           lambda v: v.tensor_scalar(scC[:, 0:S], psc, 0.0, 0.1, Alu.max, Alu.mult),
           [(sScE, 1)])
        V_([(sScE, 3 * si + 3)],
           lambda v: v.scalar_tensor_tensor(scC[:, 0:S], scA[:, 0:S], 0.1, scC[:, 0:S], Alu.mult, Alu.add),
           [(sTh, 1)])
        V_([(sTh, 4 * si + 1)],
           lambda v: v.scalar_tensor_tensor(scC[:, 0:S], scB[:, 0:S], 0.15, scC[:, 0:S], Alu.mult, Alu.add),
           [(sTh, 1)])
        V_([(sTh, 4 * si + 2)],
           (lambda si_=si: (lambda v: v.tensor_reduce(
               bcolc[:, si_:si_ + 1], scC[:, 0:S], AX, Alu.max, negate=True)))(),
           [(sTh, 1)])
        V_([(sTh, 4 * si + 3)],
           (lambda si_=si: (lambda v: v.tensor_scalar(
               bcolc[:, si_:si_ + 1], bcolc[:, si_:si_ + 1], 0.6, None, Alu.add)))(),
           [(sTh, 1)])
        wP = [(sTh, 4 * si + 4)]
        if si >= 2:
            wP.append((sPT, T * (si - 1)))
        S_(wP, (lambda si_=si, p_=par: (lambda s: s.activation(
            Pst_s[p_][:, 0:S], scC[:, 0:S], Act.Exp,
            bias=bcolc[:, si_:si_ + 1], scale=1.0,
            accum_out=rsc[:, si_:si_ + 1])))(),
           [(sP, 1)])

    def emit_PT(si):
        sp = si % SPB
        par = si % 2
        blkv = si // SPB
        for tc in range(T):
            pb = 4 + (tc % 2)
            w2 = [(sP, si + 1), (sem("sIdent"), 2)]
            if tc == 0:
                w2.append((sLqE, sLqE.n))
            if sPTe.n >= 2:
                w2.append((sPTe, sPTe.n - 1))
            if blkv >= 1:
                w2.append((sPV, DC * blkv))
                w2.append((sA1, SPB * blkv))
            T_(w2, (lambda tc_=tc, pb_=pb, p_=par: (lambda t: t.matmul(
                bank(pb_, 128), Pst_s[p_][:, tc_ * 128:(tc_ + 1) * 128], ident[:, :],
                start=True, stop=True)))(),
               [(sPT, 1)])
            S_([(sPT, sPT.n)],
               (lambda tc_=tc, pb_=pb, sp_=sp: (lambda s: s.copy(
                   PT[tc_][:, sp_ * 128:(sp_ + 1) * 128], bank(pb_, 128))))(),
               [(sPTe, 1)])

    PTN = [0]

    def flush_PT(upto_si):
        while PTN[0] <= upto_si:
            emit_PT(PTN[0])
            PTN[0] += 1

    emit_qload(0)
    for blk in range(NBLK):
        for sp in range(SPB):
            si = blk * SPB + sp
            if si + 1 < T:
                emit_qload(si + 1)
            emit_scores(si)
            flush_PT(si - 1)
            emit_theta(si)
            while LQ["r"] < 8 + ((si + 1) * 80) // T:
                lq_round()
        flush_PT(blk * SPB + SPB - 1)
        nPTe = sPTe.n
        for dc in range(DC if upto >= 35 else 0):
            pb = 6 + dc % 2
            for tc in range(T):
                w = []
                if tc == 0:
                    w.append((sPTe, nPTe))
                    if sOT.n >= 2:
                        w.append((sOT, sOT.n - 1))
                T_(w, (lambda dc_=dc, tc_=tc, pb_=pb: (lambda t: t.matmul(
                    bank(pb_, SB), v_sl(tc_, dc_ * 128, dc_ * 128 + 128),
                    PT[tc_][:, 0:SB], start=(tc_ == 0), stop=(tc_ == T - 1))))(),
                   [(sPV, 1)] if tc == T - 1 else [])
            S_([(sPV, sPV.n)],
               (lambda dc_=dc, pb_=pb: (lambda s: s.copy(oT_sl(dc_, 0, SB), bank(pb_, SB))))(),
               [(sOT, 1)])
        if upto >= 33:
            V_([(sP, (blk + 1) * SPB)],
               (lambda b_=blk: (lambda v: v.reciprocal(rrs[:, b_ * SPB:(b_ + 1) * SPB],
                                                       rsc[:, b_ * SPB:(b_ + 1) * SPB])))(),
               [(sRRS, 1)])
        nOT = sOT.n
        for sc2 in range(SPB if upto >= 40 else 0):
            si = blk * SPB + sc2
            par = si % 2
            w = []
            if si >= 2:
                w.append((sRes, 3 * (si - 1)))
            SY(w, (lambda si_=si, p_=par: (lambda h: h.dma_start(
                out=xr_s[p_][:, :], in_=x_d[si_ * 128:(si_ + 1) * 128, :])))(),
               [(rX[par], 16)])
            xr_tok[si] = rX[par].n
            for ne in range(2):
                pb = 4 + ne
                for dc in range(DC):
                    w = []
                    if dc == 0:
                        w.append((sOT, nOT))
                        w.append((wring["wo"], wtok["wo"]))
                        w.append((sPT, T * SPB * (blk + 1)))
                        w.append((sLqE, sLqE.n))
                        if sA1.n >= 1 and ne == 0:
                            pass
                        if si * 2 + ne >= 2:
                            w.append((sA1, sA1.n - 0 if False else (si * 2 + ne) // 2))
                    T_(w, (lambda sc_=sc2, ne_=ne, dc_=dc, pb_=pb: (lambda t: t.matmul(
                        bank(pb_), oT_sl(dc_, sc_ * 128, sc_ * 128 + 128),
                        wsl(wB_s, dc_, ne_ * 512, (ne_ + 1) * 512),
                        start=(dc_ == 0), stop=(dc_ == DC - 1))))(),
                       [(sPR, 1)] if dc == DC - 1 else [])
            wEv = [(sPR, sPR.n), (sRRS, blk + 1)]
            if si >= 2:
                wEv.append((rX1s[par], x1st_tok[si - 2]))
            S_(wEv,
               (lambda si_=si, p_=par: (lambda s: s.activation(
                   a1_s[p_][:, :], ps_all[:, 4 * 512:4 * 512 + D], Act.Identity,
                   bias=zeroc[:, 0:1], scale=rrs[:, si_:si_ + 1])))(),
               [(sA1, 1)])
            wR1 = [(sA1, sA1.n), (sModP, nModP)]
            V_(wR1,
               (lambda p_=par: (lambda v: v.tensor_tensor(a1_s[p_][:, :], a1_s[p_][:, :], gab[:, :], Alu.mult)))(),
               [(sRes, 1)])
            V_([(sRes, sRes.n)],
               (lambda p_=par: (lambda v: v.tensor_tensor(a1_s[p_][:, :], a1_s[p_][:, :], gabob[:, :], Alu.add)))(),
               [(sRes, 1)])
            V_([(sRes, sRes.n), (rX[par], xr_tok[si])],
               (lambda p_=par: (lambda v: v.tensor_tensor(a1_s[p_][:, :], a1_s[p_][:, :], xr_s[p_][:, :], Alu.add)))(),
               [(sRes, 1)])
            SY([(sRes, sRes.n)],
               (lambda si_=si, p_=par: (lambda h: h.dma_start(
                   out=x1_d[si_ * 128:(si_ + 1) * 128, :], in_=a1_s[p_][:, :])))(),
               [(rX1s[par], 16)])
            x1st_tok[si] = rX1s[par].n

    AB.release(kT_ids); AB.release(v_ids); AB.release(PT_ids); AB.release(Pst_ids)
    AB.release(oT_ids); AB.release(qst_ids); AB.release(wB_ids)
    AF.release(scA_ids); AF.release(scB_ids); AF.release(scC_ids)
    AF.release(a1_ids); AF.release(xr_ids)

    (x1r_s, x1r_ids) = AF.alloc(3)
    (lsc2_s, lsc2_ids) = AF.alloc(1)
    LQ["lam2"] = [lsc2_s[0][:, 16 * j:16 * (j + 1)] for j in range(2)]
    LQ["lss2"] = [lsc2_s[0][:, 32 + 16 * j:48 + 16 * j] for j in range(2)]
    LQ["lrr2"] = [lsc2_s[0][:, 64 + 16 * j:80 + 16 * j] for j in range(2)]
    LQ["th2"] = LQ["r"]
    sSq2 = sem("sSq2")
    sSum2 = sem("sSum2")
    sStat2 = sem("sStat2")
    x1a_tok = []
    for i in range(T):
        b = i % 3
        w = [(rX1s[i % 2], x1st_tok[i])]
        if i >= 3:
            w.append((sSq2, i - 2))
            w.append((sSum2, i - 2))
        SY(w, (lambda i_=i, b_=b: (lambda h: h.dma_start(
            out=x1r_s[b_][:, :], in_=x1_d[i_ * 128:(i_ + 1) * 128, :])))(),
           [(rA[b], 16)])
        x1a_tok.append(rA[b].n)
    for i in range(T):
        b = i % 3
        nld = x1a_tok[i]
        S_([(rA[b], nld), (sSq2, i)],
           (lambda i_=i, b_=b: (lambda s: s.activation(
               trash[:, :], x1r_s[b_][:, :], Act.Square, bias=0.0, scale=1.0,
               accum_out=sqc[:, i_:i_ + 1])))(),
           [(sSq2, 1)])
        V_([(rA[b], nld)],
           (lambda i_=i, b_=b: (lambda v: v.reduce_sum(sumc[:, i_:i_ + 1], x1r_s[b_][:, :], AX)))(),
           [(sSum2, 1)])
        while LQ["r"] < 88 + ((i + 1) * 28) // T:
            lq_round()
    ln_stats(sSum2, sSq2, sStat2)
    nStat2 = sStat2.n

    (h2_s, h2_ids) = AB.alloc(DC)

    def h2_sl(i):
        return h2_s[i // 2][:, (i % 2) * 1024:(i % 2 + 1) * 1024]

    (h2f_s, h2f_ids) = AF.alloc(2)
    sXn2 = sem("sXn2")
    sH2 = sem("sH2")
    sH2V = sem("sH2V")
    sH2M = sem("sH2M")
    x1b_tok = []
    for i in range(T):
        b = i % 3
        w = [(sSq2, T), (sSum2, T)]
        if i >= 3:
            w.append((sXn2, i - 2))
        SY(w, (lambda i_=i, b_=b: (lambda h: h.dma_start(
            out=x1r_s[b_][:, :], in_=x1_d[i_ * 128:(i_ + 1) * 128, :])))(),
           [(rA[b], 16)])
        x1b_tok.append(rA[b].n)
    for i in range(T):
        b = i % 3
        bx = i % 2
        nld = x1b_tok[i]
        w = [(rA[b], nld), (sStat2, nStat2)]
        if i >= 2:
            w.append((sH2, 2 * (i - 1)))
        S_(w, (lambda i_=i, b_=b, bx_=bx: (lambda s: s.activation(
            h2f_s[bx_][:, :], x1r_s[b_][:, :], Act.Identity,
            bias=nbc[:, i_:i_ + 1], scale=rstdc[:, i_:i_ + 1])))(),
           [(sXn2, 1)])
        V_([(sXn2, i + 1), (sModP, nModP)],
           (lambda bx_=bx: (lambda v: v.tensor_tensor(h2f_s[bx_][:, :], h2f_s[bx_][:, :], cb1b[:, :], Alu.mult)))(),
           [(sH2, 1)])
        V_([(sH2, 2 * i + 1)],
           (lambda i_=i, bx_=bx: (lambda v: v.tensor_tensor(h2_sl(i_), h2f_s[bx_][:, :], sbb[:, :], Alu.add)))(),
           [(sH2, 1)])
        V_([(sH2, 2 * i + 2)],
           (lambda i_=i: (lambda v: v.tensor_reduce(
               bcolc[:, i_:i_ + 1], h2_sl(i_), AX, Alu.max,
               apply_absolute_value=True)))(),
           [(sH2V, 1)])
        G_([(sH2V, i + 1)],
           (lambda i_=i: (lambda g: g.tensor_reduce(redrow[0:1, 96 + i_:97 + i_],
                                                    bcolc[:, i_:i_ + 1],
                                                    mybir.AxisListType.XYZWC, Alu.max)))(),
           [(sH2M, 1)])
        while LQ["r"] < 116 + ((i + 1) * 28) // T:
            lq_round()
    while LQ["r"] < NPAN:
        lq_round()
    for k_ in range(8):
        lq_tail(NPAN - 8 + k_)
    G_([(rStUp, rStUp.n), (rStG, rStG.n), (rStDn, rStDn.n)],
       lambda g: g.memset(scal[0:1, 63:64], 0.0), [(sLqDone, 1)])
    AB.release(lq_ids)
    AF.release(lsc2_ids)

    sBar = sem("sBar")
    sCC = gCC
    sCCD = gCCD

    def cross_core_max(srcs, cc_idx, waits):
        for j, (lo, hi, idx) in enumerate(srcs):
            V_(list(waits) + [(sBar, sBar.n)],
               (lambda lo_=lo, hi_=hi, i_=idx: (lambda v: v.tensor_reduce(
                   scal[0:1, i_:i_ + 1], redrow[0:1, lo_:hi_], AX, Alu.max)))(),
               [(sBar, 1)])
        idx0 = srcs[0][2]
        nv = len(srcs)
        G_([(sBar, sBar.n)],
           (lambda ci=cc_idx, i0=idx0, n_=nv: (lambda g: g.dma_start(
               out=cc_in[ci][:, 0:n_], in_=scal[0:1, i0:i0 + n_])))(),
           [(sCCD, 16)])
        G_([(sCCD, sCCD.n)],
           (lambda ci=cc_idx, n_=nv: (lambda g: g.collective_compute(
               "AllReduce", Alu.max, replica_groups=[list(range(ncores))],
               ins=[cc_in[ci][:, 0:n_].opt()], outs=[cc_out[ci][:, 0:n_].opt()])))(),
           [(sCC, 1)])
        G_([(sCC, sCC.n)],
           (lambda ci=cc_idx, i0=idx0, n_=nv: (lambda g: g.dma_start(
               out=scal[0:1, i0:i0 + n_], in_=cc_out[ci][:, 0:n_])))(),
           [(sCCD, 16)])

    def bcast_scalar(src_idx, dst_col, waits):
        via_b7(
            list(waits),
            (lambda i_=src_idx: (lambda t: t.matmul(bank(7, 128)[:, 0:1], onesf[:, :],
                                                    scal[0:1, i_:i_ + 1], start=True, stop=True)))(),
            "scalar",
            (lambda c_=dst_col: (lambda s: s.copy(bsc[:, c_:c_ + 1], bank(7, 128)[:, 0:1])))(),
            [(sBar, 1)])

    cross_core_max([(96, 96 + T, 8)], 0, [(sH2M, T)])
    V_([(sCCD, sCCD.n)],
       lambda v: v.tensor_scalar(scal[0:1, 0:1], scal[0:1, 8:9], 1.0 / QMAX, EPS_Q, Alu.mult, Alu.max),
       [(sBar, 1)])
    V_([(sBar, sBar.n)],
       lambda v: v.reciprocal(scal[0:1, 1:2], scal[0:1, 0:1]),
       [(sBar, 1)])
    bcast_scalar(1, 0, [(sBar, sBar.n)])
    nBar1 = sBar.n

    (nfx_s, nfx_ids) = AB.alloc(DC)

    def nfx_sl(i):
        return nfx_s[i // 2][:, (i % 2) * 1024:(i % 2 + 1) * 1024]

    sTq = sem("sTq")
    sNfx = sem("sNfx")
    for i in range(T):
        w = [(sBar, nBar1)]
        if i >= 1:
            w.append((sNfx, i))
        S_(w, (lambda i_=i: (lambda s: s.activation(
            qscr[:, 0:1024], h2_sl(i_), Act.Identity,
            bias=0.0, scale=bsc[:, 0:1])))(),
           [(sTq, 1)])
        V_([(sTq, i + 1)],
           (lambda i_=i: (lambda v: v.tensor_copy(nfx_sl(i_), qscr[:, 0:1024])))(),
           [(sNfx, 1)])
    (nfxT_s, nfxT_ids) = AB.alloc(DC)
    nfxT = [nfxT_s[dc][:, 0:S] for dc in range(DC)]
    sTr2 = sem("sTr2")
    sNfxT = sem("sNfxT")
    for i in range(T):
        for dc in range(DC):
            pb = (DC * i + dc) % 4
            w = [(sNfx, i + 1), (sem("sIdent"), 2)]
            prev = DC * i + dc - 4
            if prev >= 0:
                w.append((sNfxT, prev + 1))
            T_(w, (lambda i_=i, dc_=dc, pb_=pb: (lambda t: t.matmul(
                bank(pb_, 128), nfx_sl(i_)[:, dc_ * 128:(dc_ + 1) * 128], ident[:, :],
                start=True, stop=True)))(),
               [(sTr2, 1)])
            S_([(sTr2, DC * i + dc + 1)],
               (lambda i_=i, dc_=dc, pb_=pb: (lambda s: s.copy(
                   nfxT[dc_][:, i_ * 128:(i_ + 1) * 128], bank(pb_, 128))))(),
               [(sNfxT, 1)])
    nNfxT = sNfxT.n
    if upto <= 50:
        return _finish([(sNfxT, nNfxT)])
    AB.release(h2_ids)
    AB.release(nfx_ids)
    AF.release(x1r_ids); AF.release(h2f_ids)

    (wfb, wfb_ids) = AB.alloc(1)

    def wf_buf(j):
        return wfb[0][:, (j % 2) * 1024:(j % 2) * 1024 + 1024]

    (upc_s, upc_ids) = AB.alloc(2)
    sMMu = sem("sMMu")
    sUpE = sem("sUpE")
    sUpMin = sem("sUpMin")
    wup_tok = [0] * IC
    upst_tok = [0] * IC

    def wup_load(ic):
        j = ic % 2
        w = [(rStUp, 24 * 16), (sNfxT, nNfxT)]
        if ic < 2:
            w.append((sLqDone, 1))
        if ic >= 2:
            w.append((sMMu, ic - 1))
        src = wup_bd[:, ic * 128:(ic + 1) * 128].rearrange("(c p) n -> p c n", p=128)
        SY(w, (lambda s_=src, j_=j: (lambda h: h.dma_start(
            out=wf_buf(j_).rearrange("p (c n) -> p c n", n=128), in_=s_)))(),
           [(rW[j], 16)])
        wup_tok[ic] = rW[j].n

    for ic in range(2):
        wup_load(ic)
    for ic in range(IC):
        j = ic % 2
        pbase = (ic % 2) * 4
        for kc in range(DC):
            for ns2 in range(NS):
                w = [(rW[j], wup_tok[ic])]
                if kc == 0 and ns2 == 0:
                    w.append((sNfxT, nNfxT))
                    if ic < 2:
                        w.append((sLqE, NPAN))
                    if ic >= 2:
                        w.append((sUpE, ic - 1))
                T_(w, (lambda kc_=kc, ns_=ns2, pb_=pbase, j_=j: (lambda t: t.matmul(
                    bank(pb_ + ns_), wf_buf(j_)[:, kc_ * 128:(kc_ + 1) * 128],
                    nfxT[kc_][:, ns_ * 512:(ns_ + 1) * 512],
                    start=(kc_ == 0), stop=(kc_ == DC - 1))))(),
                   [(sMMu, 1)] if (kc == DC - 1 and ns2 == NS - 1) else [])
        par = ic % 2
        w = [(sMMu, ic + 1)]
        if ic < 2:
            w.append((sLqDone, 1))
        if ic >= 2:
            w.append((rUpSt[par], upst_tok[ic - 2]))
        S_(w, (lambda ic_=ic, p_=par, pb_=pbase: (lambda s: s.copy(
            upc_s[p_][:, 0:S], ps_all[:, pb_ * 512:pb_ * 512 + S])))(),
           [(sUpE, 1)])
        G_([(sUpE, ic + 1)],
           (lambda ic_=ic, p_=par: (lambda g: g.tensor_reduce(
               redrow[0:1, ic_:ic_ + 1],
               upc_s[p_][:, 0:S], mybir.AxisListType.XYZWC, Alu.max, apply_absolute_value=True)))(),
           [(sUpMin, 1)])
        SY([(sUpMin, ic + 1)],
           (lambda ic_=ic, p_=par: (lambda h: h.dma_start(
               out=upraw_d[ic_, :, :], in_=upc_s[p_][:, 0:S])))(),
           [(rUpSt[par], 16)])
        upst_tok[ic] = rUpSt[par].n
        if ic + 2 < IC:
            wup_load(ic + 2)
    nUpSt0 = rUpSt[0].n
    nUpSt1 = rUpSt[1].n
    if upto <= 60:
        return _finish([(rUpSt[0], nUpSt0), (rUpSt[1], nUpSt1)])

    (gate_s, gate_ids) = AB.alloc(IC)

    def gate_sl(ic):
        return gate_s[ic][:, 0:S]

    sMMg = sem("sMMg")
    sGE = sem("sGE")
    sGMin = sem("sGMin")
    wg_tok = []
    for ic in range(IC):
        j = ic % 2
        w = [(rStG, 24 * 16), (sMMu, IC)]
        if ic >= 2:
            w.append((sMMg, ic - 1))
        src = wg_bd[:, ic * 128:(ic + 1) * 128].rearrange("(c p) n -> p c n", p=128)
        SY(w, (lambda s_=src, j_=j: (lambda h: h.dma_start(
            out=wf_buf(j_).rearrange("p (c n) -> p c n", n=128), in_=s_)))(),
           [(rW[j], 16)])
        wg_tok.append(rW[j].n)
    for ic in range(IC):
        j = ic % 2
        pbase = (ic % 2) * 4
        for kc in range(DC):
            for ns2 in range(NS):
                w = [(rW[j], wg_tok[ic])]
                if kc == 0 and ns2 == 0:
                    w.append((sUpE, IC))
                    if ic >= 2:
                        w.append((sGE, ic - 1))
                T_(w, (lambda kc_=kc, ns_=ns2, pb_=pbase, j_=j: (lambda t: t.matmul(
                    bank(pb_ + ns_), wf_buf(j_)[:, kc_ * 128:(kc_ + 1) * 128],
                    nfxT[kc_][:, ns_ * 512:(ns_ + 1) * 512],
                    start=(kc_ == 0), stop=(kc_ == DC - 1))))(),
                   [(sMMg, 1)] if (kc == DC - 1 and ns2 == NS - 1) else [])
        S_([(sMMg, ic + 1)],
           (lambda ic_=ic, pb_=pbase: (lambda s: s.copy(
               gate_sl(ic_), ps_all[:, pb_ * 512:pb_ * 512 + S])))(),
           [(sGE, 1)])
        G_([(sGE, ic + 1)],
           (lambda ic_=ic: (lambda g: g.tensor_reduce(
               redrow[0:1, 24 + ic_:25 + ic_], gate_sl(ic_),
               mybir.AxisListType.XYZWC, Alu.max, apply_absolute_value=True)))(),
           [(sGMin, 1)])
    AB.release(nfxT_ids)

    cross_core_max([(0, IC, 10), (24, 24 + IC, 11)], 1, [(sUpMin, IC), (sGMin, IC)])
    nCCD2 = sCCD.n
    V_([(sCCD, nCCD2)],
       lambda v: v.scalar_tensor_tensor(scal[0:1, 2:3], scal[0:1, 10:11], 1.0 / QMAX, scal[0:1, 0:1], Alu.mult, Alu.mult),
       [(sBar, 1)])
    V_([(sBar, sBar.n)],
       lambda v: v.tensor_scalar(scal[0:1, 2:3], scal[0:1, 2:3], EPS_Q, None, Alu.max), [(sBar, 1)])
    V_([(sCCD, nCCD2)],
       lambda v: v.scalar_tensor_tensor(scal[0:1, 3:4], scal[0:1, 11:12], 1.0 / QMAX, scal[0:1, 0:1], Alu.mult, Alu.mult),
       [(sBar, 1)])
    V_([(sBar, sBar.n)],
       lambda v: v.tensor_scalar(scal[0:1, 3:4], scal[0:1, 3:4], EPS_Q, None, Alu.max), [(sBar, 1)])
    V_([(sBar, sBar.n)],
       lambda v: v.reciprocal(scal[0:1, 4:5], scal[0:1, 3:4]), [(sBar, 1)])
    V_([(sBar, sBar.n)],
       lambda v: v.scalar_tensor_tensor(scal[0:1, 5:6], scal[0:1, 0:1], 1.0, scal[0:1, 4:5], Alu.mult, Alu.mult),
       [(sBar, 1)])
    V_([(sBar, sBar.n)],
       lambda v: v.tensor_scalar(scal[0:1, 6:7], scal[0:1, 3:4], 0.5, None, Alu.mult),
       [(sBar, 1)])
    bcast_scalar(5, 1, [(sBar, sBar.n)])
    bcast_scalar(6, 2, [(sBar, sBar.n)])
    nBar2 = sBar.n
    if upto <= 70:
        return _finish([(sBar, nBar2)])

    tgA, tgA_ids = AF.alloc2()
    tgB, tgB_ids = AF.alloc2()
    (nfs_s, nfs_ids) = AB.alloc(2)
    sTg = sem("sTg")
    sNfg = sem("sNfg")
    sTnh = sem("sTnh")
    sT2 = sem("sT2")
    sNfs = sem("sNfs")
    sClip = sem("sClip")
    sG2 = sem("sG2")
    sG2V = sem("sG2V")
    sG2M = sem("sG2M")
    sQs = sem("sQs")
    NH = max(S // 1024, 1)
    HW_ = min(1024, S)

    def qpass(src_fn, dst_fn, scale_col, extra_w_first=(), extra_inc_last=(),
              pre=None):
        for hh in range(NH):
            lo, hi = hh * HW_, (hh + 1) * HW_
            w = [(sQs, sQs.n)]
            if hh == 0:
                w += list(extra_w_first)
            V_(w, (lambda lo_=lo, hi_=hi, sf=src_fn, sc_=scale_col: (lambda v: v.tensor_tensor(
                qscr[:, 0:hi_ - lo_], sf(lo_, hi_),
                bsc[:, sc_:sc_ + 1].broadcast_to([128, hi_ - lo_]), Alu.mult)))(),
               [(sQs, 1)])
            V_([(sQs, sQs.n)],
               (lambda lo_=lo, hi_=hi, df=dst_fn: (lambda v: v.tensor_copy(
                   df(lo_, hi_), qscr[:, 0:hi_ - lo_])))(),
               [(sQs, 1)] + (list(extra_inc_last) if hh == NH - 1 else []))

    qscr2A, qscr2A_ids = AB.alloc2()
    qscr2B, qscr2B_ids = AB.alloc2()
    qscr2 = [qscr2A.bitcast(I32), qscr2B.bitcast(I32)]
    sQ2 = sem("sQ2")
    q2_tok = {0: 0, 1: 0}

    def qpass_sg(src_fn, dst_fn, scale_col, par2, extra_w_first=(), extra_inc_last=()):
        w = [(sQ2, q2_tok[par2])] + list(extra_w_first)
        S_(w, (lambda sf=src_fn, sc_=scale_col, p_=par2: (lambda s: s.activation(
            qscr2[p_][:, 0:S], sf(0, S), Act.Identity,
            bias=0.0, scale=bsc[:, sc_:sc_ + 1])))(),
           [(sQ2, 1)])
        V_([(sQ2, sQ2.n)],
           (lambda df=dst_fn, p_=par2: (lambda v: v.tensor_copy(
               df(0, S), qscr2[p_][:, 0:S])))(),
           [(sQ2, 1)] + list(extra_inc_last))
        q2_tok[par2] = sQ2.n

    def p8_stage1(ic):
        par = ic % 2
        qpass_sg((lambda lo, hi, ic_=ic: gate_sl(ic_)[:, lo:hi]),
                 (lambda lo, hi, ic_=ic: gate_sl(ic_)[:, lo:hi]),
                 1, par, extra_w_first=[(sGE, IC), (sBar, nBar2)],
                 extra_inc_last=[(sNfg, 1)])

    def p8_stage2(ic):
        par = ic % 2
        tg = tgA if par == 0 else tgB
        wTn = [(sNfg, ic + 1)]
        if ic >= 2:
            wTn.append((sT2, ic - 1))
        S_(wTn,
           (lambda ic_=ic, t_=tg: (lambda s: s.activation(
               t_[:, 0:S], gate_sl(ic_), Act.Tanh, bias=zeroc[:, 0:1], scale=bsc[:, 2:3])))(),
           [(sTnh, 1)])
        wSg = [(sTnh, ic + 1), (sQ2, q2_tok[par])]
        if ic == 0:
            wSg.append((sem("sConst"), 2))
        S_(wSg,
           (lambda t_=tg, p_=par: (lambda s: s.activation(
               qscr2[p_][:, 0:S], t_[:, 0:S], Act.Identity,
               bias=bsc[:, 7:8], scale=32768.0)))(),
           [(sQ2, 1), (sT2, 1)])
        w = [(sQ2, sQ2.n)]
        if ic >= 2:
            w.append((sG2, ic - 1))
        V_(w, (lambda p_=par: (lambda v: v.tensor_scalar(
            nfs_s[p_][:, 0:S], qscr2[p_][:, 0:S], 65535.0, None, Alu.min)))(),
           [(sQ2, 1), (sClip, 1)])
        q2_tok[par] = sQ2.n
        V_([(sClip, ic + 1)],
           (lambda ic_=ic, p_=par: (lambda v: v.tensor_tensor(
               gate_sl(ic_), gate_sl(ic_), nfs_s[p_][:, 0:S], Alu.mult)))(),
           [(sG2, 1)])
        wRm = [(sG2, ic + 1)]
        if ic >= 16:
            wRm.append((sG2M, ic - 15))
        V_(wRm,
           (lambda ic_=ic: (lambda v: v.tensor_reduce(
               bcolc[:, ic_ % 16:ic_ % 16 + 1], gate_sl(ic_), AX, Alu.max,
               apply_absolute_value=True)))(),
           [(sG2V, 1)])
        G_([(sG2V, ic + 1)],
           (lambda ic_=ic: (lambda g: g.tensor_reduce(
               redrow[0:1, 48 + ic_:49 + ic_], bcolc[:, ic_ % 16:ic_ % 16 + 1],
               mybir.AxisListType.XYZWC, Alu.max)))(),
           [(sG2M, 1)])

    for ic in range(IC + 1):
        if ic < IC:
            p8_stage1(ic)
        if ic >= 1:
            p8_stage2(ic - 1)
    AF.release(tgA_ids)
    AF.release(tgB_ids)
    cross_core_max([(48, 48 + IC, 12)], 2, [(sG2M, IC)])
    nCCD3 = sCCD.n
    V_([(sCCD, nCCD3)],
       lambda v: v.scalar_tensor_tensor(scal[0:1, 7:8], scal[0:1, 12:13], 2.0 ** -16 / QMAX, scal[0:1, 3:4], Alu.mult, Alu.mult),
       [(sBar, 1)])
    V_([(sBar, sBar.n)],
       lambda v: v.tensor_scalar(scal[0:1, 7:8], scal[0:1, 7:8], EPS_Q, None, Alu.max), [(sBar, 1)])
    V_([(sBar, sBar.n)],
       lambda v: v.reciprocal(scal[0:1, 16:17], scal[0:1, 7:8]), [(sBar, 1)])
    V_([(sBar, sBar.n)],
       lambda v: v.scalar_tensor_tensor(scal[0:1, 17:18], scal[0:1, 3:4], 2.0 ** -16, scal[0:1, 16:17], Alu.mult, Alu.mult),
       [(sBar, 1)])
    V_([(sBar, sBar.n)],
       lambda v: v.reciprocal(scal[0:1, 18:19], scal[0:1, 2:3]), [(sBar, 1)])
    V_([(sBar, sBar.n)],
       lambda v: v.scalar_tensor_tensor(scal[0:1, 19:20], scal[0:1, 0:1], 1.0, scal[0:1, 18:19], Alu.mult, Alu.mult),
       [(sBar, 1)])
    bcast_scalar(17, 3, [(sBar, sBar.n)])
    bcast_scalar(19, 4, [(sBar, sBar.n)])
    nBar3 = sBar.n
    if upto <= 80:
        return _finish([(sBar, nBar3)])

    AB.release(upc_ids)
    (upr_s, upr_ids) = AB.alloc(3)
    (nfu_s, nfu_ids) = AB.alloc(2)
    sTu = sem("sTu")
    sNfu = sem("sNfu")
    sTg2 = sem("sTg2")
    sNfg2 = sem("sNfg2")
    sO = sem("sO")
    sOV = sem("sOV")
    sOM = sem("sOM")
    upr_tok = []
    for ic in range(IC):
        b = ic % 3
        w = [(rUpSt[0], nUpSt0), (rUpSt[1], nUpSt1), (sMMg, IC)]
        if ic >= 3:
            w.append((sNfu, ic - 2))
        SY(w, (lambda ic_=ic, b_=b: (lambda h: h.dma_start(
            out=upr_s[b_][:, 0:S], in_=upraw_d[ic_, :, :])))(),
           [(rA[b], 16)])
        upr_tok.append(rA[b].n)
    def p9_stage1(ic):
        b = ic % 3
        par = ic % 2
        nld = upr_tok[ic]
        wA1 = [(rA[b], nld), (sBar, nBar3)]
        if ic >= 2:
            wA1.append((sO, ic - 1))
        qpass_sg((lambda lo, hi, b_=b: upr_s[b_][:, lo:hi]),
                 (lambda lo, hi, p_=par: nfu_s[p_][:, lo:hi]),
                 4, par, extra_w_first=wA1, extra_inc_last=[(sNfu, 1)])

    def p9_stage2(ic):
        par = ic % 2
        qpass_sg((lambda lo, hi, ic_=ic: gate_sl(ic_)[:, lo:hi]),
                 (lambda lo, hi, ic_=ic: gate_sl(ic_)[:, lo:hi]),
                 3, par, extra_w_first=[(sG2M, IC), (sBar, nBar3), (sNfu, ic + 1)],
                 extra_inc_last=[(sNfg2, 1)])
        V_([(sNfg2, ic + 1), (sNfu, ic + 1)],
           (lambda ic_=ic, p_=par: (lambda v: v.tensor_tensor(
               gate_sl(ic_), gate_sl(ic_), nfu_s[p_][:, 0:S], Alu.mult)))(),
           [(sO, 1)])
        wRm9 = [(sO, ic + 1)]
        if ic == 0:
            wRm9.append((sG2M, IC))
        if ic >= 16:
            wRm9.append((sOM, ic - 15))
        V_(wRm9,
           (lambda ic_=ic: (lambda v: v.tensor_reduce(
               bcolc[:, ic_ % 16:ic_ % 16 + 1], gate_sl(ic_), AX, Alu.max,
               apply_absolute_value=True)))(),
           [(sOV, 1)])
        G_([(sOV, ic + 1)],
           (lambda ic_=ic: (lambda g: g.tensor_reduce(
               redrow[0:1, 72 + ic_:73 + ic_], bcolc[:, ic_ % 16:ic_ % 16 + 1],
               mybir.AxisListType.XYZWC, Alu.max)))(),
           [(sOM, 1)])

    for ic in range(IC + 1):
        if ic < IC:
            p9_stage1(ic)
        if ic >= 1:
            p9_stage2(ic - 1)
    cross_core_max([(72, 72 + IC, 13)], 3, [(sOM, IC)])
    nCCD4 = sCCD.n
    V_([(sCCD, nCCD4)],
       lambda v: v.scalar_tensor_tensor(scal[0:1, 20:21], scal[0:1, 2:3], 1.0, scal[0:1, 7:8], Alu.mult, Alu.mult),
       [(sBar, 1)])
    V_([(sBar, sBar.n)],
       lambda v: v.scalar_tensor_tensor(scal[0:1, 21:22], scal[0:1, 13:14], 1.0 / QMAX, scal[0:1, 20:21], Alu.mult, Alu.mult),
       [(sBar, 1)])
    V_([(sBar, sBar.n)],
       lambda v: v.tensor_scalar(scal[0:1, 21:22], scal[0:1, 21:22], EPS_Q, None, Alu.max), [(sBar, 1)])
    V_([(sBar, sBar.n)],
       lambda v: v.reciprocal(scal[0:1, 22:23], scal[0:1, 21:22]), [(sBar, 1)])
    V_([(sBar, sBar.n)],
       lambda v: v.scalar_tensor_tensor(scal[0:1, 23:24], scal[0:1, 20:21], 1.0, scal[0:1, 22:23], Alu.mult, Alu.mult),
       [(sBar, 1)])
    bcast_scalar(23, 5, [(sBar, sBar.n)])
    bcast_scalar(21, 6, [(sBar, sBar.n)])
    nBar4 = sBar.n

    sTo = sem("sTo")
    sNfo = sem("sNfo")
    q3A, q3A_ids = AF.alloc2()
    q3B, q3B_ids = AF.alloc2()
    qscr3 = [q3A.bitcast(I32), q3B.bitcast(I32)]
    sQ3 = sem("sQ3")
    q3_tok = {0: 0, 1: 0}
    for ic in range(IC):
        par3 = ic % 2
        S_([(sQ3, q3_tok[par3]), (sBar, nBar4), (sOM, IC)],
           (lambda ic_=ic, p_=par3: (lambda s: s.activation(
               qscr3[p_][:, 0:S], gate_sl(ic_), Act.Identity,
               bias=0.0, scale=bsc[:, 5:6])))(),
           [(sQ3, 1)])
        V_([(sQ3, sQ3.n)],
           (lambda ic_=ic, p_=par3: (lambda v: v.tensor_copy(
               gate_sl(ic_), qscr3[p_][:, 0:S])))(),
           [(sQ3, 1), (sNfo, 1)])
        q3_tok[par3] = sQ3.n
    AB.release(upr_ids); AB.release(nfu_ids)
    AB.release(nfs_ids); AB.release(wfb_ids)
    AB.release(qscr2A_ids); AB.release(qscr2B_ids)
    (wdn_s, wdn_ids) = AB.alloc(12)

    def wdn_sl(ic):
        return wdn_s[ic // 2][:, (ic % 2) * 1024:(ic % 2 + 1) * 1024]

    for ic in range(IC):
        src = wdn_bd[ic * 128:(ic + 1) * 128, :]
        SY([(rStDn, 24 * 16), (sO, IC)],
           (lambda s_=src, ic_=ic: (lambda h: h.dma_start(out=wdn_sl(ic_), in_=s_)))(),
           [(rW[0], 16)])
    nWdnL = rW[0].n

    sMMd = sem("sMMd")
    sYE = sem("sYE")
    sYR = sem("sYR")
    (x1rr_s, x1rr_ids) = AF.alloc(2)
    (yv_s, yv_ids) = AF.alloc(2)
    x1rr_tok = [0] * T
    yst_tok = [0] * T

    def x1rr_load(sc):
        par = sc % 2
        w = [(rX1s[sc % 2], x1st_tok[sc])]
        if sc < 2:
            w.append((sT2, IC))
        if sc >= 2:
            w.append((sYR, 2 * (sc - 1)))
        SY(w, (lambda sc_=sc, p_=par: (lambda h: h.dma_start(
            out=x1rr_s[p_][:, :], in_=x1_d[sc_ * 128:(sc_ + 1) * 128, :])))(),
           [(rX[par], 16)])
        x1rr_tok[sc] = rX[par].n

    for sc in range(2):
        x1rr_load(sc)
    for sc in range(T):
        par = sc % 2
        pb = 4 + 2 * (sc % 2)
        for ic in range(IC):
            for ne in range(2):
                w = [(rW[0], nWdnL), (sNfo, ic + 1)]
                if ic == 0 and ne == 0 and sc >= 2:
                    w.append((sYE, sc - 1))
                T_(w, (lambda sc_=sc, ic_=ic, ne_=ne, pb_=pb: (lambda t: t.matmul(
                    bank(pb_ + ne_), gate_sl(ic_)[:, sc_ * 128:(sc_ + 1) * 128],
                    wdn_sl(ic_)[:, ne_ * 512:(ne_ + 1) * 512],
                    start=(ic_ == 0), stop=(ic_ == IC - 1))))(),
                   [(sMMd, 1)] if (ic == IC - 1 and ne == 1) else [])
        wYE = [(sMMd, sc + 1), (sBar, nBar4)]
        if sc >= 2:
            wYE.append((rYSt[par], yst_tok[sc - 2]))
        S_(wYE,
           (lambda p_=par, pb_=pb: (lambda s: s.activation(
               yv_s[p_][:, :], ps_all[:, pb_ * 512:pb_ * 512 + D], Act.Identity,
               bias=0.0, scale=bsc[:, 6:7])))(),
           [(sYE, 1)])
        w = [(sYE, sc + 1), (sModP, nModP)]
        V_(w, (lambda p_=par: (lambda v: v.tensor_tensor(yv_s[p_][:, :], yv_s[p_][:, :], gbb[:, :], Alu.mult)))(),
           [(sYR, 1)])
        V_([(sYR, 2 * sc + 1), (rX[par], x1rr_tok[sc])],
           (lambda p_=par: (lambda v: v.tensor_tensor(yv_s[p_][:, :], yv_s[p_][:, :], x1rr_s[p_][:, :], Alu.add)))(),
           [(sYR, 1)])
        SY([(sYR, 2 * (sc + 1))],
           (lambda sc_=sc, p_=par: (lambda h: h.dma_start(
               out=y_d[sc_ * 128:(sc_ + 1) * 128, :], in_=yv_s[p_][:, :])))(),
           [(rYSt[par], 16)])
        yst_tok[sc] = rYSt[par].n
        if sc + 2 < T:
            x1rr_load(sc + 2)
    rec.wait("sync", [(rYSt[0], rYSt[0].n), (rYSt[1], rYSt[1].n)])
    rec.wait("gpsimd", [(rYSt[0], rYSt[0].n), (rYSt[1], rYSt[1].n)])

    with nc.Block() as block:
        block.sync(lambda h: [f(h) for f in rec.ops["sync"]])
        block.tensor(lambda h: [f(h) for f in rec.ops["tensor"]])
        block.vector(lambda h: [f(h) for f in rec.ops["vector"]])
        block.scalar(lambda h: [f(h) for f in rec.ops["scalar"]])
        block.gpsimd(lambda h: [f(h) for f in rec.ops["gpsimd"]])
    st.close()
    return nc


def make_in_maps(inputs, S, ncores):
    x = np.ascontiguousarray(np.asarray(inputs["L0_attn_x"], dtype=np.float32))
    wqT = np.ascontiguousarray(np.asarray(inputs["L0_attn_Wq"]).T)
    wkT = np.ascontiguousarray(np.asarray(inputs["L0_attn_Wk"]).T)
    wvT = np.ascontiguousarray(np.asarray(inputs["L0_attn_Wv"]).T)
    woT = np.ascontiguousarray(np.asarray(inputs["L0_attn_Wo"]).T)
    wup_o = np.ascontiguousarray(np.asarray(inputs["L0_ffn_w_up"], dtype=np.float32))
    wg_o = np.ascontiguousarray(np.asarray(inputs["L0_ffn_w_gate"], dtype=np.float32))
    wdn_o = np.ascontiguousarray(np.asarray(inputs["L0_ffn_w_down"], dtype=np.float32))
    wmod = np.ascontiguousarray(np.asarray(inputs["L0_wmod"]))
    c = np.asarray(inputs["L0_c"], dtype=np.float32)
    bq = np.asarray(inputs["L0_attn_bq"], dtype=np.float32)
    bk = np.asarray(inputs["L0_attn_bk"], dtype=np.float32)
    bv = np.asarray(inputs["L0_attn_bv"], dtype=np.float32)
    bo = np.asarray(inputs["L0_attn_bo"], dtype=np.float32)
    bqcol = np.ascontiguousarray(bq.reshape(DC, 128).T)
    bkcol = np.ascontiguousarray(bk.reshape(DC, 128).T)
    maps = []
    for b in range(ncores):
        maps.append({
            "x": np.ascontiguousarray(x[b, :S]),
            "wqT": wqT, "wkT": wkT, "wvT": wvT, "woT": woT,
            "wup_o": wup_o, "wg_o": wg_o, "wdn_o": wdn_o, "wmod": wmod,
            "ccol": np.ascontiguousarray(c[b].reshape(DC, 128).T),
            "bqcol": bqcol, "bkcol": bkcol,
            "bvrow": np.ascontiguousarray(bv.reshape(1, D)),
            "borow": np.ascontiguousarray(bo.reshape(1, D)),
        })
    return maps


def kernel(**inputs):
    S = int(inputs["L0_attn_x"].shape[1])
    ncores = int(inputs["L0_attn_x"].shape[0])
    nc = build_program(S, ncores)
    in_maps = make_in_maps(inputs, S, ncores)
    res = run_bass_kernel_spmd(nc, in_maps, core_ids=list(range(ncores)))
    out = np.stack([np.asarray(res.results[b]["y"]) for b in range(ncores)], axis=0)
    return out.astype(np.float32)


# version 25
# speedup vs baseline: 1.0162x; 1.0162x over previous
import numpy as np

import concourse.bass as bass
import concourse.mybir as mybir
from concourse.bass_utils import run_bass_kernel_spmd

F32 = mybir.dt.float32
BF16 = mybir.dt.bfloat16
I32 = mybir.dt.int32
AX = mybir.AxisListType.X
Alu = mybir.AluOpType
Act = mybir.ActivationFunctionType

D = 1024
DC = D // 128
FF = 3072
IC = FF // 128
QMAX = 32767.0
EPS_LN = 1e-5
EPS_Q = 1e-12

ENGINES = ("sync", "tensor", "vector", "scalar", "gpsimd")


class Sem:

    def __init__(self, h):
        self.h = h
        self.n = 0

    def inc(self, k=1):
        self.n += k
        return self.n


class EngSem(Sem):
    pass


class VSem:

    def __init__(self, name):
        self.name = name
        self.n = 0
        self.map = {}
        self._cur = {}

    def inc_record(self, eng_sem, shared_val, k=1):
        self.n += k
        self._cur = dict(self._cur)
        self._cur[id(eng_sem)] = (eng_sem, shared_val)
        self.map[self.n] = self._cur

    def translate(self, v):
        assert v in self.map, f"VSem {self.name}: wait on unrecorded count {v} (max {self.n})"
        return list(self.map[v].values())


class Rec:
    def __init__(self, eng_sems):
        self.ops = {e: [] for e in ENGINES}
        self.eng_sems = eng_sems

    @staticmethod
    def _resolve(waits):
        tw = {}
        for (s, v) in waits:
            if v <= 0:
                continue
            if isinstance(s, VSem):
                pairs = s.translate(v)
            else:
                pairs = [(s, v)]
            for sh_sem, sv in pairs:
                tw[id(sh_sem)] = (sh_sem.h, max(tw.get(id(sh_sem), (None, 0))[1], sv))
        return list(tw.values())

    def op(self, eng, waits, emit, incs):
        waits = [(s, v) for (s, v) in waits]
        real_incs = []
        v_incs = [s for (s, k) in incs if isinstance(s, VSem)]
        for (s, k) in incs:
            if not isinstance(s, VSem):
                real_incs.append((s.h, k))
                s.inc(k)
        es = self.eng_sems[eng]
        if v_incs:
            es.inc(1)
            for s in v_incs:
                s.inc_record(es, es.n)
            real_incs.append((es.h, 1))

        def fn(h):
            for sh, v in Rec._resolve(waits):
                h.wait_ge(sh, v)
            ins = emit(h)
            for sh, k in real_incs:
                ins = ins.then_inc(sh, k)
        self.ops[eng].append(fn)

    def wait(self, eng, waits):
        waits = [(s, v) for (s, v) in waits]

        def fn(h):
            for sh, v in Rec._resolve(waits):
                h.wait_ge(sh, v)
        self.ops[eng].append(fn)


class SlotAlloc:
    def __init__(self, tensor, n_slots, slot_elems, name):
        self.t = tensor
        self.free = list(range(n_slots))
        self.se = slot_elems
        self.name = name

    def alloc(self, k=1):
        assert len(self.free) >= k, f"{self.name}: need {k} slots, have {len(self.free)}"
        ids = [self.free.pop(0) for _ in range(k)]
        return [self.t[:, s * self.se:(s + 1) * self.se] for s in ids], ids

    def alloc2(self):
        for j in range(len(self.free) - 1):
            if self.free[j + 1] == self.free[j] + 1:
                a = self.free.pop(j + 1)
                b = self.free.pop(j)
                return self.t[:, b * self.se:(a + 1) * self.se], [b, a]
        raise AssertionError(f"{self.name}: no contiguous pair")

    def release(self, ids):
        self.free = sorted(self.free + list(ids))


def build_program(S, ncores, upto=99):
    T = S // 128
    NS = S // 512 if S >= 512 else 1
    SB = min(512, S)
    NBLK = max(S // SB, 1)
    SPB = SB // 128
    assert S % 512 == 0 and S <= 2048

    nc = bass.Bass("TRN2", target_bir_lowering=False, debug=False)

    x_d = nc.dram_tensor("x", [S, D], F32, kind="ExternalInput").ap()
    wqT_d = nc.dram_tensor("wqT", [D, D], F32, kind="ExternalInput").ap()
    wkT_d = nc.dram_tensor("wkT", [D, D], F32, kind="ExternalInput").ap()
    wvT_d = nc.dram_tensor("wvT", [D, D], F32, kind="ExternalInput").ap()
    woT_d = nc.dram_tensor("woT", [D, D], F32, kind="ExternalInput").ap()
    wmod_d = nc.dram_tensor("wmod", [D, 6 * D], F32, kind="ExternalInput").ap()
    ccol_d = nc.dram_tensor("ccol", [128, DC], F32, kind="ExternalInput").ap()
    bqcol_d = nc.dram_tensor("bqcol", [128, DC], F32, kind="ExternalInput").ap()
    bkcol_d = nc.dram_tensor("bkcol", [128, DC], F32, kind="ExternalInput").ap()
    bvrow_d = nc.dram_tensor("bvrow", [1, D], F32, kind="ExternalInput").ap()
    borow_d = nc.dram_tensor("borow", [1, D], F32, kind="ExternalInput").ap()

    y_d = nc.dram_tensor("y", [S, D], F32, kind="ExternalOutput").ap()

    wup_o_d = nc.dram_tensor("wup_o", [FF, D], F32, kind="ExternalInput").ap()
    wg_o_d = nc.dram_tensor("wg_o", [FF, D], F32, kind="ExternalInput").ap()
    wdn_o_d = nc.dram_tensor("wdn_o", [D, FF], F32, kind="ExternalInput").ap()

    x1_d = nc.dram_tensor("x1buf", [S, D], F32).ap()
    qTs_d = nc.dram_tensor("qTspill", [D, S], BF16).ap()
    upraw_d = nc.dram_tensor("upraw", [IC, 128, S], BF16).ap()
    wup_bd = nc.dram_tensor("wup_b", [D, FF], BF16).ap()
    wg_bd = nc.dram_tensor("wg_b", [D, FF], BF16).ap()
    wdn_bd = nc.dram_tensor("wdn_b", [FF, D], BF16).ap()
    cc_in = [nc.dram_tensor(f"cc_in{i}", [1, 2], F32).ap() for i in range(4)]
    cc_out = [nc.dram_tensor(f"cc_out{i}", [1, 2], F32).ap() for i in range(4)]

    from contextlib import ExitStack
    st = ExitStack()
    sbt = lambda n_, sh, dt: st.enter_context(nc.sbuf_tensor(n_, sh, dt))
    eng_sems = {e: EngSem(st.enter_context(nc.semaphore("p_" + e))) for e in ENGINES}
    rec = Rec(eng_sems)
    sems = {}
    REAL = set()

    def sem(name):
        if name not in sems:
            if name in REAL:
                sems[name] = Sem(st.enter_context(nc.semaphore(name)))
            else:
                sems[name] = VSem(name)
        return sems[name]

    def rsem(name):
        if name not in sems:
            REAL.add(name)
        return sem(name)

    BSLOT = 2048
    NBS = 36
    FSLOT = 1024
    NFS = 10
    arena_b = sbt("arena_b", [128, NBS * BSLOT], BF16)
    arena_f = sbt("arena_f", [128, NFS * FSLOT], F32)
    AB = SlotAlloc(arena_b, NBS, BSLOT, "bf16")
    AF = SlotAlloc(arena_f, NFS, FSLOT, "f32")
    (trashb_s, trashb_ids) = AB.alloc(1)
    trashb = trashb_s[0][:, 0:D]

    ident = sbt("ident", [128, 128], BF16)
    onesf = sbt("onesf", [1, 128], F32)
    ccol = sbt("ccol_sb", [128, DC], F32)
    silu_b = sbt("silu_b", [128, DC], BF16)
    bqcol = sbt("bqcol_sb", [128, DC], F32)
    bkcol = sbt("bkcol_sb", [128, DC], F32)
    ca1col = sbt("ca1col", [128, DC], F32)
    sacol = sbt("sacol", [128, DC], F32)
    bvb = sbt("bvb", [128, D], BF16)
    gab = sbt("gab", [128, D], BF16)
    gabob = sbt("gabob", [128, D], BF16)
    cb1b = sbt("cb1b", [128, D], BF16)
    sbb = sbt("sbb", [128, D], BF16)
    gbb = sbt("gbb", [128, D], BF16)
    bob = sbt("bob", [128, D], BF16)
    sumc = sbt("sumc", [128, T], F32)
    sqc = sbt("sqc", [128, T], F32)
    stat1 = sbt("stat1", [128, T], F32)
    stat2 = sbt("stat2", [128, T], F32)
    rstdc = sbt("rstdc", [128, T], F32)
    nbc = sbt("nbc", [128, T], F32)
    bcolc = sbt("bcolc", [128, T], F32)
    rsc = sbt("rsc", [128, T], F32)
    rrs = sbt("rrs", [128, T], F32)
    scal = sbt("scal", [1, 64], F32)
    bsc = sbt("bsc", [128, 16], F32)
    redrow = sbt("redrow", [1, 128], F32)
    zeroc = sbt("zeroc", [128, 1], F32)
    scr1b = sbt("scr1b", [128, D], BF16)
    trash = scr1b
    qscr = sbt("qscr", [128, 1024], I32)

    (modrow_slots, modrow_ids) = AF.alloc(6)
    assert modrow_ids == [0, 1, 2, 3, 4, 5]
    modrow = arena_f[0:1, 0:6 * D]

    ps_all = st.enter_context(nc.psum_tensor("ps", [128, 4096], F32))

    def bank(b, w=512):
        return ps_all[:, b * 512: b * 512 + w]

    def _finish(waits):
        rFin = rsem("rFin")
        rec.op("sync", list(waits),
               lambda h: h.dma_start(out=y_d[0:128, :], in_=arena_f[:, 0:D]),
               [(rFin, 16)])
        rec.wait("sync", [(rFin, 16)])
        with nc.Block() as block:
            block.sync(lambda h: [f(h) for f in rec.ops["sync"]])
            block.tensor(lambda h: [f(h) for f in rec.ops["tensor"]])
            block.vector(lambda h: [f(h) for f in rec.ops["vector"]])
            block.scalar(lambda h: [f(h) for f in rec.ops["scalar"]])
            block.gpsimd(lambda h: [f(h) for f in rec.ops["gpsimd"]])
        st.close()
        return nc

    def T_(waits, emit, incs=()):
        rec.op("tensor", waits, emit, incs)

    def V_(waits, emit, incs=()):
        rec.op("vector", waits, emit, incs)

    def S_(waits, emit, incs=()):
        rec.op("scalar", waits, emit, incs)

    def G_(waits, emit, incs=()):
        rec.op("gpsimd", waits, emit, incs)

    def SY(waits, emit, incs=()):
        rec.op("sync", waits, emit, incs)

    sB7e = sem("sB7e")
    sB7p = sem("sB7p")

    def via_b7(extra_waits, emitT, evac_eng, emitE, evac_incs=()):
        prior = sB7e.n
        T_(list(extra_waits) + [(sB7e, prior), (sem("sIdent"), 2)], emitT, [(sB7p, 1)])
        rec.op(evac_eng, [(sB7p, sB7p.n)], emitE, list(evac_incs) + [(sB7e, 1)])

    sProD = rsem("sProD")
    sPro = sem("sPro")
    sBc0 = rsem("gBc0")
    sBc1 = rsem("gBc1")
    rStUp = rsem("rStUp")
    rStG = rsem("rStG")
    rStDn = rsem("rStDn")
    rA = [rsem("rA0"), rsem("rA1"), rsem("rA2")]
    rW = [rsem("rW0"), rsem("rW1"), rsem("rW2"), rsem("rW3")]
    rQ = [rsem("rQ0"), rsem("rQ1")]
    rX = [rsem("rX0"), rsem("rX1")]
    rX1s = [rsem("rX1s0"), rsem("rX1s1")]
    rUpSt = [rsem("rUpSt0"), rsem("rUpSt1")]
    rYSt = [rsem("rYSt0"), rsem("rYSt1")]
    rQSp = [rsem("rQSp0"), rsem("rQSp1")]
    gCCD = rsem("gCCD")
    gCC = rsem("gCC")

    G_([], lambda g: g.iota(qscr[:, 0:128], pattern=[[1, 128]], base=0,
                            channel_multiplier=-1), [(sem("sIota"), 1)])
    V_([(sem("sIota"), 1)],
       lambda v: v.tensor_single_scalar(ident[:, :], qscr[:, 0:128], 0.0, Alu.is_equal),
       [(sem("sIdent"), 1)])
    V_([], lambda v: v.memset(onesf[:, :], 1.0), [(sem("sIdent"), 1)])
    V_([], lambda v: v.memset(zeroc[:, :], 0.0), [(sem("sConst"), 1)])
    V_([], lambda v: v.memset(bsc[:, 7:8], 32768.0), [(sem("sConst"), 1)])

    for dst, src in ((ccol, ccol_d), (bqcol, bqcol_d), (bkcol, bkcol_d)):
        SY([], (lambda d=dst, s=src: (lambda h: h.dma_start(out=d[:, :], in_=s[:, :])))(),
           [(sProD, 16)])
    G_([], lambda g: g.dma_start(out=bvb[:, :], in_=bvrow_d[:, :].partition_broadcast(128)), [(sBc0, 16)])
    G_([], lambda g: g.dma_start(out=bob[:, :], in_=borow_d[:, :].partition_broadcast(128)), [(sBc1, 16)])

    S_([(sProD, 48)],
       lambda s: s.activation(bsc[:, 8:8 + DC], ccol[:, :], Act.Tanh, bias=0.0, scale=0.5),
       [(sPro, 1)])
    V_([(sPro, 1)],
       lambda v: v.tensor_scalar(bsc[:, 8:8 + DC], bsc[:, 8:8 + DC], 0.5, 0.5, Alu.mult, Alu.add),
       [(sPro, 1)])
    V_([(sPro, 2)],
       lambda v: v.tensor_tensor(silu_b[:, :], ccol[:, :], bsc[:, 8:8 + DC], Alu.mult),
       [(sPro, 1)])

    (xf_s, xf_ids) = AF.alloc(3)
    (hT_s, hT_ids) = AB.alloc(DC)
    hT = [hT_s[dc][:, 0:S] for dc in range(DC)]
    (xnv, xn_ids) = AB.alloc(1)
    xn_buf = [xnv[0][:, 0:D], xnv[0][:, D:2 * D]]

    sSq = sem("sSq")
    sSum = sem("sSum")
    sStat = sem("sStat")
    xa_tok = []
    for i in range(T):
        b = i % 3
        w = []
        if i >= 3:
            w.append((sSq, i - 2))
            w.append((sSum, i - 2))
        SY(w, (lambda i_=i, b_=b: (lambda h: h.dma_start(
            out=xf_s[b_][:, :], in_=x_d[i_ * 128:(i_ + 1) * 128, :])))(),
           [(rA[b], 16)])
        xa_tok.append(rA[b].n)
    for i in range(T):
        b = i % 3
        nld = xa_tok[i]
        S_([(rA[b], nld), (sSq, i)],
           (lambda i_=i, b_=b: (lambda s: s.activation(
               trashb[:, :], xf_s[b_][:, :], Act.Square, bias=0.0, scale=1.0,
               accum_out=sqc[:, i_:i_ + 1])))(),
           [(sSq, 1)])
        V_([(rA[b], nld)],
           (lambda i_=i, b_=b: (lambda v: v.reduce_sum(sumc[:, i_:i_ + 1], xf_s[b_][:, :], AX)))(),
           [(sSum, 1)])

    def ln_stats(sS, sQ, sOut):
        V_([(sS, T)], lambda v: v.tensor_scalar(stat1[:, :], sumc[:, :], 1.0 / D, None, Alu.mult),
           [(sOut, 1)])
        V_([(sQ, T), (sOut, sOut.n)],
           lambda v: v.tensor_scalar(stat2[:, :], sqc[:, :], 1.0 / D, None, Alu.mult),
           [(sOut, 1)])
        V_([(sOut, sOut.n)],
           lambda v: v.scalar_tensor_tensor(stat2[:, :], stat1[:, :], 1.0, stat2[:, :], Alu.mult, Alu.subtract),
           [(sOut, 1)])
        V_([(sOut, sOut.n)],
           lambda v: v.tensor_scalar(stat2[:, :], stat2[:, :], -1.0, EPS_LN, Alu.mult, Alu.add),
           [(sOut, 1)])
        S_([(sOut, sOut.n)],
           lambda s: s.activation(rstdc[:, :], stat2[:, :], Act.Sqrt, bias=0.0, scale=1.0),
           [(sOut, 1)])
        V_([(sOut, sOut.n)],
           lambda v: v.reciprocal(rstdc[:, :], rstdc[:, :]),
           [(sOut, 1)])
        V_([(sOut, sOut.n)],
           lambda v: v.scalar_tensor_tensor(nbc[:, :], stat1[:, :], -1.0, rstdc[:, :], Alu.mult, Alu.mult),
           [(sOut, 1)])

    ln_stats(sSum, sSq, sStat)
    nStat = sStat.n

    (wmod_v, wmod_ids) = AB.alloc(4)
    wmod_buf = [arena_b[:, wmod_ids[0] * BSLOT:(wmod_ids[0] + 2) * BSLOT],
                arena_b[:, wmod_ids[2] * BSLOT:(wmod_ids[2] + 2) * BSLOT]]
    assert wmod_ids[1] == wmod_ids[0] + 1 and wmod_ids[3] == wmod_ids[2] + 1
    sModC = sem("sModC")
    sModE = sem("sModE")
    n_slices = 6 * D // 512
    modL_tok = []
    for sl in range(n_slices):
        b = sl % 2
        w = []
        if sl >= 2:
            w.append((sModC, sl - 1))
        src = wmod_d[:, sl * 512:(sl + 1) * 512].rearrange("(c p) n -> p c n", p=128)
        dstv = wmod_buf[b].rearrange("p (c n) -> p c n", n=512)
        G_(w, (lambda s_=src, d_=dstv: (lambda g: g.dma_start(out=d_, in_=s_)))(),
           [(rQ[b], 16)])
        modL_tok.append(rQ[b].n)
    for sl in range(n_slices):
        b = sl % 2
        for kc in range(DC):
            w = [(rQ[b], modL_tok[sl])]
            if kc == 0:
                w.append((sPro, 3))
                if sl >= 1:
                    w.append((sModE, sl))
            T_(w, (lambda b_=b, kc_=kc: (lambda t: t.matmul(
                bank(6)[0:1, :], silu_b[:, kc_:kc_ + 1],
                wmod_buf[b_][:, kc_ * 512:(kc_ + 1) * 512],
                start=(kc_ == 0), stop=(kc_ == DC - 1))))(),
               [(sModC, 1)] if kc == DC - 1 else [])
        S_([(sModC, sl + 1)],
           (lambda sl_=sl: (lambda s: s.copy(modrow[:, sl_ * 512:(sl_ + 1) * 512],
                                             bank(6)[0:1, :])))(),
           [(sModE, 1)])
    AB.release(wmod_ids)

    sModB = sem("sModB")

    def bcast_vec(dst, off):
        for half in range(2):
            via_b7(
                [(sModE, n_slices)],
                (lambda h_=half, o_=off: (lambda t: t.matmul(
                    bank(7), onesf[:, :], modrow[0:1, o_ + h_ * 512: o_ + (h_ + 1) * 512],
                    start=True, stop=True)))(),
                "scalar",
                (lambda h_=half, d_=dst: (lambda s: s.copy(d_[:, h_ * 512:(h_ + 1) * 512], bank(7))))(),
                [(sModB, 1)])

    bcast_vec(gab, 2 * D)
    bcast_vec(cb1b, 4 * D)
    bcast_vec(sbb, 3 * D)
    bcast_vec(gbb, 5 * D)
    nModB = sModB.n

    sModCC = sem("sModCC")
    for dstc, off, add1 in ((ca1col, D, True), (sacol, 0, False)):
        for kc in range(DC):
            via_b7(
                [(sModE, n_slices)],
                (lambda o_=off, kc_=kc: (lambda t: t.matmul(
                    bank(7, 128)[:, 0:1], modrow[0:1, o_ + kc_ * 128: o_ + (kc_ + 1) * 128],
                    onesf[0:1, 0:1], start=True, stop=True)))(),
                "scalar",
                (lambda d_=dstc, kc_=kc, a_=add1: (lambda s: s.activation(
                    d_[:, kc_:kc_ + 1], bank(7, 128)[:, 0:1], Act.Identity,
                    bias=(1.0 if a_ else 0.0), scale=1.0)))(),
                [(sModCC, 1)])
    nModCC = sModCC.n

    sModP = sem("sModP")

    def gate_fix(dst):
        base = sModP.n
        S_([(sModB, nModB), (sModP, base)],
           lambda s, d=dst: s.activation(scr1b[:, :], d[:, :], Act.Abs, bias=0.0, scale=1.0),
           [(sModP, 1)])
        V_([(sModP, base + 1)],
           lambda v: v.tensor_scalar(scr1b[:, :], scr1b[:, :], 0.25, None, Alu.max),
           [(sModP, 1)])
        V_([(sModP, base + 2)],
           lambda v, d=dst: v.tensor_single_scalar(d[:, :], d[:, :], 0.0, Alu.is_ge),
           [(sModP, 1)])
        V_([(sModP, base + 3)],
           lambda v, d=dst: v.tensor_tensor(d[:, :], d[:, :], scr1b[:, :], Alu.mult),
           [(sModP, 1)])
        V_([(sModP, base + 4)],
           lambda v, d=dst: v.scalar_tensor_tensor(d[:, :], d[:, :], 2.0, scr1b[:, :], Alu.mult, Alu.subtract),
           [(sModP, 1)])

    gate_fix(gab)
    gate_fix(gbb)
    V_([(sModP, sModP.n), (sBc1, 16)],
       lambda v: v.tensor_tensor(gabob[:, :], gab[:, :], bob[:, :], Alu.mult),
       [(sModP, 1)])
    V_([(sModB, nModB)],
       lambda v: v.tensor_scalar(cb1b[:, :], cb1b[:, :], 1.0, None, Alu.add),
       [(sModP, 1)])
    nModP = sModP.n
    if upto <= 0:
        return _finish([(sModP, nModP), (sModCC, nModCC)])

    sXn = sem("sXn")
    sTr1 = sem("sTr1")
    sHT = sem("sHT")
    xb_tok = []
    for i in range(T):
        b = i % 3
        w = [(sSq, T), (sSum, T)]
        if i >= 3:
            w.append((sXn, i - 2))
        SY(w, (lambda i_=i, b_=b: (lambda h: h.dma_start(
            out=xf_s[b_][:, :], in_=x_d[i_ * 128:(i_ + 1) * 128, :])))(),
           [(rA[b], 16)])
        xb_tok.append(rA[b].n)
    for i in range(T):
        b = i % 3
        bx = i % 2
        nld = xb_tok[i]
        w = [(rA[b], nld), (sStat, nStat)]
        if i >= 2:
            w.append((sTr1, DC * (i - 1)))
        S_(w, (lambda i_=i, b_=b, bx_=bx: (lambda s: s.activation(
            xn_buf[bx_][:, :], xf_s[b_][:, :], Act.Identity,
            bias=nbc[:, i_:i_ + 1], scale=rstdc[:, i_:i_ + 1])))(),
           [(sXn, 1)])
        for dc in range(DC):
            pb = (DC * i + dc) % 2
            w2 = [(sXn, i + 1), (sem("sIdent"), 2)]
            prev = DC * i + dc - 2
            if prev >= 0:
                w2.append((sHT, prev + 1))
            T_(w2, (lambda dc_=dc, pb_=pb, bx_=bx: (lambda t: t.matmul(
                bank(pb_, 128), xn_buf[bx_][:, dc_ * 128:(dc_ + 1) * 128], ident[:, :],
                start=True, stop=True)))(),
               [(sTr1, 1)])
            S_([(sTr1, DC * i + dc + 1), (sModCC, nModCC)],
               (lambda i_=i, dc_=dc, pb_=pb: (lambda s: s.activation(
                   hT[dc_][:, i_ * 128:(i_ + 1) * 128], bank(pb_, 128), Act.Identity,
                   bias=sacol[:, dc_:dc_ + 1], scale=ca1col[:, dc_:dc_ + 1])))(),
               [(sHT, 1)])
    nHT = sHT.n
    if upto <= 1:
        return _finish([(sHT, nHT)])
    AF.release(xf_ids)

    (wA_s, wA_ids) = AB.alloc(4)
    (wB_s, wB_ids) = AB.alloc(4)
    AB.release(xn_ids)

    def wsl(bufs, kc, lo, hi):
        return bufs[kc // 2][:, (kc % 2) * 1024 + lo:(kc % 2) * 1024 + hi]

    wring = {"wq": rW[0], "wk": rW[1], "wv": rW[2], "wo": rW[3]}
    wtok = {}

    def load_weight(bufs, wname, src_fd, extra_wait=()):
        r = wring[wname]
        for kc in range(DC):
            G_(list(extra_wait),
               (lambda s_=src_fd[kc * 128:(kc + 1) * 128, :], d_=wsl(bufs, kc, 0, 1024):
                (lambda g: g.dma_start(out=d_, in_=s_)))(),
               [(r, 16)])
        wtok[wname] = r.n

    load_weight(wA_s, "wq", wqT_d)
    load_weight(wB_s, "wk", wkT_d)

    sMMq = sem("sMMq")
    sQT = sem("sQT")

    def proj_fm(bufs, wname, bias_col, dst_of_ec, spill):
        sp_tok = {}
        for ns in range(NS):
            for ec in range(DC):
                gidx = ns * DC + ec
                pb = 4 + gidx % 2
                for kc in range(DC):
                    w = [(wring[wname], wtok[wname])]
                    if kc == 0:
                        w.append((sHT, min(DC * (4 * ns + 4), nHT)))
                        if sQT.n >= 1:
                            w.append((sQT, sQT.n - 1))
                    T_(w, (lambda ec_=ec, ns_=ns, kc_=kc, pb_=pb: (lambda t: t.matmul(
                        bank(pb_), wsl(bufs, kc_, ec_ * 128, ec_ * 128 + 128),
                        hT[kc_][:, ns_ * 512:(ns_ + 1) * 512],
                        start=(kc_ == 0), stop=(kc_ == DC - 1))))(),
                       [(sMMq, 1)] if kc == DC - 1 else [])
                wE = [(sMMq, sMMq.n)]
                if spill and ec >= 2:
                    wE.append((rQSp[ec % 2], sp_tok[(ec - 2, ns)]))
                S_(wE,
                   (lambda ec_=ec, ns_=ns, pb_=pb: (lambda s: s.activation(
                       dst_of_ec(ec_)[:, ns_ * 512:(ns_ + 1) * 512], bank(pb_), Act.Identity,
                       bias=bias_col[:, ec_:ec_ + 1], scale=1.0)))(),
                   [(sQT, 1)])
                if spill:
                    SY([(sQT, sQT.n)],
                       (lambda ec_=ec, ns_=ns: (lambda h: h.dma_start(
                           out=qTs_d[ec_ * 128:(ec_ + 1) * 128, ns_ * 512:(ns_ + 1) * 512],
                           in_=dst_of_ec(ec_)[:, ns_ * 512:(ns_ + 1) * 512])))(),
                       [(rQSp[ec % 2], 16)])
                    sp_tok[(ec, ns)] = rQSp[ec % 2].n

    (qst2, qst2_ids) = AB.alloc(2)
    proj_fm(wA_s, "wq", bqcol, lambda ec: qst2[ec % 2][:, 0:S], True)
    nQSp0 = rQSp[0].n
    nQSp1 = rQSp[1].n
    (kT_s, kT_ids) = AB.alloc(DC)
    kT = [kT_s[dc][:, 0:S] for dc in range(DC)]
    proj_fm(wB_s, "wk", bkcol, lambda ec: kT[ec], False)
    load_weight(wA_s, "wv", wvT_d, extra_wait=[(rQSp[0], nQSp0), (rQSp[1], nQSp1)])
    nQT_afterK = sQT.n

    (v_s, v_ids) = AB.alloc(DC)

    def v_sl(sc, lo, hi):
        return v_s[sc // 2][:, (sc % 2) * 1024 + lo:(sc % 2) * 1024 + hi]

    sMMv = sem("sMMv")
    sV = sem("sV")
    VG = [0]

    def emit_v_groups(n):
        tgt = min(2 * T, VG[0] + n)
        while VG[0] < tgt:
            gidx = VG[0]
            VG[0] += 1
            sc, ne = gidx // 2, gidx % 2
            pb = 4 + gidx % 2
            for kc in range(DC):
                w = [(wring["wv"], wtok["wv"])]
                if kc == 0:
                    w.append((sQT, nQT_afterK))
                    w.append((sem("sPTe"), sem("sPTe").n))
                    w.append((sem("sLqE"), sem("sLqE").n))
                    if gidx >= 2:
                        w.append((sV, gidx - 1))
                T_(w, (lambda sc_=sc, ne_=ne, kc_=kc, pb_=pb: (lambda t: t.matmul(
                    bank(pb_), hT[kc_][:, sc_ * 128:(sc_ + 1) * 128],
                    wsl(wA_s, kc_, ne_ * 512, (ne_ + 1) * 512),
                    start=(kc_ == 0), stop=(kc_ == DC - 1))))(),
                   [(sMMv, 1)] if kc == DC - 1 else [])
            V_([(sMMv, sMMv.n), (sBc0, 16)],
               (lambda sc_=sc, ne_=ne, pb_=pb: (lambda v: v.tensor_tensor(
                   v_sl(sc_, ne_ * 512, (ne_ + 1) * 512), bank(pb_),
                   bvb[:, ne_ * 512:(ne_ + 1) * 512], Alu.add)))(),
               [(sV, 1)])

    nV = 2 * T
    emit_v_groups(2 * T)
    load_weight(wB_s, "wo", woT_d, extra_wait=[(sQT, nQT_afterK)])
    if upto <= 2:
        return _finish([(sV, nV)])

    AB.release(hT_ids)
    AB.release(qst2_ids)
    AB.release(wA_ids)

    AB.release(trashb_ids)
    (lq_ab, lq_ids) = AB.alloc(7)
    lw = ([lq_ab[0].bitcast(F32)[:, 512 * j:512 * (j + 1)] for j in range(2)]
          + [lq_ab[1].bitcast(F32)[:, 512 * j:512 * (j + 1)] for j in range(2)])
    lq32 = [lq_ab[2].bitcast(I32)[:, 512 * j:512 * (j + 1)] for j in range(2)]
    lwq = [lq_ab[3 + i // 4][:, (i % 4) * 512:(i % 4 + 1) * 512] for i in range(12)]
    lwqT = [lq_ab[6][:, 512 * j:512 * (j + 1)] for j in range(2)]
    lam = [sumc[:, 0:16], sqc[:, 0:16]]
    lss = [stat1[:, 0:16], stat2[:, 0:16]]
    lrr = [rstdc[:, 0:16], nbc[:, 0:16]]

    lpanels = []
    for t in range(FF // 128):
        for hh2 in range(2):
            lpanels.append((wup_o_d, wup_bd, t, hh2, rStUp))
    for t in range(FF // 128):
        for hh2 in range(2):
            lpanels.append((wg_o_d, wg_bd, t, hh2, rStG))
    for t in range(D // 128):
        for hh2 in range(2 * IC // DC):
            lpanels.append((wdn_o_d, wdn_bd, t, hh2, rStDn))
    NPAN = len(lpanels)

    sLqV = sem("sLqV")
    sLqT = sem("sLqT")
    sLqE = sem("sLqE")
    sLqDone = sem("sLqDone")
    rLq = [rsem(f"rLq{j}") for j in range(4)]
    LQ = {"r": 0, "ld_tok": [0] * NPAN, "st_tok": [None] * NPAN,
          "vt_Q": [0] * NPAN, "vt_W": [0] * NPAN}

    def lq_load(j):
        if j >= NPAN:
            return
        src_d, dst_d, t, hh2, rSt = lpanels[j]
        w = []
        if j < 4:
            w += [(sMMv, 2 * T), (rQSp[0], nQSp0), (rQSp[1], nQSp1), (sSq, T)]
        else:
            w.append((sLqV, LQ["vt_Q"][j - 4]))
        G_(w, (lambda s_=src_d[t * 128:t * 128 + 128, hh2 * 512:hh2 * 512 + 512],
               d_=lw[j % 4]: (lambda g: g.dma_start(out=d_, in_=s_)))(),
           [(rLq[j % 4], 16)])
        LQ["ld_tok"][j] = rLq[j % 4].n

    def lq_tail(r):
        if r < 0 or r >= NPAN:
            return
        par = r % 2
        src_d, dst_d, t, hh2, rSt = lpanels[r]
        pb = 4 + r % 2
        for c in range(4):
            w2 = [(sem("sIdent"), 2)]
            if c == 0:
                w2 += [(sLqV, LQ["vt_W"][r]),
                       (sem("sPTe"), sem("sPTe").n), (sem("sA1"), sem("sA1").n),
                       (sQT, sQT.n), (sV, sV.n)]
                if r >= 2:
                    w2.append((sLqE, r - 1))
            T_(w2, (lambda m_=r % 12, c_=c, pb_=pb: (lambda tt: tt.matmul(
                bank(pb_)[:, c_ * 128:(c_ + 1) * 128],
                lwq[m_][:, c_ * 128:(c_ + 1) * 128], ident[:, :],
                start=True, stop=True)))(),
               [(sLqT, 1)])
        wE_ = [(sLqT, 4 * (r + 1))]
        if r >= 2 and LQ["st_tok"][r - 2] is not None:
            wE_.append(LQ["st_tok"][r - 2])
        S_(wE_, (lambda p_=par, pb_=pb: (lambda s: s.copy(lwqT[p_], bank(pb_))))(),
           [(sLqE, 1)])
        G_([(sLqE, r + 1)],
           (lambda p_=par, d_=dst_d, t_=t, h_=hh2: (lambda g: g.dma_start(
               out=d_[h_ * 512:h_ * 512 + 512, t_ * 128:t_ * 128 + 128]
               .rearrange("(c p) n -> p c n", p=128),
               in_=lwqT[p_].rearrange("p (c n) -> p c n", n=128))))(),
           [(rSt, 16)])
        LQ["st_tok"][r] = (rSt, rSt.n)

    def lq_round():
        r = LQ["r"]
        if r >= NPAN:
            return
        if r == 0:
            for j_ in range(3):
                lq_load(j_)
        LQ["r"] = r + 1
        par = r % 2
        th2 = LQ.get("th2", NPAN + 1)
        lamX = lam if r < th2 else LQ["lam2"]
        lssX = lss if r < th2 else LQ["lss2"]
        lrrX = lrr if r < th2 else LQ["lrr2"]
        wA_ = [(sLqV, sLqV.n), (rLq[r % 4], LQ["ld_tok"][r])]
        if r < 2:
            wA_ += [(sStat, nStat), (sXn, T)]
        V_(wA_, (lambda p_=par, q_=r % 4, la_=lamX: (lambda v: v.tensor_reduce(
            la_[p_], lw[q_].rearrange("p (b k) -> p b k", k=32),
            AX, Alu.max, apply_absolute_value=True)))(),
           [(sLqV, 1)])
        V_([(sLqV, sLqV.n)],
           (lambda p_=par, la_=lamX, ls_=lssX: (lambda v: v.tensor_scalar(
               ls_[p_], la_[p_], 1.0 / 7.0, EPS_Q, Alu.mult, Alu.max)))(),
           [(sLqV, 1)])
        V_([(sLqV, sLqV.n)],
           (lambda p_=par, ls_=lssX, lr_=lrrX: (lambda v: v.reciprocal(lr_[p_], ls_[p_])))(),
           [(sLqV, 1)])
        V_([(sLqV, sLqV.n)],
           (lambda p_=par, q_=r % 4, lr_=lrrX: (lambda v: v.tensor_tensor(
               lq32[p_].rearrange("p (b k) -> p b k", k=32),
               lw[q_].rearrange("p (b k) -> p b k", k=32),
               lr_[p_].unsqueeze(2).broadcast_to([128, 16, 32]), Alu.mult)))(),
           [(sLqV, 1)])
        LQ["vt_Q"][r] = sLqV.n
        wW_ = [(sLqV, sLqV.n)]
        if r >= 12:
            wW_.append((sLqT, 4 * (r - 11)))
        V_(wW_, (lambda p_=par, m_=r % 12, ls_=lssX: (lambda v: v.tensor_tensor(
            lwq[m_].rearrange("p (b k) -> p b k", k=32),
            lq32[p_].rearrange("p (b k) -> p b k", k=32),
            ls_[p_].unsqueeze(2).broadcast_to([128, 16, 32]), Alu.mult)))(),
           [(sLqV, 1)])
        LQ["vt_W"][r] = sLqV.n
        lq_tail(r - 8)
        lq_load(r + 3)



    n_pt_slots = max(1, (T * SB) // 2048)
    (PT_s, PT_ids) = AB.alloc(n_pt_slots)
    PSB = 2048 // SB
    PT = [PT_s[tc // PSB][:, (tc % PSB) * SB:(tc % PSB + 1) * SB] for tc in range(T)]
    (Pst_s, Pst_ids) = AB.alloc(2)
    (oT_s, oT_ids) = AB.alloc(2 if SB == 512 else 2)

    def oT_sl(dc, lo, hi):
        return oT_s[dc // 4][:, (dc % 4) * SB + lo:(dc % 4) * SB + hi]

    (qstv, qst_ids) = AB.alloc(1)

    def qstrip(par):
        return qstv[0][:, par * 1024:(par + 1) * 1024]

    AF.release(modrow_ids)
    scA, scA_ids = AF.alloc2()
    scB, scB_ids = AF.alloc2()
    scC, scC_ids = AF.alloc2()

    sSc = sem("sSc")
    sScE = sem("sScE")
    sTh = sem("sTh")
    sP = sem("sP")
    sPT = sem("sPT")
    sPTe = sem("sPTe")
    sPV = sem("sPV")
    sOT = sem("sOT")
    sPR = sem("sPR")
    sA1 = sem("sA1")
    sRes = sem("sRes")
    sRRS = sem("sRRS")
    (a1_s, a1_ids) = AF.alloc(2)
    (xr_s, xr_ids) = AF.alloc(2)
    ql_tok = [0] * T
    xr_tok = [0] * T
    x1st_tok = [0] * T

    def emit_qload(si):
        par = si % 2
        w = [(rQSp[0], nQSp0), (rQSp[1], nQSp1)]
        if si >= 2:
            w.append((sSc, si - 1))
        SY(w, (lambda si_=si, p_=par: (lambda h: h.dma_start(
            out=qstrip(p_).rearrange("p (c n) -> p c n", n=128),
            in_=qTs_d[:, si_ * 128:(si_ + 1) * 128].rearrange("(c p) n -> p c n", p=128))))(),
           [(rQ[par], 16)])
        ql_tok[si] = rQ[par].n

    def emit_scores(si):
        par = si % 2
        for nb4 in range(NS):
            for kc in range(DC):
                w = [(rQ[par], ql_tok[si])]
                if kc == 0:
                    w.append((sQT, nQT_afterK))
                    if si >= 1:
                        w.append((sScE, 3 * si))
                T_(w, (lambda nb_=nb4, kc_=kc, p_=par: (lambda t: t.matmul(
                    bank(nb_), qstrip(p_)[:, kc_ * 128:(kc_ + 1) * 128],
                    kT[kc_][:, nb_ * 512:(nb_ + 1) * 512],
                    start=(kc_ == 0), stop=(kc_ == DC - 1))))(),
                   [(sSc, 1)] if (kc == DC - 1 and nb4 == NS - 1) else [])

    def emit_theta(si):
        par = si % 2
        psc = ps_all[:, 0:S]
        wA_ = [(sSc, si + 1)]
        wB_ = [(sSc, si + 1)]
        wC_ = [(sSc, si + 1)]
        if si >= 1:
            wA_.append((sTh, 4 * (si - 1) + 1))
            wB_.append((sTh, 4 * (si - 1) + 2))
            wC_.append((sP, si))
        S_(wA_, lambda s: s.activation(scA[:, 0:S], psc, Act.Tanh, bias=0.0, scale=0.5),
           [(sScE, 1)])
        S_(wB_, lambda s: s.activation(scB[:, 0:S], psc, Act.Tanh, bias=0.0, scale=1.0),
           [(sScE, 1)])
        V_(wC_ + [(sScE, 3 * si + 2)],
           lambda v: v.tensor_scalar(scC[:, 0:S], psc, 0.0, 0.1, Alu.max, Alu.mult),
           [(sScE, 1)])
        V_([(sScE, 3 * si + 3)],
           lambda v: v.scalar_tensor_tensor(scC[:, 0:S], scA[:, 0:S], 0.1, scC[:, 0:S], Alu.mult, Alu.add),
           [(sTh, 1)])
        V_([(sTh, 4 * si + 1)],
           lambda v: v.scalar_tensor_tensor(scC[:, 0:S], scB[:, 0:S], 0.15, scC[:, 0:S], Alu.mult, Alu.add),
           [(sTh, 1)])
        V_([(sTh, 4 * si + 2)],
           (lambda si_=si: (lambda v: v.tensor_reduce(
               bcolc[:, si_:si_ + 1], scC[:, 0:S], AX, Alu.max, negate=True)))(),
           [(sTh, 1)])
        V_([(sTh, 4 * si + 3)],
           (lambda si_=si: (lambda v: v.tensor_scalar(
               bcolc[:, si_:si_ + 1], bcolc[:, si_:si_ + 1], 0.6, None, Alu.add)))(),
           [(sTh, 1)])
        wP = [(sTh, 4 * si + 4)]
        if si >= 2:
            wP.append((sPT, T * (si - 1)))
        S_(wP, (lambda si_=si, p_=par: (lambda s: s.activation(
            Pst_s[p_][:, 0:S], scC[:, 0:S], Act.Exp,
            bias=bcolc[:, si_:si_ + 1], scale=1.0,
            accum_out=rsc[:, si_:si_ + 1])))(),
           [(sP, 1)])

    def emit_PT(si):
        sp = si % SPB
        par = si % 2
        blkv = si // SPB
        for tc in range(T):
            pb = 4 + (tc % 2)
            w2 = [(sP, si + 1), (sem("sIdent"), 2)]
            if tc == 0:
                w2.append((sLqE, sLqE.n))
                w2.append((sV, sV.n))
            if sPTe.n >= 2:
                w2.append((sPTe, sPTe.n - 1))
            if blkv >= 1:
                w2.append((sPV, DC * blkv))
                w2.append((sA1, SPB * blkv))
            T_(w2, (lambda tc_=tc, pb_=pb, p_=par: (lambda t: t.matmul(
                bank(pb_, 128), Pst_s[p_][:, tc_ * 128:(tc_ + 1) * 128], ident[:, :],
                start=True, stop=True)))(),
               [(sPT, 1)])
            V_([(sPT, sPT.n)],
               (lambda tc_=tc, pb_=pb, sp_=sp: (lambda v: v.tensor_copy(
                   PT[tc_][:, sp_ * 128:(sp_ + 1) * 128], bank(pb_, 128))))(),
               [(sPTe, 1)])

    PTN = [0]

    def flush_PT(upto_si):
        while PTN[0] <= upto_si:
            emit_PT(PTN[0])
            PTN[0] += 1

    emit_qload(0)
    for blk in range(NBLK):
        for sp in range(SPB):
            si = blk * SPB + sp
            if si + 1 < T:
                emit_qload(si + 1)
            emit_scores(si)
            flush_PT(si - 1)
            emit_theta(si)
            while LQ["r"] < 8 + ((si + 1) * 80) // T:
                lq_round()
        flush_PT(blk * SPB + SPB - 1)
        nPTe = sPTe.n
        for dc in range(DC if upto >= 35 else 0):
            pb = 6 + dc % 2
            for tc in range(T):
                w = []
                if tc == 0:
                    w.append((sPTe, nPTe))
                    w.append((sV, nV))
                    if sOT.n >= 2:
                        w.append((sOT, sOT.n - 1))
                T_(w, (lambda dc_=dc, tc_=tc, pb_=pb: (lambda t: t.matmul(
                    bank(pb_, SB), v_sl(tc_, dc_ * 128, dc_ * 128 + 128),
                    PT[tc_][:, 0:SB], start=(tc_ == 0), stop=(tc_ == T - 1))))(),
                   [(sPV, 1)] if tc == T - 1 else [])
            S_([(sPV, sPV.n)],
               (lambda dc_=dc, pb_=pb: (lambda s: s.copy(oT_sl(dc_, 0, SB), bank(pb_, SB))))(),
               [(sOT, 1)])
        if upto >= 33:
            V_([(sP, (blk + 1) * SPB)],
               (lambda b_=blk: (lambda v: v.reciprocal(rrs[:, b_ * SPB:(b_ + 1) * SPB],
                                                       rsc[:, b_ * SPB:(b_ + 1) * SPB])))(),
               [(sRRS, 1)])
        nOT = sOT.n
        for sc2 in range(SPB if upto >= 40 else 0):
            si = blk * SPB + sc2
            par = si % 2
            w = []
            if si >= 2:
                w.append((sRes, 3 * (si - 1)))
            SY(w, (lambda si_=si, p_=par: (lambda h: h.dma_start(
                out=xr_s[p_][:, :], in_=x_d[si_ * 128:(si_ + 1) * 128, :])))(),
               [(rX[par], 16)])
            xr_tok[si] = rX[par].n
            for ne in range(2):
                pb = 4 + ne
                for dc in range(DC):
                    w = []
                    if dc == 0:
                        w.append((sOT, nOT))
                        w.append((wring["wo"], wtok["wo"]))
                        w.append((sPT, T * SPB * (blk + 1)))
                        w.append((sLqE, sLqE.n))
                        if sA1.n >= 1 and ne == 0:
                            pass
                        if si * 2 + ne >= 2:
                            w.append((sA1, sA1.n - 0 if False else (si * 2 + ne) // 2))
                    T_(w, (lambda sc_=sc2, ne_=ne, dc_=dc, pb_=pb: (lambda t: t.matmul(
                        bank(pb_), oT_sl(dc_, sc_ * 128, sc_ * 128 + 128),
                        wsl(wB_s, dc_, ne_ * 512, (ne_ + 1) * 512),
                        start=(dc_ == 0), stop=(dc_ == DC - 1))))(),
                       [(sPR, 1)] if dc == DC - 1 else [])
            wEv = [(sPR, sPR.n), (sRRS, blk + 1)]
            if si >= 2:
                wEv.append((rX1s[par], x1st_tok[si - 2]))
            S_(wEv,
               (lambda si_=si, p_=par: (lambda s: s.activation(
                   a1_s[p_][:, :], ps_all[:, 4 * 512:4 * 512 + D], Act.Identity,
                   bias=zeroc[:, 0:1], scale=rrs[:, si_:si_ + 1])))(),
               [(sA1, 1)])
            wR1 = [(sA1, sA1.n), (sModP, nModP)]
            V_(wR1,
               (lambda p_=par: (lambda v: v.tensor_tensor(a1_s[p_][:, :], a1_s[p_][:, :], gab[:, :], Alu.mult)))(),
               [(sRes, 1)])
            V_([(sRes, sRes.n)],
               (lambda p_=par: (lambda v: v.tensor_tensor(a1_s[p_][:, :], a1_s[p_][:, :], gabob[:, :], Alu.add)))(),
               [(sRes, 1)])
            V_([(sRes, sRes.n), (rX[par], xr_tok[si])],
               (lambda p_=par: (lambda v: v.tensor_tensor(a1_s[p_][:, :], a1_s[p_][:, :], xr_s[p_][:, :], Alu.add)))(),
               [(sRes, 1)])
            SY([(sRes, sRes.n)],
               (lambda si_=si, p_=par: (lambda h: h.dma_start(
                   out=x1_d[si_ * 128:(si_ + 1) * 128, :], in_=a1_s[p_][:, :])))(),
               [(rX1s[par], 16)])
            x1st_tok[si] = rX1s[par].n

    AB.release(kT_ids); AB.release(v_ids); AB.release(PT_ids); AB.release(Pst_ids)
    AB.release(oT_ids); AB.release(qst_ids); AB.release(wB_ids)
    AF.release(scA_ids); AF.release(scB_ids); AF.release(scC_ids)
    AF.release(a1_ids); AF.release(xr_ids)

    (x1r_s, x1r_ids) = AF.alloc(3)
    (lsc2_s, lsc2_ids) = AF.alloc(1)
    LQ["lam2"] = [lsc2_s[0][:, 16 * j:16 * (j + 1)] for j in range(2)]
    LQ["lss2"] = [lsc2_s[0][:, 32 + 16 * j:48 + 16 * j] for j in range(2)]
    LQ["lrr2"] = [lsc2_s[0][:, 64 + 16 * j:80 + 16 * j] for j in range(2)]
    LQ["th2"] = LQ["r"]
    sSq2 = sem("sSq2")
    sSum2 = sem("sSum2")
    sStat2 = sem("sStat2")
    x1a_tok = []
    for i in range(T):
        b = i % 3
        w = [(rX1s[i % 2], x1st_tok[i])]
        if i >= 3:
            w.append((sSq2, i - 2))
            w.append((sSum2, i - 2))
        SY(w, (lambda i_=i, b_=b: (lambda h: h.dma_start(
            out=x1r_s[b_][:, :], in_=x1_d[i_ * 128:(i_ + 1) * 128, :])))(),
           [(rA[b], 16)])
        x1a_tok.append(rA[b].n)
    for i in range(T):
        b = i % 3
        nld = x1a_tok[i]
        S_([(rA[b], nld), (sSq2, i)],
           (lambda i_=i, b_=b: (lambda s: s.activation(
               trash[:, :], x1r_s[b_][:, :], Act.Square, bias=0.0, scale=1.0,
               accum_out=sqc[:, i_:i_ + 1])))(),
           [(sSq2, 1)])
        V_([(rA[b], nld)],
           (lambda i_=i, b_=b: (lambda v: v.reduce_sum(sumc[:, i_:i_ + 1], x1r_s[b_][:, :], AX)))(),
           [(sSum2, 1)])
        while LQ["r"] < 88 + ((i + 1) * 28) // T:
            lq_round()
    ln_stats(sSum2, sSq2, sStat2)
    nStat2 = sStat2.n

    (h2_s, h2_ids) = AB.alloc(DC)

    def h2_sl(i):
        return h2_s[i // 2][:, (i % 2) * 1024:(i % 2 + 1) * 1024]

    (h2f_s, h2f_ids) = AF.alloc(2)
    sXn2 = sem("sXn2")
    sH2 = sem("sH2")
    sH2V = sem("sH2V")
    sH2M = sem("sH2M")
    x1b_tok = []
    for i in range(T):
        b = i % 3
        w = [(sSq2, T), (sSum2, T)]
        if i >= 3:
            w.append((sXn2, i - 2))
        SY(w, (lambda i_=i, b_=b: (lambda h: h.dma_start(
            out=x1r_s[b_][:, :], in_=x1_d[i_ * 128:(i_ + 1) * 128, :])))(),
           [(rA[b], 16)])
        x1b_tok.append(rA[b].n)
    for i in range(T):
        b = i % 3
        bx = i % 2
        nld = x1b_tok[i]
        w = [(rA[b], nld), (sStat2, nStat2)]
        if i >= 2:
            w.append((sH2, 2 * (i - 1)))
        S_(w, (lambda i_=i, b_=b, bx_=bx: (lambda s: s.activation(
            h2f_s[bx_][:, :], x1r_s[b_][:, :], Act.Identity,
            bias=nbc[:, i_:i_ + 1], scale=rstdc[:, i_:i_ + 1])))(),
           [(sXn2, 1)])
        V_([(sXn2, i + 1), (sModP, nModP)],
           (lambda bx_=bx: (lambda v: v.tensor_tensor(h2f_s[bx_][:, :], h2f_s[bx_][:, :], cb1b[:, :], Alu.mult)))(),
           [(sH2, 1)])
        V_([(sH2, 2 * i + 1)],
           (lambda i_=i, bx_=bx: (lambda v: v.tensor_tensor(h2_sl(i_), h2f_s[bx_][:, :], sbb[:, :], Alu.add)))(),
           [(sH2, 1)])
        V_([(sH2, 2 * i + 2)],
           (lambda i_=i: (lambda v: v.tensor_reduce(
               bcolc[:, i_:i_ + 1], h2_sl(i_), AX, Alu.max,
               apply_absolute_value=True)))(),
           [(sH2V, 1)])
        G_([(sH2V, i + 1)],
           (lambda i_=i: (lambda g: g.tensor_reduce(redrow[0:1, 96 + i_:97 + i_],
                                                    bcolc[:, i_:i_ + 1],
                                                    mybir.AxisListType.XYZWC, Alu.max)))(),
           [(sH2M, 1)])
        while LQ["r"] < 116 + ((i + 1) * 28) // T:
            lq_round()
    while LQ["r"] < NPAN:
        lq_round()
    for k_ in range(8):
        lq_tail(NPAN - 8 + k_)
    G_([(rStUp, rStUp.n), (rStG, rStG.n), (rStDn, rStDn.n)],
       lambda g: g.memset(scal[0:1, 63:64], 0.0), [(sLqDone, 1)])
    AB.release(lq_ids)
    AF.release(lsc2_ids)

    sBar = sem("sBar")
    sCC = gCC
    sCCD = gCCD

    def cross_core_max(srcs, cc_idx, waits):
        for j, (lo, hi, idx) in enumerate(srcs):
            V_(list(waits) + [(sBar, sBar.n)],
               (lambda lo_=lo, hi_=hi, i_=idx: (lambda v: v.tensor_reduce(
                   scal[0:1, i_:i_ + 1], redrow[0:1, lo_:hi_], AX, Alu.max)))(),
               [(sBar, 1)])
        idx0 = srcs[0][2]
        nv = len(srcs)
        G_([(sBar, sBar.n)],
           (lambda ci=cc_idx, i0=idx0, n_=nv: (lambda g: g.dma_start(
               out=cc_in[ci][:, 0:n_], in_=scal[0:1, i0:i0 + n_])))(),
           [(sCCD, 16)])
        G_([(sCCD, sCCD.n)],
           (lambda ci=cc_idx, n_=nv: (lambda g: g.collective_compute(
               "AllReduce", Alu.max, replica_groups=[list(range(ncores))],
               ins=[cc_in[ci][:, 0:n_].opt()], outs=[cc_out[ci][:, 0:n_].opt()])))(),
           [(sCC, 1)])
        G_([(sCC, sCC.n)],
           (lambda ci=cc_idx, i0=idx0, n_=nv: (lambda g: g.dma_start(
               out=scal[0:1, i0:i0 + n_], in_=cc_out[ci][:, 0:n_])))(),
           [(sCCD, 16)])

    def bcast_scalar(src_idx, dst_col, waits):
        via_b7(
            list(waits),
            (lambda i_=src_idx: (lambda t: t.matmul(bank(7, 128)[:, 0:1], onesf[:, :],
                                                    scal[0:1, i_:i_ + 1], start=True, stop=True)))(),
            "scalar",
            (lambda c_=dst_col: (lambda s: s.copy(bsc[:, c_:c_ + 1], bank(7, 128)[:, 0:1])))(),
            [(sBar, 1)])

    cross_core_max([(96, 96 + T, 8)], 0, [(sH2M, T)])
    V_([(sCCD, sCCD.n)],
       lambda v: v.tensor_scalar(scal[0:1, 0:1], scal[0:1, 8:9], 1.0 / QMAX, EPS_Q, Alu.mult, Alu.max),
       [(sBar, 1)])
    V_([(sBar, sBar.n)],
       lambda v: v.reciprocal(scal[0:1, 1:2], scal[0:1, 0:1]),
       [(sBar, 1)])
    bcast_scalar(1, 0, [(sBar, sBar.n)])
    nBar1 = sBar.n

    (nfx_s, nfx_ids) = AB.alloc(DC)

    def nfx_sl(i):
        return nfx_s[i // 2][:, (i % 2) * 1024:(i % 2 + 1) * 1024]

    sTq = sem("sTq")
    sNfx = sem("sNfx")
    for i in range(T):
        w = [(sBar, nBar1)]
        if i >= 1:
            w.append((sNfx, i))
        S_(w, (lambda i_=i: (lambda s: s.activation(
            qscr[:, 0:1024], h2_sl(i_), Act.Identity,
            bias=0.0, scale=bsc[:, 0:1])))(),
           [(sTq, 1)])
        V_([(sTq, i + 1)],
           (lambda i_=i: (lambda v: v.tensor_copy(nfx_sl(i_), qscr[:, 0:1024])))(),
           [(sNfx, 1)])
    (nfxT_s, nfxT_ids) = AB.alloc(DC)
    nfxT = [nfxT_s[dc][:, 0:S] for dc in range(DC)]
    sTr2 = sem("sTr2")
    sNfxT = sem("sNfxT")
    for i in range(T):
        for dc in range(DC):
            pb = (DC * i + dc) % 4
            w = [(sNfx, i + 1), (sem("sIdent"), 2)]
            prev = DC * i + dc - 4
            if prev >= 0:
                w.append((sNfxT, prev + 1))
            T_(w, (lambda i_=i, dc_=dc, pb_=pb: (lambda t: t.matmul(
                bank(pb_, 128), nfx_sl(i_)[:, dc_ * 128:(dc_ + 1) * 128], ident[:, :],
                start=True, stop=True)))(),
               [(sTr2, 1)])
            S_([(sTr2, DC * i + dc + 1)],
               (lambda i_=i, dc_=dc, pb_=pb: (lambda s: s.copy(
                   nfxT[dc_][:, i_ * 128:(i_ + 1) * 128], bank(pb_, 128))))(),
               [(sNfxT, 1)])
    nNfxT = sNfxT.n
    if upto <= 50:
        return _finish([(sNfxT, nNfxT)])
    AB.release(h2_ids)
    AB.release(nfx_ids)
    AF.release(x1r_ids); AF.release(h2f_ids)

    (wfb, wfb_ids) = AB.alloc(1)

    def wf_buf(j):
        return wfb[0][:, (j % 2) * 1024:(j % 2) * 1024 + 1024]

    (upc_s, upc_ids) = AB.alloc(2)
    sMMu = sem("sMMu")
    sUpE = sem("sUpE")
    sUpMin = sem("sUpMin")
    wup_tok = [0] * IC
    upst_tok = [0] * IC

    def wup_load(ic):
        j = ic % 2
        w = [(rStUp, 24 * 16), (sNfxT, nNfxT)]
        if ic < 2:
            w.append((sLqDone, 1))
        if ic >= 2:
            w.append((sMMu, ic - 1))
        src = wup_bd[:, ic * 128:(ic + 1) * 128].rearrange("(c p) n -> p c n", p=128)
        SY(w, (lambda s_=src, j_=j: (lambda h: h.dma_start(
            out=wf_buf(j_).rearrange("p (c n) -> p c n", n=128), in_=s_)))(),
           [(rW[j], 16)])
        wup_tok[ic] = rW[j].n

    for ic in range(2):
        wup_load(ic)
    for ic in range(IC):
        j = ic % 2
        pbase = (ic % 2) * 4
        for kc in range(DC):
            for ns2 in range(NS):
                w = [(rW[j], wup_tok[ic])]
                if kc == 0 and ns2 == 0:
                    w.append((sNfxT, nNfxT))
                    if ic < 2:
                        w.append((sLqE, NPAN))
                    if ic >= 2:
                        w.append((sUpE, ic - 1))
                T_(w, (lambda kc_=kc, ns_=ns2, pb_=pbase, j_=j: (lambda t: t.matmul(
                    bank(pb_ + ns_), wf_buf(j_)[:, kc_ * 128:(kc_ + 1) * 128],
                    nfxT[kc_][:, ns_ * 512:(ns_ + 1) * 512],
                    start=(kc_ == 0), stop=(kc_ == DC - 1))))(),
                   [(sMMu, 1)] if (kc == DC - 1 and ns2 == NS - 1) else [])
        par = ic % 2
        w = [(sMMu, ic + 1)]
        if ic < 2:
            w.append((sLqDone, 1))
        if ic >= 2:
            w.append((rUpSt[par], upst_tok[ic - 2]))
        S_(w, (lambda ic_=ic, p_=par, pb_=pbase: (lambda s: s.copy(
            upc_s[p_][:, 0:S], ps_all[:, pb_ * 512:pb_ * 512 + S])))(),
           [(sUpE, 1)])
        G_([(sUpE, ic + 1)],
           (lambda ic_=ic, p_=par: (lambda g: g.tensor_reduce(
               redrow[0:1, ic_:ic_ + 1],
               upc_s[p_][:, 0:S], mybir.AxisListType.XYZWC, Alu.max, apply_absolute_value=True)))(),
           [(sUpMin, 1)])
        SY([(sUpMin, ic + 1)],
           (lambda ic_=ic, p_=par: (lambda h: h.dma_start(
               out=upraw_d[ic_, :, :], in_=upc_s[p_][:, 0:S])))(),
           [(rUpSt[par], 16)])
        upst_tok[ic] = rUpSt[par].n
        if ic + 2 < IC:
            wup_load(ic + 2)
    nUpSt0 = rUpSt[0].n
    nUpSt1 = rUpSt[1].n
    if upto <= 60:
        return _finish([(rUpSt[0], nUpSt0), (rUpSt[1], nUpSt1)])

    (gate_s, gate_ids) = AB.alloc(IC)

    def gate_sl(ic):
        return gate_s[ic][:, 0:S]

    sMMg = sem("sMMg")
    sGE = sem("sGE")
    sGMin = sem("sGMin")
    wg_tok = []
    for ic in range(IC):
        j = ic % 2
        w = [(rStG, 24 * 16), (sMMu, IC)]
        if ic >= 2:
            w.append((sMMg, ic - 1))
        src = wg_bd[:, ic * 128:(ic + 1) * 128].rearrange("(c p) n -> p c n", p=128)
        SY(w, (lambda s_=src, j_=j: (lambda h: h.dma_start(
            out=wf_buf(j_).rearrange("p (c n) -> p c n", n=128), in_=s_)))(),
           [(rW[j], 16)])
        wg_tok.append(rW[j].n)
    for ic in range(IC):
        j = ic % 2
        pbase = (ic % 2) * 4
        for kc in range(DC):
            for ns2 in range(NS):
                w = [(rW[j], wg_tok[ic])]
                if kc == 0 and ns2 == 0:
                    w.append((sUpE, IC))
                    if ic >= 2:
                        w.append((sGE, ic - 1))
                T_(w, (lambda kc_=kc, ns_=ns2, pb_=pbase, j_=j: (lambda t: t.matmul(
                    bank(pb_ + ns_), wf_buf(j_)[:, kc_ * 128:(kc_ + 1) * 128],
                    nfxT[kc_][:, ns_ * 512:(ns_ + 1) * 512],
                    start=(kc_ == 0), stop=(kc_ == DC - 1))))(),
                   [(sMMg, 1)] if (kc == DC - 1 and ns2 == NS - 1) else [])
        S_([(sMMg, ic + 1)],
           (lambda ic_=ic, pb_=pbase: (lambda s: s.copy(
               gate_sl(ic_), ps_all[:, pb_ * 512:pb_ * 512 + S])))(),
           [(sGE, 1)])
        G_([(sGE, ic + 1)],
           (lambda ic_=ic: (lambda g: g.tensor_reduce(
               redrow[0:1, 24 + ic_:25 + ic_], gate_sl(ic_),
               mybir.AxisListType.XYZWC, Alu.max, apply_absolute_value=True)))(),
           [(sGMin, 1)])
    AB.release(nfxT_ids)

    cross_core_max([(0, IC, 10), (24, 24 + IC, 11)], 1, [(sUpMin, IC), (sGMin, IC)])
    nCCD2 = sCCD.n
    V_([(sCCD, nCCD2)],
       lambda v: v.scalar_tensor_tensor(scal[0:1, 2:3], scal[0:1, 10:11], 1.0 / QMAX, scal[0:1, 0:1], Alu.mult, Alu.mult),
       [(sBar, 1)])
    V_([(sBar, sBar.n)],
       lambda v: v.tensor_scalar(scal[0:1, 2:3], scal[0:1, 2:3], EPS_Q, None, Alu.max), [(sBar, 1)])
    V_([(sCCD, nCCD2)],
       lambda v: v.scalar_tensor_tensor(scal[0:1, 3:4], scal[0:1, 11:12], 1.0 / QMAX, scal[0:1, 0:1], Alu.mult, Alu.mult),
       [(sBar, 1)])
    V_([(sBar, sBar.n)],
       lambda v: v.tensor_scalar(scal[0:1, 3:4], scal[0:1, 3:4], EPS_Q, None, Alu.max), [(sBar, 1)])
    V_([(sBar, sBar.n)],
       lambda v: v.reciprocal(scal[0:1, 4:5], scal[0:1, 3:4]), [(sBar, 1)])
    V_([(sBar, sBar.n)],
       lambda v: v.scalar_tensor_tensor(scal[0:1, 5:6], scal[0:1, 0:1], 1.0, scal[0:1, 4:5], Alu.mult, Alu.mult),
       [(sBar, 1)])
    V_([(sBar, sBar.n)],
       lambda v: v.tensor_scalar(scal[0:1, 6:7], scal[0:1, 3:4], 0.5, None, Alu.mult),
       [(sBar, 1)])
    bcast_scalar(5, 1, [(sBar, sBar.n)])
    bcast_scalar(6, 2, [(sBar, sBar.n)])
    nBar2 = sBar.n
    if upto <= 70:
        return _finish([(sBar, nBar2)])

    tgA, tgA_ids = AF.alloc2()
    tgB, tgB_ids = AF.alloc2()
    (nfs_s, nfs_ids) = AB.alloc(2)
    sTg = sem("sTg")
    sNfg = sem("sNfg")
    sTnh = sem("sTnh")
    sT2 = sem("sT2")
    sNfs = sem("sNfs")
    sClip = sem("sClip")
    sG2 = sem("sG2")
    sG2V = sem("sG2V")
    sG2M = sem("sG2M")
    sQs = sem("sQs")
    NH = max(S // 1024, 1)
    HW_ = min(1024, S)

    def qpass(src_fn, dst_fn, scale_col, extra_w_first=(), extra_inc_last=(),
              pre=None):
        for hh in range(NH):
            lo, hi = hh * HW_, (hh + 1) * HW_
            w = [(sQs, sQs.n)]
            if hh == 0:
                w += list(extra_w_first)
            V_(w, (lambda lo_=lo, hi_=hi, sf=src_fn, sc_=scale_col: (lambda v: v.tensor_tensor(
                qscr[:, 0:hi_ - lo_], sf(lo_, hi_),
                bsc[:, sc_:sc_ + 1].broadcast_to([128, hi_ - lo_]), Alu.mult)))(),
               [(sQs, 1)])
            V_([(sQs, sQs.n)],
               (lambda lo_=lo, hi_=hi, df=dst_fn: (lambda v: v.tensor_copy(
                   df(lo_, hi_), qscr[:, 0:hi_ - lo_])))(),
               [(sQs, 1)] + (list(extra_inc_last) if hh == NH - 1 else []))

    qscr2A, qscr2A_ids = AB.alloc2()
    qscr2B, qscr2B_ids = AB.alloc2()
    qscr2 = [qscr2A.bitcast(I32), qscr2B.bitcast(I32)]
    sQ2 = sem("sQ2")
    q2_tok = {0: 0, 1: 0}

    def qpass_sg(src_fn, dst_fn, scale_col, par2, extra_w_first=(), extra_inc_last=()):
        w = [(sQ2, q2_tok[par2])] + list(extra_w_first)
        S_(w, (lambda sf=src_fn, sc_=scale_col, p_=par2: (lambda s: s.activation(
            qscr2[p_][:, 0:S], sf(0, S), Act.Identity,
            bias=0.0, scale=bsc[:, sc_:sc_ + 1])))(),
           [(sQ2, 1)])
        V_([(sQ2, sQ2.n)],
           (lambda df=dst_fn, p_=par2: (lambda v: v.tensor_copy(
               df(0, S), qscr2[p_][:, 0:S])))(),
           [(sQ2, 1)] + list(extra_inc_last))
        q2_tok[par2] = sQ2.n

    def p8_stage1(ic):
        par = ic % 2
        qpass_sg((lambda lo, hi, ic_=ic: gate_sl(ic_)[:, lo:hi]),
                 (lambda lo, hi, ic_=ic: gate_sl(ic_)[:, lo:hi]),
                 1, par, extra_w_first=[(sGE, IC), (sBar, nBar2)],
                 extra_inc_last=[(sNfg, 1)])

    def p8_stage2(ic):
        par = ic % 2
        tg = tgA if par == 0 else tgB
        wTn = [(sNfg, ic + 1)]
        if ic >= 2:
            wTn.append((sT2, ic - 1))
        S_(wTn,
           (lambda ic_=ic, t_=tg: (lambda s: s.activation(
               t_[:, 0:S], gate_sl(ic_), Act.Tanh, bias=zeroc[:, 0:1], scale=bsc[:, 2:3])))(),
           [(sTnh, 1)])
        wSg = [(sTnh, ic + 1), (sQ2, q2_tok[par])]
        if ic == 0:
            wSg.append((sem("sConst"), 2))
        S_(wSg,
           (lambda t_=tg, p_=par: (lambda s: s.activation(
               qscr2[p_][:, 0:S], t_[:, 0:S], Act.Identity,
               bias=bsc[:, 7:8], scale=32768.0)))(),
           [(sQ2, 1), (sT2, 1)])
        w = [(sQ2, sQ2.n)]
        if ic >= 2:
            w.append((sG2, ic - 1))
        V_(w, (lambda p_=par: (lambda v: v.tensor_scalar(
            nfs_s[p_][:, 0:S], qscr2[p_][:, 0:S], 65535.0, None, Alu.min)))(),
           [(sQ2, 1), (sClip, 1)])
        q2_tok[par] = sQ2.n
        V_([(sClip, ic + 1)],
           (lambda ic_=ic, p_=par: (lambda v: v.tensor_tensor(
               gate_sl(ic_), gate_sl(ic_), nfs_s[p_][:, 0:S], Alu.mult)))(),
           [(sG2, 1)])
        wRm = [(sG2, ic + 1)]
        if ic >= 16:
            wRm.append((sG2M, ic - 15))
        V_(wRm,
           (lambda ic_=ic: (lambda v: v.tensor_reduce(
               bcolc[:, ic_ % 16:ic_ % 16 + 1], gate_sl(ic_), AX, Alu.max,
               apply_absolute_value=True)))(),
           [(sG2V, 1)])
        G_([(sG2V, ic + 1)],
           (lambda ic_=ic: (lambda g: g.tensor_reduce(
               redrow[0:1, 48 + ic_:49 + ic_], bcolc[:, ic_ % 16:ic_ % 16 + 1],
               mybir.AxisListType.XYZWC, Alu.max)))(),
           [(sG2M, 1)])

    for ic in range(IC + 1):
        if ic < IC:
            p8_stage1(ic)
        if ic >= 1:
            p8_stage2(ic - 1)
    AF.release(tgA_ids)
    AF.release(tgB_ids)
    cross_core_max([(48, 48 + IC, 12)], 2, [(sG2M, IC)])
    nCCD3 = sCCD.n
    V_([(sCCD, nCCD3)],
       lambda v: v.scalar_tensor_tensor(scal[0:1, 7:8], scal[0:1, 12:13], 2.0 ** -16 / QMAX, scal[0:1, 3:4], Alu.mult, Alu.mult),
       [(sBar, 1)])
    V_([(sBar, sBar.n)],
       lambda v: v.tensor_scalar(scal[0:1, 7:8], scal[0:1, 7:8], EPS_Q, None, Alu.max), [(sBar, 1)])
    V_([(sBar, sBar.n)],
       lambda v: v.reciprocal(scal[0:1, 16:17], scal[0:1, 7:8]), [(sBar, 1)])
    V_([(sBar, sBar.n)],
       lambda v: v.scalar_tensor_tensor(scal[0:1, 17:18], scal[0:1, 3:4], 2.0 ** -16, scal[0:1, 16:17], Alu.mult, Alu.mult),
       [(sBar, 1)])
    V_([(sBar, sBar.n)],
       lambda v: v.reciprocal(scal[0:1, 18:19], scal[0:1, 2:3]), [(sBar, 1)])
    V_([(sBar, sBar.n)],
       lambda v: v.scalar_tensor_tensor(scal[0:1, 19:20], scal[0:1, 0:1], 1.0, scal[0:1, 18:19], Alu.mult, Alu.mult),
       [(sBar, 1)])
    bcast_scalar(17, 3, [(sBar, sBar.n)])
    bcast_scalar(19, 4, [(sBar, sBar.n)])
    nBar3 = sBar.n
    if upto <= 80:
        return _finish([(sBar, nBar3)])

    AB.release(upc_ids)
    (upr_s, upr_ids) = AB.alloc(3)
    (nfu_s, nfu_ids) = AB.alloc(2)
    sTu = sem("sTu")
    sNfu = sem("sNfu")
    sTg2 = sem("sTg2")
    sNfg2 = sem("sNfg2")
    sO = sem("sO")
    sOV = sem("sOV")
    sOM = sem("sOM")
    upr_tok = []
    for ic in range(IC):
        b = ic % 3
        w = [(rUpSt[0], nUpSt0), (rUpSt[1], nUpSt1), (sMMg, IC)]
        if ic >= 3:
            w.append((sNfu, ic - 2))
        SY(w, (lambda ic_=ic, b_=b: (lambda h: h.dma_start(
            out=upr_s[b_][:, 0:S], in_=upraw_d[ic_, :, :])))(),
           [(rA[b], 16)])
        upr_tok.append(rA[b].n)
    def p9_stage1(ic):
        b = ic % 3
        par = ic % 2
        nld = upr_tok[ic]
        wA1 = [(rA[b], nld), (sBar, nBar3)]
        if ic >= 2:
            wA1.append((sO, ic - 1))
        qpass_sg((lambda lo, hi, b_=b: upr_s[b_][:, lo:hi]),
                 (lambda lo, hi, p_=par: nfu_s[p_][:, lo:hi]),
                 4, par, extra_w_first=wA1, extra_inc_last=[(sNfu, 1)])

    def p9_stage2(ic):
        par = ic % 2
        qpass_sg((lambda lo, hi, ic_=ic: gate_sl(ic_)[:, lo:hi]),
                 (lambda lo, hi, ic_=ic: gate_sl(ic_)[:, lo:hi]),
                 3, par, extra_w_first=[(sG2M, IC), (sBar, nBar3), (sNfu, ic + 1)],
                 extra_inc_last=[(sNfg2, 1)])
        V_([(sNfg2, ic + 1), (sNfu, ic + 1)],
           (lambda ic_=ic, p_=par: (lambda v: v.tensor_tensor(
               gate_sl(ic_), gate_sl(ic_), nfu_s[p_][:, 0:S], Alu.mult)))(),
           [(sO, 1)])
        wRm9 = [(sO, ic + 1)]
        if ic == 0:
            wRm9.append((sG2M, IC))
        if ic >= 16:
            wRm9.append((sOM, ic - 15))
        V_(wRm9,
           (lambda ic_=ic: (lambda v: v.tensor_reduce(
               bcolc[:, ic_ % 16:ic_ % 16 + 1], gate_sl(ic_), AX, Alu.max,
               apply_absolute_value=True)))(),
           [(sOV, 1)])
        G_([(sOV, ic + 1)],
           (lambda ic_=ic: (lambda g: g.tensor_reduce(
               redrow[0:1, 72 + ic_:73 + ic_], bcolc[:, ic_ % 16:ic_ % 16 + 1],
               mybir.AxisListType.XYZWC, Alu.max)))(),
           [(sOM, 1)])

    for ic in range(IC + 1):
        if ic < IC:
            p9_stage1(ic)
        if ic >= 1:
            p9_stage2(ic - 1)
    cross_core_max([(72, 72 + IC, 13)], 3, [(sOM, IC)])
    nCCD4 = sCCD.n
    V_([(sCCD, nCCD4)],
       lambda v: v.scalar_tensor_tensor(scal[0:1, 20:21], scal[0:1, 2:3], 1.0, scal[0:1, 7:8], Alu.mult, Alu.mult),
       [(sBar, 1)])
    V_([(sBar, sBar.n)],
       lambda v: v.scalar_tensor_tensor(scal[0:1, 21:22], scal[0:1, 13:14], 1.0 / QMAX, scal[0:1, 20:21], Alu.mult, Alu.mult),
       [(sBar, 1)])
    V_([(sBar, sBar.n)],
       lambda v: v.tensor_scalar(scal[0:1, 21:22], scal[0:1, 21:22], EPS_Q, None, Alu.max), [(sBar, 1)])
    V_([(sBar, sBar.n)],
       lambda v: v.reciprocal(scal[0:1, 22:23], scal[0:1, 21:22]), [(sBar, 1)])
    V_([(sBar, sBar.n)],
       lambda v: v.scalar_tensor_tensor(scal[0:1, 23:24], scal[0:1, 20:21], 1.0, scal[0:1, 22:23], Alu.mult, Alu.mult),
       [(sBar, 1)])
    bcast_scalar(23, 5, [(sBar, sBar.n)])
    bcast_scalar(21, 6, [(sBar, sBar.n)])
    nBar4 = sBar.n

    sTo = sem("sTo")
    sNfo = sem("sNfo")
    q3A, q3A_ids = AF.alloc2()
    q3B, q3B_ids = AF.alloc2()
    qscr3 = [q3A.bitcast(I32), q3B.bitcast(I32)]
    sQ3 = sem("sQ3")
    q3_tok = {0: 0, 1: 0}
    for ic in range(IC):
        par3 = ic % 2
        S_([(sQ3, q3_tok[par3]), (sBar, nBar4), (sOM, IC)],
           (lambda ic_=ic, p_=par3: (lambda s: s.activation(
               qscr3[p_][:, 0:S], gate_sl(ic_), Act.Identity,
               bias=0.0, scale=bsc[:, 5:6])))(),
           [(sQ3, 1)])
        V_([(sQ3, sQ3.n)],
           (lambda ic_=ic, p_=par3: (lambda v: v.tensor_copy(
               gate_sl(ic_), qscr3[p_][:, 0:S])))(),
           [(sQ3, 1), (sNfo, 1)])
        q3_tok[par3] = sQ3.n
    AB.release(upr_ids); AB.release(nfu_ids)
    AB.release(nfs_ids); AB.release(wfb_ids)
    AB.release(qscr2A_ids); AB.release(qscr2B_ids)
    (wdn_s, wdn_ids) = AB.alloc(12)

    def wdn_sl(ic):
        return wdn_s[ic // 2][:, (ic % 2) * 1024:(ic % 2 + 1) * 1024]

    for ic in range(IC):
        src = wdn_bd[ic * 128:(ic + 1) * 128, :]
        SY([(rStDn, 24 * 16), (sO, IC)],
           (lambda s_=src, ic_=ic: (lambda h: h.dma_start(out=wdn_sl(ic_), in_=s_)))(),
           [(rW[0], 16)])
    nWdnL = rW[0].n

    sMMd = sem("sMMd")
    sYE = sem("sYE")
    sYR = sem("sYR")
    (x1rr_s, x1rr_ids) = AF.alloc(2)
    (yv_s, yv_ids) = AF.alloc(2)
    x1rr_tok = [0] * T
    yst_tok = [0] * T

    def x1rr_load(sc):
        par = sc % 2
        w = [(rX1s[sc % 2], x1st_tok[sc])]
        if sc < 2:
            w.append((sT2, IC))
        if sc >= 2:
            w.append((sYR, 2 * (sc - 1)))
        SY(w, (lambda sc_=sc, p_=par: (lambda h: h.dma_start(
            out=x1rr_s[p_][:, :], in_=x1_d[sc_ * 128:(sc_ + 1) * 128, :])))(),
           [(rX[par], 16)])
        x1rr_tok[sc] = rX[par].n

    for sc in range(2):
        x1rr_load(sc)
    for sc in range(T):
        par = sc % 2
        pb = 4 + 2 * (sc % 2)
        for ic in range(IC):
            for ne in range(2):
                w = [(rW[0], nWdnL), (sNfo, ic + 1)]
                if ic == 0 and ne == 0 and sc >= 2:
                    w.append((sYE, sc - 1))
                T_(w, (lambda sc_=sc, ic_=ic, ne_=ne, pb_=pb: (lambda t: t.matmul(
                    bank(pb_ + ne_), gate_sl(ic_)[:, sc_ * 128:(sc_ + 1) * 128],
                    wdn_sl(ic_)[:, ne_ * 512:(ne_ + 1) * 512],
                    start=(ic_ == 0), stop=(ic_ == IC - 1))))(),
                   [(sMMd, 1)] if (ic == IC - 1 and ne == 1) else [])
        wYE = [(sMMd, sc + 1), (sBar, nBar4)]
        if sc >= 2:
            wYE.append((rYSt[par], yst_tok[sc - 2]))
        S_(wYE,
           (lambda p_=par, pb_=pb: (lambda s: s.activation(
               yv_s[p_][:, :], ps_all[:, pb_ * 512:pb_ * 512 + D], Act.Identity,
               bias=0.0, scale=bsc[:, 6:7])))(),
           [(sYE, 1)])
        w = [(sYE, sc + 1), (sModP, nModP)]
        V_(w, (lambda p_=par: (lambda v: v.tensor_tensor(yv_s[p_][:, :], yv_s[p_][:, :], gbb[:, :], Alu.mult)))(),
           [(sYR, 1)])
        V_([(sYR, 2 * sc + 1), (rX[par], x1rr_tok[sc])],
           (lambda p_=par: (lambda v: v.tensor_tensor(yv_s[p_][:, :], yv_s[p_][:, :], x1rr_s[p_][:, :], Alu.add)))(),
           [(sYR, 1)])
        SY([(sYR, 2 * (sc + 1))],
           (lambda sc_=sc, p_=par: (lambda h: h.dma_start(
               out=y_d[sc_ * 128:(sc_ + 1) * 128, :], in_=yv_s[p_][:, :])))(),
           [(rYSt[par], 16)])
        yst_tok[sc] = rYSt[par].n
        if sc + 2 < T:
            x1rr_load(sc + 2)
    rec.wait("sync", [(rYSt[0], rYSt[0].n), (rYSt[1], rYSt[1].n)])
    rec.wait("gpsimd", [(rYSt[0], rYSt[0].n), (rYSt[1], rYSt[1].n)])

    with nc.Block() as block:
        block.sync(lambda h: [f(h) for f in rec.ops["sync"]])
        block.tensor(lambda h: [f(h) for f in rec.ops["tensor"]])
        block.vector(lambda h: [f(h) for f in rec.ops["vector"]])
        block.scalar(lambda h: [f(h) for f in rec.ops["scalar"]])
        block.gpsimd(lambda h: [f(h) for f in rec.ops["gpsimd"]])
    st.close()
    return nc


def make_in_maps(inputs, S, ncores):
    x = np.ascontiguousarray(np.asarray(inputs["L0_attn_x"], dtype=np.float32))
    wqT = np.ascontiguousarray(np.asarray(inputs["L0_attn_Wq"]).T)
    wkT = np.ascontiguousarray(np.asarray(inputs["L0_attn_Wk"]).T)
    wvT = np.ascontiguousarray(np.asarray(inputs["L0_attn_Wv"]).T)
    woT = np.ascontiguousarray(np.asarray(inputs["L0_attn_Wo"]).T)
    wup_o = np.ascontiguousarray(np.asarray(inputs["L0_ffn_w_up"], dtype=np.float32))
    wg_o = np.ascontiguousarray(np.asarray(inputs["L0_ffn_w_gate"], dtype=np.float32))
    wdn_o = np.ascontiguousarray(np.asarray(inputs["L0_ffn_w_down"], dtype=np.float32))
    wmod = np.ascontiguousarray(np.asarray(inputs["L0_wmod"]))
    c = np.asarray(inputs["L0_c"], dtype=np.float32)
    bq = np.asarray(inputs["L0_attn_bq"], dtype=np.float32)
    bk = np.asarray(inputs["L0_attn_bk"], dtype=np.float32)
    bv = np.asarray(inputs["L0_attn_bv"], dtype=np.float32)
    bo = np.asarray(inputs["L0_attn_bo"], dtype=np.float32)
    bqcol = np.ascontiguousarray(bq.reshape(DC, 128).T)
    bkcol = np.ascontiguousarray(bk.reshape(DC, 128).T)
    maps = []
    for b in range(ncores):
        maps.append({
            "x": np.ascontiguousarray(x[b, :S]),
            "wqT": wqT, "wkT": wkT, "wvT": wvT, "woT": woT,
            "wup_o": wup_o, "wg_o": wg_o, "wdn_o": wdn_o, "wmod": wmod,
            "ccol": np.ascontiguousarray(c[b].reshape(DC, 128).T),
            "bqcol": bqcol, "bkcol": bkcol,
            "bvrow": np.ascontiguousarray(bv.reshape(1, D)),
            "borow": np.ascontiguousarray(bo.reshape(1, D)),
        })
    return maps


def kernel(**inputs):
    S = int(inputs["L0_attn_x"].shape[1])
    ncores = int(inputs["L0_attn_x"].shape[0])
    nc = build_program(S, ncores)
    in_maps = make_in_maps(inputs, S, ncores)
    res = run_bass_kernel_spmd(nc, in_maps, core_ids=list(range(ncores)))
    out = np.stack([np.asarray(res.results[b]["y"]) for b in range(ncores)], axis=0)
    return out.astype(np.float32)


# version 27 (final)
# speedup vs baseline: 1.0326x; 1.0161x over previous
import numpy as np

import concourse.bass as bass
import concourse.mybir as mybir
from concourse.bass_utils import run_bass_kernel_spmd

F32 = mybir.dt.float32
BF16 = mybir.dt.bfloat16
I32 = mybir.dt.int32
AX = mybir.AxisListType.X
Alu = mybir.AluOpType
Act = mybir.ActivationFunctionType

D = 1024
DC = D // 128
FF = 3072
IC = FF // 128
QMAX = 32767.0
EPS_LN = 1e-5
EPS_Q = 1e-12

ENGINES = ("sync", "tensor", "vector", "scalar", "gpsimd")


class Sem:

    def __init__(self, h):
        self.h = h
        self.n = 0

    def inc(self, k=1):
        self.n += k
        return self.n


class EngSem(Sem):
    pass


class VSem:

    def __init__(self, name):
        self.name = name
        self.n = 0
        self.map = {}
        self._cur = {}

    def inc_record(self, eng_sem, shared_val, k=1):
        self.n += k
        self._cur = dict(self._cur)
        self._cur[id(eng_sem)] = (eng_sem, shared_val)
        self.map[self.n] = self._cur

    def translate(self, v):
        assert v in self.map, f"VSem {self.name}: wait on unrecorded count {v} (max {self.n})"
        return list(self.map[v].values())


class Rec:
    def __init__(self, eng_sems):
        self.ops = {e: [] for e in ENGINES}
        self.eng_sems = eng_sems

    @staticmethod
    def _resolve(waits):
        tw = {}
        for (s, v) in waits:
            if v <= 0:
                continue
            if isinstance(s, VSem):
                pairs = s.translate(v)
            else:
                pairs = [(s, v)]
            for sh_sem, sv in pairs:
                tw[id(sh_sem)] = (sh_sem.h, max(tw.get(id(sh_sem), (None, 0))[1], sv))
        return list(tw.values())

    def op(self, eng, waits, emit, incs):
        waits = [(s, v) for (s, v) in waits]
        real_incs = []
        v_incs = [s for (s, k) in incs if isinstance(s, VSem)]
        for (s, k) in incs:
            if not isinstance(s, VSem):
                real_incs.append((s.h, k))
                s.inc(k)
        es = self.eng_sems[eng]
        if v_incs:
            es.inc(1)
            for s in v_incs:
                s.inc_record(es, es.n)
            real_incs.append((es.h, 1))

        def fn(h):
            for sh, v in Rec._resolve(waits):
                h.wait_ge(sh, v)
            ins = emit(h)
            for sh, k in real_incs:
                ins = ins.then_inc(sh, k)
        self.ops[eng].append(fn)

    def wait(self, eng, waits):
        waits = [(s, v) for (s, v) in waits]

        def fn(h):
            for sh, v in Rec._resolve(waits):
                h.wait_ge(sh, v)
        self.ops[eng].append(fn)


class SlotAlloc:
    def __init__(self, tensor, n_slots, slot_elems, name):
        self.t = tensor
        self.free = list(range(n_slots))
        self.se = slot_elems
        self.name = name

    def alloc(self, k=1):
        assert len(self.free) >= k, f"{self.name}: need {k} slots, have {len(self.free)}"
        ids = [self.free.pop(0) for _ in range(k)]
        return [self.t[:, s * self.se:(s + 1) * self.se] for s in ids], ids

    def alloc2(self):
        for j in range(len(self.free) - 1):
            if self.free[j + 1] == self.free[j] + 1:
                a = self.free.pop(j + 1)
                b = self.free.pop(j)
                return self.t[:, b * self.se:(a + 1) * self.se], [b, a]
        raise AssertionError(f"{self.name}: no contiguous pair")

    def release(self, ids):
        self.free = sorted(self.free + list(ids))


def build_program(S, ncores, upto=99):
    T = S // 128
    NS = S // 512 if S >= 512 else 1
    SB = min(512, S)
    NBLK = max(S // SB, 1)
    SPB = SB // 128
    assert S % 512 == 0 and S <= 2048

    nc = bass.Bass("TRN2", target_bir_lowering=False, debug=False)

    x_d = nc.dram_tensor("x", [S, D], F32, kind="ExternalInput").ap()
    wqT_d = nc.dram_tensor("wqT", [D, D], F32, kind="ExternalInput").ap()
    wkT_d = nc.dram_tensor("wkT", [D, D], F32, kind="ExternalInput").ap()
    wvT_d = nc.dram_tensor("wvT", [D, D], F32, kind="ExternalInput").ap()
    woT_d = nc.dram_tensor("woT", [D, D], F32, kind="ExternalInput").ap()
    wmod_d = nc.dram_tensor("wmod", [D, 6 * D], F32, kind="ExternalInput").ap()
    ccol_d = nc.dram_tensor("ccol", [128, DC], F32, kind="ExternalInput").ap()
    bqcol_d = nc.dram_tensor("bqcol", [128, DC], F32, kind="ExternalInput").ap()
    bkcol_d = nc.dram_tensor("bkcol", [128, DC], F32, kind="ExternalInput").ap()
    bvrow_d = nc.dram_tensor("bvrow", [1, D], F32, kind="ExternalInput").ap()
    borow_d = nc.dram_tensor("borow", [1, D], F32, kind="ExternalInput").ap()

    y_d = nc.dram_tensor("y", [S, D], F32, kind="ExternalOutput").ap()

    wup_o_d = nc.dram_tensor("wup_o", [FF, D], F32, kind="ExternalInput").ap()
    wg_o_d = nc.dram_tensor("wg_o", [FF, D], F32, kind="ExternalInput").ap()
    wdn_o_d = nc.dram_tensor("wdn_o", [D, FF], F32, kind="ExternalInput").ap()

    x1_d = nc.dram_tensor("x1buf", [S, D], F32).ap()
    qTs_d = nc.dram_tensor("qTspill", [D, S], BF16).ap()
    upraw_d = nc.dram_tensor("upraw", [IC, 128, S], BF16).ap()
    wup_bd = nc.dram_tensor("wup_b", [D, FF], BF16).ap()
    wg_bd = nc.dram_tensor("wg_b", [D, FF], BF16).ap()
    wdn_bd = nc.dram_tensor("wdn_b", [FF, D], BF16).ap()
    cc_in = [nc.dram_tensor(f"cc_in{i}", [1, 2], F32).ap() for i in range(4)]
    cc_out = [nc.dram_tensor(f"cc_out{i}", [1, 2], F32).ap() for i in range(4)]

    from contextlib import ExitStack
    st = ExitStack()
    sbt = lambda n_, sh, dt: st.enter_context(nc.sbuf_tensor(n_, sh, dt))
    eng_sems = {e: EngSem(st.enter_context(nc.semaphore("p_" + e))) for e in ENGINES}
    rec = Rec(eng_sems)
    sems = {}
    REAL = set()

    def sem(name):
        if name not in sems:
            if name in REAL:
                sems[name] = Sem(st.enter_context(nc.semaphore(name)))
            else:
                sems[name] = VSem(name)
        return sems[name]

    def rsem(name):
        if name not in sems:
            REAL.add(name)
        return sem(name)

    BSLOT = 2048
    NBS = 36
    FSLOT = 1024
    NFS = 10
    arena_b = sbt("arena_b", [128, NBS * BSLOT], BF16)
    arena_f = sbt("arena_f", [128, NFS * FSLOT], F32)
    AB = SlotAlloc(arena_b, NBS, BSLOT, "bf16")
    AF = SlotAlloc(arena_f, NFS, FSLOT, "f32")
    (trashb_s, trashb_ids) = AB.alloc(1)
    trashb = trashb_s[0][:, 0:D]

    ident = sbt("ident", [128, 128], BF16)
    onesf = sbt("onesf", [1, 128], F32)
    ccol = sbt("ccol_sb", [128, DC], F32)
    silu_b = sbt("silu_b", [128, DC], BF16)
    bqcol = sbt("bqcol_sb", [128, DC], F32)
    bkcol = sbt("bkcol_sb", [128, DC], F32)
    ca1col = sbt("ca1col", [128, DC], F32)
    sacol = sbt("sacol", [128, DC], F32)
    bvb = sbt("bvb", [128, D], BF16)
    gab = sbt("gab", [128, D], BF16)
    gabob = sbt("gabob", [128, D], BF16)
    cb1b = sbt("cb1b", [128, D], BF16)
    sbb = sbt("sbb", [128, D], BF16)
    gbb = sbt("gbb", [128, D], BF16)
    bob = sbt("bob", [128, D], BF16)
    sumc = sbt("sumc", [128, T], F32)
    sqc = sbt("sqc", [128, T], F32)
    stat1 = sbt("stat1", [128, T], F32)
    stat2 = sbt("stat2", [128, T], F32)
    rstdc = sbt("rstdc", [128, T], F32)
    nbc = sbt("nbc", [128, T], F32)
    bcolc = sbt("bcolc", [128, T], F32)
    rsc = sbt("rsc", [128, T], F32)
    rrs = sbt("rrs", [128, T], F32)
    scal = sbt("scal", [1, 64], F32)
    bsc = sbt("bsc", [128, 16], F32)
    redrow = sbt("redrow", [1, 128], F32)
    zeroc = sbt("zeroc", [128, 1], F32)
    scr1b = sbt("scr1b", [128, D], BF16)
    trash = scr1b
    qscr = sbt("qscr", [128, 1024], I32)

    (modrow_slots, modrow_ids) = AF.alloc(6)
    assert modrow_ids == [0, 1, 2, 3, 4, 5]
    modrow = arena_f[0:1, 0:6 * D]

    ps_all = st.enter_context(nc.psum_tensor("ps", [128, 4096], F32))

    def bank(b, w=512):
        return ps_all[:, b * 512: b * 512 + w]

    def _finish(waits):
        rFin = rsem("rFin")
        rec.op("sync", list(waits),
               lambda h: h.dma_start(out=y_d[0:128, :], in_=arena_f[:, 0:D]),
               [(rFin, 16)])
        rec.wait("sync", [(rFin, 16)])
        with nc.Block() as block:
            block.sync(lambda h: [f(h) for f in rec.ops["sync"]])
            block.tensor(lambda h: [f(h) for f in rec.ops["tensor"]])
            block.vector(lambda h: [f(h) for f in rec.ops["vector"]])
            block.scalar(lambda h: [f(h) for f in rec.ops["scalar"]])
            block.gpsimd(lambda h: [f(h) for f in rec.ops["gpsimd"]])
        st.close()
        return nc

    def T_(waits, emit, incs=()):
        rec.op("tensor", waits, emit, incs)

    def V_(waits, emit, incs=()):
        rec.op("vector", waits, emit, incs)

    def S_(waits, emit, incs=()):
        rec.op("scalar", waits, emit, incs)

    def G_(waits, emit, incs=()):
        rec.op("gpsimd", waits, emit, incs)

    def SY(waits, emit, incs=()):
        rec.op("sync", waits, emit, incs)

    sB7e = sem("sB7e")
    sB7p = sem("sB7p")

    def via_b7(extra_waits, emitT, evac_eng, emitE, evac_incs=()):
        prior = sB7e.n
        T_(list(extra_waits) + [(sB7e, prior), (sem("sIdent"), 2)], emitT, [(sB7p, 1)])
        rec.op(evac_eng, [(sB7p, sB7p.n)], emitE, list(evac_incs) + [(sB7e, 1)])

    sProD = rsem("sProD")
    sPro = sem("sPro")
    sBc0 = rsem("gBc0")
    sBc1 = rsem("gBc1")
    rStUp = rsem("rStUp")
    rStG = rsem("rStG")
    rStDn = rsem("rStDn")
    rA = [rsem("rA0"), rsem("rA1"), rsem("rA2")]
    rW = [rsem("rW0"), rsem("rW1"), rsem("rW2"), rsem("rW3")]
    rQ = [rsem("rQ0"), rsem("rQ1")]
    rX = [rsem("rX0"), rsem("rX1")]
    rX1s = [rsem("rX1s0"), rsem("rX1s1")]
    rUpSt = [rsem("rUpSt0"), rsem("rUpSt1")]
    rYSt = [rsem("rYSt0"), rsem("rYSt1")]
    rQSp = [rsem("rQSp0"), rsem("rQSp1")]
    gCCD = rsem("gCCD")
    gCC = rsem("gCC")

    G_([], lambda g: g.iota(qscr[:, 0:128], pattern=[[1, 128]], base=0,
                            channel_multiplier=-1), [(sem("sIota"), 1)])
    V_([(sem("sIota"), 1)],
       lambda v: v.tensor_single_scalar(ident[:, :], qscr[:, 0:128], 0.0, Alu.is_equal),
       [(sem("sIdent"), 1)])
    V_([], lambda v: v.memset(onesf[:, :], 1.0), [(sem("sIdent"), 1)])
    V_([], lambda v: v.memset(zeroc[:, :], 0.0), [(sem("sConst"), 1)])
    V_([], lambda v: v.memset(bsc[:, 7:8], 32768.0), [(sem("sConst"), 1)])

    for dst, src in ((ccol, ccol_d), (bqcol, bqcol_d), (bkcol, bkcol_d)):
        SY([], (lambda d=dst, s=src: (lambda h: h.dma_start(out=d[:, :], in_=s[:, :])))(),
           [(sProD, 16)])
    G_([], lambda g: g.dma_start(out=bvb[:, :], in_=bvrow_d[:, :].partition_broadcast(128)), [(sBc0, 16)])
    G_([], lambda g: g.dma_start(out=bob[:, :], in_=borow_d[:, :].partition_broadcast(128)), [(sBc1, 16)])

    S_([(sProD, 48)],
       lambda s: s.activation(bsc[:, 8:8 + DC], ccol[:, :], Act.Tanh, bias=0.0, scale=0.5),
       [(sPro, 1)])
    V_([(sPro, 1)],
       lambda v: v.tensor_scalar(bsc[:, 8:8 + DC], bsc[:, 8:8 + DC], 0.5, 0.5, Alu.mult, Alu.add),
       [(sPro, 1)])
    V_([(sPro, 2)],
       lambda v: v.tensor_tensor(silu_b[:, :], ccol[:, :], bsc[:, 8:8 + DC], Alu.mult),
       [(sPro, 1)])

    (xf_s, xf_ids) = AF.alloc(3)
    (hT_s, hT_ids) = AB.alloc(DC)
    hT = [hT_s[dc][:, 0:S] for dc in range(DC)]
    (xnv, xn_ids) = AB.alloc(1)
    xn_buf = [xnv[0][:, 0:D], xnv[0][:, D:2 * D]]

    sSq = sem("sSq")
    sSum = sem("sSum")
    sStat = sem("sStat")
    xa_tok = []
    for i in range(T):
        b = i % 3
        w = []
        if i >= 3:
            w.append((sSq, i - 2))
            w.append((sSum, i - 2))
        SY(w, (lambda i_=i, b_=b: (lambda h: h.dma_start(
            out=xf_s[b_][:, :], in_=x_d[i_ * 128:(i_ + 1) * 128, :])))(),
           [(rA[b], 16)])
        xa_tok.append(rA[b].n)
    for i in range(T):
        b = i % 3
        nld = xa_tok[i]
        S_([(rA[b], nld), (sSq, i)],
           (lambda i_=i, b_=b: (lambda s: s.activation(
               trashb[:, :], xf_s[b_][:, :], Act.Square, bias=0.0, scale=1.0,
               accum_out=sqc[:, i_:i_ + 1])))(),
           [(sSq, 1)])
        V_([(rA[b], nld)],
           (lambda i_=i, b_=b: (lambda v: v.reduce_sum(sumc[:, i_:i_ + 1], xf_s[b_][:, :], AX)))(),
           [(sSum, 1)])

    def ln_stats(sS, sQ, sOut):
        V_([(sS, T)], lambda v: v.tensor_scalar(stat1[:, :], sumc[:, :], 1.0 / D, None, Alu.mult),
           [(sOut, 1)])
        V_([(sQ, T), (sOut, sOut.n)],
           lambda v: v.tensor_scalar(stat2[:, :], sqc[:, :], 1.0 / D, None, Alu.mult),
           [(sOut, 1)])
        V_([(sOut, sOut.n)],
           lambda v: v.scalar_tensor_tensor(stat2[:, :], stat1[:, :], 1.0, stat2[:, :], Alu.mult, Alu.subtract),
           [(sOut, 1)])
        V_([(sOut, sOut.n)],
           lambda v: v.tensor_scalar(stat2[:, :], stat2[:, :], -1.0, EPS_LN, Alu.mult, Alu.add),
           [(sOut, 1)])
        S_([(sOut, sOut.n)],
           lambda s: s.activation(rstdc[:, :], stat2[:, :], Act.Sqrt, bias=0.0, scale=1.0),
           [(sOut, 1)])
        V_([(sOut, sOut.n)],
           lambda v: v.reciprocal(rstdc[:, :], rstdc[:, :]),
           [(sOut, 1)])
        V_([(sOut, sOut.n)],
           lambda v: v.scalar_tensor_tensor(nbc[:, :], stat1[:, :], -1.0, rstdc[:, :], Alu.mult, Alu.mult),
           [(sOut, 1)])

    ln_stats(sSum, sSq, sStat)
    nStat = sStat.n

    (wmod_v, wmod_ids) = AB.alloc(4)
    wmod_buf = [arena_b[:, wmod_ids[0] * BSLOT:(wmod_ids[0] + 2) * BSLOT],
                arena_b[:, wmod_ids[2] * BSLOT:(wmod_ids[2] + 2) * BSLOT]]
    assert wmod_ids[1] == wmod_ids[0] + 1 and wmod_ids[3] == wmod_ids[2] + 1
    sModC = sem("sModC")
    sModE = sem("sModE")
    n_slices = 6 * D // 512
    modL_tok = []
    for sl in range(n_slices):
        b = sl % 2
        w = []
        if sl >= 2:
            w.append((sModC, sl - 1))
        src = wmod_d[:, sl * 512:(sl + 1) * 512].rearrange("(c p) n -> p c n", p=128)
        dstv = wmod_buf[b].rearrange("p (c n) -> p c n", n=512)
        G_(w, (lambda s_=src, d_=dstv: (lambda g: g.dma_start(out=d_, in_=s_)))(),
           [(rQ[b], 16)])
        modL_tok.append(rQ[b].n)
    for sl in range(n_slices):
        b = sl % 2
        for kc in range(DC):
            w = [(rQ[b], modL_tok[sl])]
            if kc == 0:
                w.append((sPro, 3))
                if sl >= 1:
                    w.append((sModE, sl))
            T_(w, (lambda b_=b, kc_=kc: (lambda t: t.matmul(
                bank(6)[0:1, :], silu_b[:, kc_:kc_ + 1],
                wmod_buf[b_][:, kc_ * 512:(kc_ + 1) * 512],
                start=(kc_ == 0), stop=(kc_ == DC - 1))))(),
               [(sModC, 1)] if kc == DC - 1 else [])
        S_([(sModC, sl + 1)],
           (lambda sl_=sl: (lambda s: s.copy(modrow[:, sl_ * 512:(sl_ + 1) * 512],
                                             bank(6)[0:1, :])))(),
           [(sModE, 1)])
    AB.release(wmod_ids)

    sModB = sem("sModB")

    def bcast_vec(dst, off):
        for half in range(2):
            via_b7(
                [(sModE, n_slices)],
                (lambda h_=half, o_=off: (lambda t: t.matmul(
                    bank(7), onesf[:, :], modrow[0:1, o_ + h_ * 512: o_ + (h_ + 1) * 512],
                    start=True, stop=True)))(),
                "scalar",
                (lambda h_=half, d_=dst: (lambda s: s.copy(d_[:, h_ * 512:(h_ + 1) * 512], bank(7))))(),
                [(sModB, 1)])

    bcast_vec(gab, 2 * D)
    bcast_vec(cb1b, 4 * D)
    bcast_vec(sbb, 3 * D)
    bcast_vec(gbb, 5 * D)
    nModB = sModB.n

    sModCC = sem("sModCC")
    for dstc, off, add1 in ((ca1col, D, True), (sacol, 0, False)):
        for kc in range(DC):
            via_b7(
                [(sModE, n_slices)],
                (lambda o_=off, kc_=kc: (lambda t: t.matmul(
                    bank(7, 128)[:, 0:1], modrow[0:1, o_ + kc_ * 128: o_ + (kc_ + 1) * 128],
                    onesf[0:1, 0:1], start=True, stop=True)))(),
                "scalar",
                (lambda d_=dstc, kc_=kc, a_=add1: (lambda s: s.activation(
                    d_[:, kc_:kc_ + 1], bank(7, 128)[:, 0:1], Act.Identity,
                    bias=(1.0 if a_ else 0.0), scale=1.0)))(),
                [(sModCC, 1)])
    nModCC = sModCC.n

    sModP = sem("sModP")

    def gate_fix(dst):
        base = sModP.n
        S_([(sModB, nModB), (sModP, base)],
           lambda s, d=dst: s.activation(scr1b[:, :], d[:, :], Act.Abs, bias=0.0, scale=1.0),
           [(sModP, 1)])
        V_([(sModP, base + 1)],
           lambda v: v.tensor_scalar(scr1b[:, :], scr1b[:, :], 0.25, None, Alu.max),
           [(sModP, 1)])
        V_([(sModP, base + 2)],
           lambda v, d=dst: v.tensor_single_scalar(d[:, :], d[:, :], 0.0, Alu.is_ge),
           [(sModP, 1)])
        V_([(sModP, base + 3)],
           lambda v, d=dst: v.tensor_tensor(d[:, :], d[:, :], scr1b[:, :], Alu.mult),
           [(sModP, 1)])
        V_([(sModP, base + 4)],
           lambda v, d=dst: v.scalar_tensor_tensor(d[:, :], d[:, :], 2.0, scr1b[:, :], Alu.mult, Alu.subtract),
           [(sModP, 1)])

    gate_fix(gab)
    gate_fix(gbb)
    V_([(sModP, sModP.n), (sBc1, 16)],
       lambda v: v.tensor_tensor(gabob[:, :], gab[:, :], bob[:, :], Alu.mult),
       [(sModP, 1)])
    V_([(sModB, nModB)],
       lambda v: v.tensor_scalar(cb1b[:, :], cb1b[:, :], 1.0, None, Alu.add),
       [(sModP, 1)])
    nModP = sModP.n
    if upto <= 0:
        return _finish([(sModP, nModP), (sModCC, nModCC)])

    sXn = sem("sXn")
    sTr1 = sem("sTr1")
    sHT = sem("sHT")
    xb_tok = []
    for i in range(T):
        b = i % 3
        w = [(sSq, T), (sSum, T)]
        if i >= 3:
            w.append((sXn, i - 2))
        SY(w, (lambda i_=i, b_=b: (lambda h: h.dma_start(
            out=xf_s[b_][:, :], in_=x_d[i_ * 128:(i_ + 1) * 128, :])))(),
           [(rA[b], 16)])
        xb_tok.append(rA[b].n)
    for i in range(T):
        b = i % 3
        bx = i % 2
        nld = xb_tok[i]
        w = [(rA[b], nld), (sStat, nStat)]
        if i >= 2:
            w.append((sTr1, DC * (i - 1)))
        S_(w, (lambda i_=i, b_=b, bx_=bx: (lambda s: s.activation(
            xn_buf[bx_][:, :], xf_s[b_][:, :], Act.Identity,
            bias=nbc[:, i_:i_ + 1], scale=rstdc[:, i_:i_ + 1])))(),
           [(sXn, 1)])
        for dc in range(DC):
            pb = (DC * i + dc) % 2
            w2 = [(sXn, i + 1), (sem("sIdent"), 2)]
            prev = DC * i + dc - 2
            if prev >= 0:
                w2.append((sHT, prev + 1))
            T_(w2, (lambda dc_=dc, pb_=pb, bx_=bx: (lambda t: t.matmul(
                bank(pb_, 128), xn_buf[bx_][:, dc_ * 128:(dc_ + 1) * 128], ident[:, :],
                start=True, stop=True)))(),
               [(sTr1, 1)])
            S_([(sTr1, DC * i + dc + 1), (sModCC, nModCC)],
               (lambda i_=i, dc_=dc, pb_=pb: (lambda s: s.activation(
                   hT[dc_][:, i_ * 128:(i_ + 1) * 128], bank(pb_, 128), Act.Identity,
                   bias=sacol[:, dc_:dc_ + 1], scale=ca1col[:, dc_:dc_ + 1])))(),
               [(sHT, 1)])
    nHT = sHT.n
    if upto <= 1:
        return _finish([(sHT, nHT)])
    AF.release(xf_ids)

    (wA_s, wA_ids) = AB.alloc(4)
    (wB_s, wB_ids) = AB.alloc(4)
    AB.release(xn_ids)

    def wsl(bufs, kc, lo, hi):
        return bufs[kc // 2][:, (kc % 2) * 1024 + lo:(kc % 2) * 1024 + hi]

    wring = {"wq": rW[0], "wk": rW[1], "wv": rW[2], "wo": rW[3]}
    wtok = {}

    def load_weight(bufs, wname, src_fd, extra_wait=()):
        r = wring[wname]
        for kc in range(DC):
            G_(list(extra_wait),
               (lambda s_=src_fd[kc * 128:(kc + 1) * 128, :], d_=wsl(bufs, kc, 0, 1024):
                (lambda g: g.dma_start(out=d_, in_=s_)))(),
               [(r, 16)])
        wtok[wname] = r.n

    load_weight(wA_s, "wq", wqT_d)
    load_weight(wB_s, "wk", wkT_d)

    sMMq = sem("sMMq")
    sQT = sem("sQT")

    def proj_fm(bufs, wname, bias_col, dst_of_ec, spill):
        sp_tok = {}
        for ns in range(NS):
            for ec in range(DC):
                gidx = ns * DC + ec
                pb = 4 + gidx % 2
                for kc in range(DC):
                    w = [(wring[wname], wtok[wname])]
                    if kc == 0:
                        w.append((sHT, min(DC * (4 * ns + 4), nHT)))
                        if sQT.n >= 1:
                            w.append((sQT, sQT.n - 1))
                    T_(w, (lambda ec_=ec, ns_=ns, kc_=kc, pb_=pb: (lambda t: t.matmul(
                        bank(pb_), wsl(bufs, kc_, ec_ * 128, ec_ * 128 + 128),
                        hT[kc_][:, ns_ * 512:(ns_ + 1) * 512],
                        start=(kc_ == 0), stop=(kc_ == DC - 1))))(),
                       [(sMMq, 1)] if kc == DC - 1 else [])
                wE = [(sMMq, sMMq.n)]
                if spill and ec >= 2:
                    wE.append((rQSp[ec % 2], sp_tok[(ec - 2, ns)]))
                S_(wE,
                   (lambda ec_=ec, ns_=ns, pb_=pb: (lambda s: s.activation(
                       dst_of_ec(ec_)[:, ns_ * 512:(ns_ + 1) * 512], bank(pb_), Act.Identity,
                       bias=bias_col[:, ec_:ec_ + 1], scale=1.0)))(),
                   [(sQT, 1)])
                if spill:
                    SY([(sQT, sQT.n)],
                       (lambda ec_=ec, ns_=ns: (lambda h: h.dma_start(
                           out=qTs_d[ec_ * 128:(ec_ + 1) * 128, ns_ * 512:(ns_ + 1) * 512],
                           in_=dst_of_ec(ec_)[:, ns_ * 512:(ns_ + 1) * 512])))(),
                       [(rQSp[ec % 2], 16)])
                    sp_tok[(ec, ns)] = rQSp[ec % 2].n

    (qst2, qst2_ids) = AB.alloc(2)
    proj_fm(wA_s, "wq", bqcol, lambda ec: qst2[ec % 2][:, 0:S], True)
    nQSp0 = rQSp[0].n
    nQSp1 = rQSp[1].n
    (kT_s, kT_ids) = AB.alloc(DC)
    kT = [kT_s[dc][:, 0:S] for dc in range(DC)]
    proj_fm(wB_s, "wk", bkcol, lambda ec: kT[ec], False)
    load_weight(wA_s, "wv", wvT_d, extra_wait=[(rQSp[0], nQSp0), (rQSp[1], nQSp1)])
    nQT_afterK = sQT.n

    (v_s, v_ids) = AB.alloc(DC)

    def v_sl(sc, lo, hi):
        return v_s[sc // 2][:, (sc % 2) * 1024 + lo:(sc % 2) * 1024 + hi]

    sMMv = sem("sMMv")
    sV = sem("sV")
    for sc in range(T):
        for ne in range(2):
            gidx = sc * 2 + ne
            pb = 4 + gidx % 2
            for kc in range(DC):
                w = [(wring["wv"], wtok["wv"])]
                if kc == 0:
                    w.append((sQT, nQT_afterK))
                    if gidx >= 2:
                        w.append((sV, gidx - 1))
                T_(w, (lambda sc_=sc, ne_=ne, kc_=kc, pb_=pb: (lambda t: t.matmul(
                    bank(pb_), hT[kc_][:, sc_ * 128:(sc_ + 1) * 128],
                    wsl(wA_s, kc_, ne_ * 512, (ne_ + 1) * 512),
                    start=(kc_ == 0), stop=(kc_ == DC - 1))))(),
                   [(sMMv, 1)] if kc == DC - 1 else [])
            V_([(sMMv, sMMv.n), (sBc0, 16)],
               (lambda sc_=sc, ne_=ne, pb_=pb: (lambda v: v.tensor_tensor(
                   v_sl(sc_, ne_ * 512, (ne_ + 1) * 512), bank(pb_),
                   bvb[:, ne_ * 512:(ne_ + 1) * 512], Alu.add)))(),
               [(sV, 1)])
    nV = sV.n
    load_weight(wB_s, "wo", woT_d, extra_wait=[(sQT, nQT_afterK)])
    if upto <= 2:
        return _finish([(sV, nV)])

    AB.release(hT_ids)
    AB.release(qst2_ids)
    AB.release(wA_ids)

    AB.release(trashb_ids)
    (lq_ab, lq_ids) = AB.alloc(7)
    lw = ([lq_ab[0].bitcast(F32)[:, 512 * j:512 * (j + 1)] for j in range(2)]
          + [lq_ab[1].bitcast(F32)[:, 512 * j:512 * (j + 1)] for j in range(2)])
    lq32 = [lq_ab[2].bitcast(I32)[:, 512 * j:512 * (j + 1)] for j in range(2)]
    lwq = [lq_ab[3 + i // 4][:, (i % 4) * 512:(i % 4 + 1) * 512] for i in range(12)]
    lwqT = [lq_ab[6][:, 512 * j:512 * (j + 1)] for j in range(2)]
    lam = [sumc[:, 0:16], sqc[:, 0:16]]
    lss = [stat1[:, 0:16], stat2[:, 0:16]]
    lrr = [rstdc[:, 0:16], nbc[:, 0:16]]

    lpanels = []
    for t in range(FF // 128):
        for hh2 in range(2):
            lpanels.append((wup_o_d, wup_bd, t, hh2, rStUp))
    for t in range(FF // 128):
        for hh2 in range(2):
            lpanels.append((wg_o_d, wg_bd, t, hh2, rStG))
    for t in range(D // 128):
        for hh2 in range(2 * IC // DC):
            lpanels.append((wdn_o_d, wdn_bd, t, hh2, rStDn))
    NPAN = len(lpanels)

    sLqV = sem("sLqV")
    sLqT = sem("sLqT")
    sLqE = sem("sLqE")
    sLqDone = sem("sLqDone")
    rLq = [rsem(f"rLq{j}") for j in range(4)]
    LQ = {"r": 0, "ld_tok": [0] * NPAN, "st_tok": [None] * NPAN,
          "vt_Q": [0] * NPAN, "vt_W": [0] * NPAN}

    def lq_load(j):
        if j >= NPAN:
            return
        src_d, dst_d, t, hh2, rSt = lpanels[j]
        w = []
        if j < 4:
            w += [(sMMv, 2 * T), (rQSp[0], nQSp0), (rQSp[1], nQSp1), (sSq, T)]
        else:
            w.append((sLqV, LQ["vt_Q"][j - 4]))
        G_(w, (lambda s_=src_d[t * 128:t * 128 + 128, hh2 * 512:hh2 * 512 + 512],
               d_=lw[j % 4]: (lambda g: g.dma_start(out=d_, in_=s_)))(),
           [(rLq[j % 4], 16)])
        LQ["ld_tok"][j] = rLq[j % 4].n

    def lq_tail(r):
        if r < 0 or r >= NPAN:
            return
        par = r % 2
        src_d, dst_d, t, hh2, rSt = lpanels[r]
        pb = 4 + r % 2
        for c in range(4):
            w2 = [(sem("sIdent"), 2)]
            if c == 0:
                w2 += [(sLqV, LQ["vt_W"][r]),
                       (sem("sPTe"), sem("sPTe").n), (sem("sA1"), sem("sA1").n),
                       (sQT, sQT.n), (sV, sV.n)]
                if r >= 2:
                    w2.append((sLqE, r - 1))
            T_(w2, (lambda m_=r % 12, c_=c, pb_=pb: (lambda tt: tt.matmul(
                bank(pb_)[:, c_ * 128:(c_ + 1) * 128],
                lwq[m_][:, c_ * 128:(c_ + 1) * 128], ident[:, :],
                start=True, stop=True)))(),
               [(sLqT, 1)])
        wE_ = [(sLqT, 4 * (r + 1))]
        if r >= 2 and LQ["st_tok"][r - 2] is not None:
            wE_.append(LQ["st_tok"][r - 2])
        S_(wE_, (lambda p_=par, pb_=pb: (lambda s: s.copy(lwqT[p_], bank(pb_))))(),
           [(sLqE, 1)])
        G_([(sLqE, r + 1)],
           (lambda p_=par, d_=dst_d, t_=t, h_=hh2: (lambda g: g.dma_start(
               out=d_[h_ * 512:h_ * 512 + 512, t_ * 128:t_ * 128 + 128]
               .rearrange("(c p) n -> p c n", p=128),
               in_=lwqT[p_].rearrange("p (c n) -> p c n", n=128))))(),
           [(rSt, 16)])
        LQ["st_tok"][r] = (rSt, rSt.n)

    def lq_round():
        r = LQ["r"]
        if r >= NPAN:
            return
        LQ["r"] = r + 1
        par = r % 2
        th2 = LQ.get("th2", NPAN + 1)
        lamX = lam if r < th2 else LQ["lam2"]
        lssX = lss if r < th2 else LQ["lss2"]
        lrrX = lrr if r < th2 else LQ["lrr2"]
        wA_ = [(sLqV, sLqV.n), (rLq[r % 4], LQ["ld_tok"][r])]
        if r < 2:
            wA_ += [(sStat, nStat), (sXn, T)]
        V_(wA_, (lambda p_=par, q_=r % 4, la_=lamX: (lambda v: v.tensor_reduce(
            la_[p_], lw[q_].rearrange("p (b k) -> p b k", k=32),
            AX, Alu.max, apply_absolute_value=True)))(),
           [(sLqV, 1)])
        V_([(sLqV, sLqV.n)],
           (lambda p_=par, la_=lamX, ls_=lssX: (lambda v: v.tensor_scalar(
               ls_[p_], la_[p_], 1.0 / 7.0, EPS_Q, Alu.mult, Alu.max)))(),
           [(sLqV, 1)])
        V_([(sLqV, sLqV.n)],
           (lambda p_=par, ls_=lssX, lr_=lrrX: (lambda v: v.reciprocal(lr_[p_], ls_[p_])))(),
           [(sLqV, 1)])
        V_([(sLqV, sLqV.n)],
           (lambda p_=par, q_=r % 4, lr_=lrrX: (lambda v: v.tensor_tensor(
               lq32[p_].rearrange("p (b k) -> p b k", k=32),
               lw[q_].rearrange("p (b k) -> p b k", k=32),
               lr_[p_].unsqueeze(2).broadcast_to([128, 16, 32]), Alu.mult)))(),
           [(sLqV, 1)])
        LQ["vt_Q"][r] = sLqV.n
        wW_ = [(sLqV, sLqV.n)]
        if r >= 12:
            wW_.append((sLqT, 4 * (r - 11)))
        V_(wW_, (lambda p_=par, m_=r % 12, ls_=lssX: (lambda v: v.tensor_tensor(
            lwq[m_].rearrange("p (b k) -> p b k", k=32),
            lq32[p_].rearrange("p (b k) -> p b k", k=32),
            ls_[p_].unsqueeze(2).broadcast_to([128, 16, 32]), Alu.mult)))(),
           [(sLqV, 1)])
        LQ["vt_W"][r] = sLqV.n
        lq_tail(r - 8)
        lq_load(r + 3)

    for j in range(3):
        lq_load(j)
    for _ in range(8):
        lq_round()

    n_pt_slots = max(1, (T * SB) // 2048)
    (PT_s, PT_ids) = AB.alloc(n_pt_slots)
    PSB = 2048 // SB
    PT = [PT_s[tc // PSB][:, (tc % PSB) * SB:(tc % PSB + 1) * SB] for tc in range(T)]
    (Pst_s, Pst_ids) = AB.alloc(2)
    (oT_s, oT_ids) = AB.alloc(2 if SB == 512 else 2)

    def oT_sl(dc, lo, hi):
        return oT_s[dc // 4][:, (dc % 4) * SB + lo:(dc % 4) * SB + hi]

    (qstv, qst_ids) = AB.alloc(1)

    def qstrip(par):
        return qstv[0][:, par * 1024:(par + 1) * 1024]

    AF.release(modrow_ids)
    scA, scA_ids = AF.alloc2()
    scB, scB_ids = AF.alloc2()
    scC, scC_ids = AF.alloc2()

    sSc = sem("sSc")
    sScE = sem("sScE")
    sTh = sem("sTh")
    sP = sem("sP")
    sPT = sem("sPT")
    sPTe = sem("sPTe")
    sPV = sem("sPV")
    sOT = sem("sOT")
    sPR = sem("sPR")
    sA1 = sem("sA1")
    sRes = sem("sRes")
    sRRS = sem("sRRS")
    (a1_s, a1_ids) = AF.alloc(2)
    (xr_s, xr_ids) = AF.alloc(2)
    ql_tok = [0] * T
    xr_tok = [0] * T
    x1st_tok = [0] * T

    def emit_qload(si):
        par = si % 2
        w = [(rQSp[0], nQSp0), (rQSp[1], nQSp1), (sV, nV)]
        if si >= 2:
            w.append((sSc, si - 1))
        SY(w, (lambda si_=si, p_=par: (lambda h: h.dma_start(
            out=qstrip(p_).rearrange("p (c n) -> p c n", n=128),
            in_=qTs_d[:, si_ * 128:(si_ + 1) * 128].rearrange("(c p) n -> p c n", p=128))))(),
           [(rQ[par], 16)])
        ql_tok[si] = rQ[par].n

    def emit_scores(si):
        par = si % 2
        for nb4 in range(NS):
            for kc in range(DC):
                w = [(rQ[par], ql_tok[si])]
                if kc == 0:
                    w.append((sV, nV))
                    if si >= 1:
                        w.append((sScE, 3 * si))
                T_(w, (lambda nb_=nb4, kc_=kc, p_=par: (lambda t: t.matmul(
                    bank(nb_), qstrip(p_)[:, kc_ * 128:(kc_ + 1) * 128],
                    kT[kc_][:, nb_ * 512:(nb_ + 1) * 512],
                    start=(kc_ == 0), stop=(kc_ == DC - 1))))(),
                   [(sSc, 1)] if (kc == DC - 1 and nb4 == NS - 1) else [])

    def emit_theta(si):
        par = si % 2
        psc = ps_all[:, 0:S]
        wA_ = [(sSc, si + 1)]
        wB_ = [(sSc, si + 1)]
        wC_ = [(sSc, si + 1)]
        if si >= 1:
            wA_.append((sTh, 4 * (si - 1) + 1))
            wB_.append((sTh, 4 * (si - 1) + 2))
            wC_.append((sP, si))
        S_(wA_, lambda s: s.activation(scA[:, 0:S], psc, Act.Tanh, bias=0.0, scale=0.5),
           [(sScE, 1)])
        S_(wB_, lambda s: s.activation(scB[:, 0:S], psc, Act.Tanh, bias=0.0, scale=1.0),
           [(sScE, 1)])
        V_(wC_ + [(sScE, 3 * si + 2)],
           lambda v: v.tensor_scalar(scC[:, 0:S], psc, 0.0, 0.1, Alu.max, Alu.mult),
           [(sScE, 1)])
        V_([(sScE, 3 * si + 3)],
           lambda v: v.scalar_tensor_tensor(scC[:, 0:S], scA[:, 0:S], 0.1, scC[:, 0:S], Alu.mult, Alu.add),
           [(sTh, 1)])
        V_([(sTh, 4 * si + 1)],
           lambda v: v.scalar_tensor_tensor(scC[:, 0:S], scB[:, 0:S], 0.15, scC[:, 0:S], Alu.mult, Alu.add),
           [(sTh, 1)])
        V_([(sTh, 4 * si + 2)],
           (lambda si_=si: (lambda v: v.tensor_reduce(
               bcolc[:, si_:si_ + 1], scC[:, 0:S], AX, Alu.max, negate=True)))(),
           [(sTh, 1)])
        V_([(sTh, 4 * si + 3)],
           (lambda si_=si: (lambda v: v.tensor_scalar(
               bcolc[:, si_:si_ + 1], bcolc[:, si_:si_ + 1], 0.6, None, Alu.add)))(),
           [(sTh, 1)])
        wP = [(sTh, 4 * si + 4)]
        if si >= 2:
            wP.append((sPT, T * (si - 1)))
        S_(wP, (lambda si_=si, p_=par: (lambda s: s.activation(
            Pst_s[p_][:, 0:S], scC[:, 0:S], Act.Exp,
            bias=bcolc[:, si_:si_ + 1], scale=1.0,
            accum_out=rsc[:, si_:si_ + 1])))(),
           [(sP, 1)])

    def emit_PT(si):
        sp = si % SPB
        par = si % 2
        blkv = si // SPB
        for tc in range(T):
            pb = 4 + (tc % 2)
            w2 = [(sP, si + 1), (sem("sIdent"), 2)]
            if tc == 0:
                w2.append((sLqE, sLqE.n))
            if sPTe.n >= 2:
                w2.append((sPTe, sPTe.n - 1))
            if blkv >= 1:
                w2.append((sPV, DC * blkv))
                w2.append((sA1, SPB * blkv))
            T_(w2, (lambda tc_=tc, pb_=pb, p_=par: (lambda t: t.matmul(
                bank(pb_, 128), Pst_s[p_][:, tc_ * 128:(tc_ + 1) * 128], ident[:, :],
                start=True, stop=True)))(),
               [(sPT, 1)])
            V_([(sPT, sPT.n)],
               (lambda tc_=tc, pb_=pb, sp_=sp: (lambda v: v.tensor_copy(
                   PT[tc_][:, sp_ * 128:(sp_ + 1) * 128], bank(pb_, 128))))(),
               [(sPTe, 1)])

    PTN = [0]

    def flush_PT(upto_si):
        while PTN[0] <= upto_si:
            emit_PT(PTN[0])
            PTN[0] += 1

    emit_qload(0)
    for blk in range(NBLK):
        for sp in range(SPB):
            si = blk * SPB + sp
            if si + 1 < T:
                emit_qload(si + 1)
            emit_scores(si)
            flush_PT(si - 1)
            emit_theta(si)
            while LQ["r"] < 8 + ((si + 1) * 80) // T:
                lq_round()
        flush_PT(blk * SPB + SPB - 1)
        nPTe = sPTe.n
        for dc in range(DC if upto >= 35 else 0):
            pb = 6 + dc % 2
            for tc in range(T):
                w = []
                if tc == 0:
                    w.append((sPTe, nPTe))
                    if sOT.n >= 2:
                        w.append((sOT, sOT.n - 1))
                T_(w, (lambda dc_=dc, tc_=tc, pb_=pb: (lambda t: t.matmul(
                    bank(pb_, SB), v_sl(tc_, dc_ * 128, dc_ * 128 + 128),
                    PT[tc_][:, 0:SB], start=(tc_ == 0), stop=(tc_ == T - 1))))(),
                   [(sPV, 1)] if tc == T - 1 else [])
            S_([(sPV, sPV.n)],
               (lambda dc_=dc, pb_=pb: (lambda s: s.copy(oT_sl(dc_, 0, SB), bank(pb_, SB))))(),
               [(sOT, 1)])
        if upto >= 33:
            V_([(sP, (blk + 1) * SPB)],
               (lambda b_=blk: (lambda v: v.reciprocal(rrs[:, b_ * SPB:(b_ + 1) * SPB],
                                                       rsc[:, b_ * SPB:(b_ + 1) * SPB])))(),
               [(sRRS, 1)])
        nOT = sOT.n
        for sc2 in range(SPB if upto >= 40 else 0):
            si = blk * SPB + sc2
            par = si % 2
            w = []
            if si >= 2:
                w.append((sRes, 3 * (si - 1)))
            SY(w, (lambda si_=si, p_=par: (lambda h: h.dma_start(
                out=xr_s[p_][:, :], in_=x_d[si_ * 128:(si_ + 1) * 128, :])))(),
               [(rX[par], 16)])
            xr_tok[si] = rX[par].n
            for ne in range(2):
                pb = 4 + ne
                for dc in range(DC):
                    w = []
                    if dc == 0:
                        w.append((sOT, nOT))
                        w.append((wring["wo"], wtok["wo"]))
                        w.append((sPT, T * SPB * (blk + 1)))
                        w.append((sLqE, sLqE.n))
                        if sA1.n >= 1 and ne == 0:
                            pass
                        if si * 2 + ne >= 2:
                            w.append((sA1, sA1.n - 0 if False else (si * 2 + ne) // 2))
                    T_(w, (lambda sc_=sc2, ne_=ne, dc_=dc, pb_=pb: (lambda t: t.matmul(
                        bank(pb_), oT_sl(dc_, sc_ * 128, sc_ * 128 + 128),
                        wsl(wB_s, dc_, ne_ * 512, (ne_ + 1) * 512),
                        start=(dc_ == 0), stop=(dc_ == DC - 1))))(),
                       [(sPR, 1)] if dc == DC - 1 else [])
            wEv = [(sPR, sPR.n), (sRRS, blk + 1)]
            if si >= 2:
                wEv.append((rX1s[par], x1st_tok[si - 2]))
            S_(wEv,
               (lambda si_=si, p_=par: (lambda s: s.activation(
                   a1_s[p_][:, :], ps_all[:, 4 * 512:4 * 512 + D], Act.Identity,
                   bias=zeroc[:, 0:1], scale=rrs[:, si_:si_ + 1])))(),
               [(sA1, 1)])
            wR1 = [(sA1, sA1.n), (sModP, nModP)]
            V_(wR1,
               (lambda p_=par: (lambda v: v.tensor_tensor(a1_s[p_][:, :], a1_s[p_][:, :], gab[:, :], Alu.mult)))(),
               [(sRes, 1)])
            V_([(sRes, sRes.n)],
               (lambda p_=par: (lambda v: v.tensor_tensor(a1_s[p_][:, :], a1_s[p_][:, :], gabob[:, :], Alu.add)))(),
               [(sRes, 1)])
            V_([(sRes, sRes.n), (rX[par], xr_tok[si])],
               (lambda p_=par: (lambda v: v.tensor_tensor(a1_s[p_][:, :], a1_s[p_][:, :], xr_s[p_][:, :], Alu.add)))(),
               [(sRes, 1)])
            SY([(sRes, sRes.n)],
               (lambda si_=si, p_=par: (lambda h: h.dma_start(
                   out=x1_d[si_ * 128:(si_ + 1) * 128, :], in_=a1_s[p_][:, :])))(),
               [(rX1s[par], 16)])
            x1st_tok[si] = rX1s[par].n

    AB.release(kT_ids); AB.release(v_ids); AB.release(PT_ids); AB.release(Pst_ids)
    AB.release(oT_ids); AB.release(qst_ids); AB.release(wB_ids)
    AF.release(scA_ids); AF.release(scB_ids); AF.release(scC_ids)
    AF.release(a1_ids); AF.release(xr_ids)

    (x1r_s, x1r_ids) = AF.alloc(3)
    (lsc2_s, lsc2_ids) = AF.alloc(1)
    LQ["lam2"] = [lsc2_s[0][:, 16 * j:16 * (j + 1)] for j in range(2)]
    LQ["lss2"] = [lsc2_s[0][:, 32 + 16 * j:48 + 16 * j] for j in range(2)]
    LQ["lrr2"] = [lsc2_s[0][:, 64 + 16 * j:80 + 16 * j] for j in range(2)]
    LQ["th2"] = LQ["r"]
    sSq2 = sem("sSq2")
    sSum2 = sem("sSum2")
    sStat2 = sem("sStat2")
    x1a_tok = []
    for i in range(T):
        b = i % 3
        w = [(rX1s[i % 2], x1st_tok[i])]
        if i >= 3:
            w.append((sSq2, i - 2))
            w.append((sSum2, i - 2))
        SY(w, (lambda i_=i, b_=b: (lambda h: h.dma_start(
            out=x1r_s[b_][:, :], in_=x1_d[i_ * 128:(i_ + 1) * 128, :])))(),
           [(rA[b], 16)])
        x1a_tok.append(rA[b].n)
    for i in range(T):
        b = i % 3
        nld = x1a_tok[i]
        S_([(rA[b], nld), (sSq2, i)],
           (lambda i_=i, b_=b: (lambda s: s.activation(
               trash[:, :], x1r_s[b_][:, :], Act.Square, bias=0.0, scale=1.0,
               accum_out=sqc[:, i_:i_ + 1])))(),
           [(sSq2, 1)])
        V_([(rA[b], nld)],
           (lambda i_=i, b_=b: (lambda v: v.reduce_sum(sumc[:, i_:i_ + 1], x1r_s[b_][:, :], AX)))(),
           [(sSum2, 1)])
        while LQ["r"] < 88 + ((i + 1) * 28) // T:
            lq_round()
    ln_stats(sSum2, sSq2, sStat2)
    nStat2 = sStat2.n

    (h2_s, h2_ids) = AB.alloc(DC)

    def h2_sl(i):
        return h2_s[i // 2][:, (i % 2) * 1024:(i % 2 + 1) * 1024]

    (h2f_s, h2f_ids) = AF.alloc(2)
    sXn2 = sem("sXn2")
    sH2 = sem("sH2")
    sH2V = sem("sH2V")
    sH2M = sem("sH2M")
    x1b_tok = []
    for i in range(T):
        b = i % 3
        w = [(sSq2, T), (sSum2, T)]
        if i >= 3:
            w.append((sXn2, i - 2))
        SY(w, (lambda i_=i, b_=b: (lambda h: h.dma_start(
            out=x1r_s[b_][:, :], in_=x1_d[i_ * 128:(i_ + 1) * 128, :])))(),
           [(rA[b], 16)])
        x1b_tok.append(rA[b].n)
    for i in range(T):
        b = i % 3
        bx = i % 2
        nld = x1b_tok[i]
        w = [(rA[b], nld), (sStat2, nStat2)]
        if i >= 2:
            w.append((sH2, 2 * (i - 1)))
        S_(w, (lambda i_=i, b_=b, bx_=bx: (lambda s: s.activation(
            h2f_s[bx_][:, :], x1r_s[b_][:, :], Act.Identity,
            bias=nbc[:, i_:i_ + 1], scale=rstdc[:, i_:i_ + 1])))(),
           [(sXn2, 1)])
        V_([(sXn2, i + 1), (sModP, nModP)],
           (lambda bx_=bx: (lambda v: v.tensor_tensor(h2f_s[bx_][:, :], h2f_s[bx_][:, :], cb1b[:, :], Alu.mult)))(),
           [(sH2, 1)])
        V_([(sH2, 2 * i + 1)],
           (lambda i_=i, bx_=bx: (lambda v: v.tensor_tensor(h2_sl(i_), h2f_s[bx_][:, :], sbb[:, :], Alu.add)))(),
           [(sH2, 1)])
        V_([(sH2, 2 * i + 2)],
           (lambda i_=i: (lambda v: v.tensor_reduce(
               bcolc[:, i_:i_ + 1], h2_sl(i_), AX, Alu.max,
               apply_absolute_value=True)))(),
           [(sH2V, 1)])
        G_([(sH2V, i + 1)],
           (lambda i_=i: (lambda g: g.tensor_reduce(redrow[0:1, 96 + i_:97 + i_],
                                                    bcolc[:, i_:i_ + 1],
                                                    mybir.AxisListType.XYZWC, Alu.max)))(),
           [(sH2M, 1)])
        while LQ["r"] < 116 + ((i + 1) * 28) // T:
            lq_round()
    while LQ["r"] < NPAN:
        lq_round()
    for k_ in range(8):
        lq_tail(NPAN - 8 + k_)
    G_([(rStUp, rStUp.n), (rStG, rStG.n), (rStDn, rStDn.n)],
       lambda g: g.memset(scal[0:1, 63:64], 0.0), [(sLqDone, 1)])
    AB.release(lq_ids)
    AF.release(lsc2_ids)

    sBar = sem("sBar")
    sCC = gCC
    sCCD = gCCD

    def cross_core_max(srcs, cc_idx, waits):
        for j, (lo, hi, idx) in enumerate(srcs):
            V_(list(waits) + [(sBar, sBar.n)],
               (lambda lo_=lo, hi_=hi, i_=idx: (lambda v: v.tensor_reduce(
                   scal[0:1, i_:i_ + 1], redrow[0:1, lo_:hi_], AX, Alu.max)))(),
               [(sBar, 1)])
        idx0 = srcs[0][2]
        nv = len(srcs)
        G_([(sBar, sBar.n)],
           (lambda ci=cc_idx, i0=idx0, n_=nv: (lambda g: g.dma_start(
               out=cc_in[ci][:, 0:n_], in_=scal[0:1, i0:i0 + n_])))(),
           [(sCCD, 16)])
        G_([(sCCD, sCCD.n)],
           (lambda ci=cc_idx, n_=nv: (lambda g: g.collective_compute(
               "AllReduce", Alu.max, replica_groups=[list(range(ncores))],
               ins=[cc_in[ci][:, 0:n_].opt()], outs=[cc_out[ci][:, 0:n_].opt()])))(),
           [(sCC, 1)])
        G_([(sCC, sCC.n)],
           (lambda ci=cc_idx, i0=idx0, n_=nv: (lambda g: g.dma_start(
               out=scal[0:1, i0:i0 + n_], in_=cc_out[ci][:, 0:n_])))(),
           [(sCCD, 16)])

    def bcast_scalar(src_idx, dst_col, waits):
        via_b7(
            list(waits),
            (lambda i_=src_idx: (lambda t: t.matmul(bank(7, 128)[:, 0:1], onesf[:, :],
                                                    scal[0:1, i_:i_ + 1], start=True, stop=True)))(),
            "scalar",
            (lambda c_=dst_col: (lambda s: s.copy(bsc[:, c_:c_ + 1], bank(7, 128)[:, 0:1])))(),
            [(sBar, 1)])

    cross_core_max([(96, 96 + T, 8)], 0, [(sH2M, T)])
    V_([(sCCD, sCCD.n)],
       lambda v: v.tensor_scalar(scal[0:1, 0:1], scal[0:1, 8:9], 1.0 / QMAX, EPS_Q, Alu.mult, Alu.max),
       [(sBar, 1)])
    V_([(sBar, sBar.n)],
       lambda v: v.reciprocal(scal[0:1, 1:2], scal[0:1, 0:1]),
       [(sBar, 1)])
    bcast_scalar(1, 0, [(sBar, sBar.n)])
    nBar1 = sBar.n

    (nfx_s, nfx_ids) = AB.alloc(DC)

    def nfx_sl(i):
        return nfx_s[i // 2][:, (i % 2) * 1024:(i % 2 + 1) * 1024]

    (qx2_s, qx2_ids) = AF.alloc(1)
    qx2 = [qscr[:, 0:1024], qx2_s[0].bitcast(I32)]
    sTq = sem("sTq")
    sNfx = sem("sNfx")
    qx_tok = {0: 0, 1: 0}
    for i in range(T):
        px = i % 2
        w = [(sBar, nBar1), (sNfx, qx_tok[px])]
        S_(w, (lambda i_=i, p_=px: (lambda s: s.activation(
            qx2[p_], h2_sl(i_), Act.Identity,
            bias=0.0, scale=bsc[:, 0:1])))(),
           [(sTq, 1)])
        V_([(sTq, i + 1)],
           (lambda i_=i, p_=px: (lambda v: v.tensor_copy(nfx_sl(i_), qx2[p_])))(),
           [(sNfx, 1)])
        qx_tok[px] = sNfx.n
    AF.release(qx2_ids)
    (nfxT_s, nfxT_ids) = AB.alloc(DC)
    nfxT = [nfxT_s[dc][:, 0:S] for dc in range(DC)]
    sTr2 = sem("sTr2")
    sNfxT = sem("sNfxT")
    for i in range(T):
        for dc in range(DC):
            pb = (DC * i + dc) % 4
            w = [(sNfx, i + 1), (sem("sIdent"), 2)]
            prev = DC * i + dc - 4
            if prev >= 0:
                w.append((sNfxT, prev + 1))
            T_(w, (lambda i_=i, dc_=dc, pb_=pb: (lambda t: t.matmul(
                bank(pb_, 128), nfx_sl(i_)[:, dc_ * 128:(dc_ + 1) * 128], ident[:, :],
                start=True, stop=True)))(),
               [(sTr2, 1)])
            S_([(sTr2, DC * i + dc + 1)],
               (lambda i_=i, dc_=dc, pb_=pb: (lambda s: s.copy(
                   nfxT[dc_][:, i_ * 128:(i_ + 1) * 128], bank(pb_, 128))))(),
               [(sNfxT, 1)])
    nNfxT = sNfxT.n
    if upto <= 50:
        return _finish([(sNfxT, nNfxT)])
    AB.release(h2_ids)
    AB.release(nfx_ids)
    AF.release(x1r_ids); AF.release(h2f_ids)

    (wfb, wfb_ids) = AB.alloc(1)

    def wf_buf(j):
        return wfb[0][:, (j % 2) * 1024:(j % 2) * 1024 + 1024]

    (upc_s, upc_ids) = AB.alloc(2)
    sMMu = sem("sMMu")
    sUpE = sem("sUpE")
    sUpMin = sem("sUpMin")
    wup_tok = [0] * IC
    upst_tok = [0] * IC

    def wup_load(ic):
        j = ic % 2
        w = [(rStUp, 24 * 16), (sNfxT, nNfxT)]
        if ic < 2:
            w.append((sLqDone, 1))
        if ic >= 2:
            w.append((sMMu, ic - 1))
        src = wup_bd[:, ic * 128:(ic + 1) * 128].rearrange("(c p) n -> p c n", p=128)
        SY(w, (lambda s_=src, j_=j: (lambda h: h.dma_start(
            out=wf_buf(j_).rearrange("p (c n) -> p c n", n=128), in_=s_)))(),
           [(rW[j], 16)])
        wup_tok[ic] = rW[j].n

    for ic in range(2):
        wup_load(ic)
    for ic in range(IC):
        j = ic % 2
        pbase = (ic % 2) * 4
        for kc in range(DC):
            for ns2 in range(NS):
                w = [(rW[j], wup_tok[ic])]
                if kc == 0 and ns2 == 0:
                    w.append((sNfxT, nNfxT))
                    if ic < 2:
                        w.append((sLqE, NPAN))
                    if ic >= 2:
                        w.append((sUpE, ic - 1))
                T_(w, (lambda kc_=kc, ns_=ns2, pb_=pbase, j_=j: (lambda t: t.matmul(
                    bank(pb_ + ns_), wf_buf(j_)[:, kc_ * 128:(kc_ + 1) * 128],
                    nfxT[kc_][:, ns_ * 512:(ns_ + 1) * 512],
                    start=(kc_ == 0), stop=(kc_ == DC - 1))))(),
                   [(sMMu, 1)] if (kc == DC - 1 and ns2 == NS - 1) else [])
        par = ic % 2
        w = [(sMMu, ic + 1)]
        if ic < 2:
            w.append((sLqDone, 1))
        if ic >= 2:
            w.append((rUpSt[par], upst_tok[ic - 2]))
        S_(w, (lambda ic_=ic, p_=par, pb_=pbase: (lambda s: s.copy(
            upc_s[p_][:, 0:S], ps_all[:, pb_ * 512:pb_ * 512 + S])))(),
           [(sUpE, 1)])
        G_([(sUpE, ic + 1)],
           (lambda ic_=ic, p_=par: (lambda g: g.tensor_reduce(
               redrow[0:1, ic_:ic_ + 1],
               upc_s[p_][:, 0:S], mybir.AxisListType.XYZWC, Alu.max, apply_absolute_value=True)))(),
           [(sUpMin, 1)])
        SY([(sUpMin, ic + 1)],
           (lambda ic_=ic, p_=par: (lambda h: h.dma_start(
               out=upraw_d[ic_, :, :], in_=upc_s[p_][:, 0:S])))(),
           [(rUpSt[par], 16)])
        upst_tok[ic] = rUpSt[par].n
        if ic + 2 < IC:
            wup_load(ic + 2)
    nUpSt0 = rUpSt[0].n
    nUpSt1 = rUpSt[1].n
    if upto <= 60:
        return _finish([(rUpSt[0], nUpSt0), (rUpSt[1], nUpSt1)])

    (gate_s, gate_ids) = AB.alloc(IC)

    def gate_sl(ic):
        return gate_s[ic][:, 0:S]

    sMMg = sem("sMMg")
    sGE = sem("sGE")
    sGMin = sem("sGMin")
    wg_tok = []
    for ic in range(IC):
        j = ic % 2
        w = [(rStG, 24 * 16), (sMMu, IC)]
        if ic >= 2:
            w.append((sMMg, ic - 1))
        src = wg_bd[:, ic * 128:(ic + 1) * 128].rearrange("(c p) n -> p c n", p=128)
        SY(w, (lambda s_=src, j_=j: (lambda h: h.dma_start(
            out=wf_buf(j_).rearrange("p (c n) -> p c n", n=128), in_=s_)))(),
           [(rW[j], 16)])
        wg_tok.append(rW[j].n)
    for ic in range(IC):
        j = ic % 2
        pbase = (ic % 2) * 4
        for kc in range(DC):
            for ns2 in range(NS):
                w = [(rW[j], wg_tok[ic])]
                if kc == 0 and ns2 == 0:
                    w.append((sUpE, IC))
                    if ic >= 2:
                        w.append((sGE, ic - 1))
                T_(w, (lambda kc_=kc, ns_=ns2, pb_=pbase, j_=j: (lambda t: t.matmul(
                    bank(pb_ + ns_), wf_buf(j_)[:, kc_ * 128:(kc_ + 1) * 128],
                    nfxT[kc_][:, ns_ * 512:(ns_ + 1) * 512],
                    start=(kc_ == 0), stop=(kc_ == DC - 1))))(),
                   [(sMMg, 1)] if (kc == DC - 1 and ns2 == NS - 1) else [])
        S_([(sMMg, ic + 1)],
           (lambda ic_=ic, pb_=pbase: (lambda s: s.copy(
               gate_sl(ic_), ps_all[:, pb_ * 512:pb_ * 512 + S])))(),
           [(sGE, 1)])
        G_([(sGE, ic + 1)],
           (lambda ic_=ic: (lambda g: g.tensor_reduce(
               redrow[0:1, 24 + ic_:25 + ic_], gate_sl(ic_),
               mybir.AxisListType.XYZWC, Alu.max, apply_absolute_value=True)))(),
           [(sGMin, 1)])
    AB.release(nfxT_ids)

    cross_core_max([(0, IC, 10), (24, 24 + IC, 11)], 1, [(sUpMin, IC), (sGMin, IC)])
    nCCD2 = sCCD.n
    V_([(sCCD, nCCD2)],
       lambda v: v.scalar_tensor_tensor(scal[0:1, 2:3], scal[0:1, 10:11], 1.0 / QMAX, scal[0:1, 0:1], Alu.mult, Alu.mult),
       [(sBar, 1)])
    V_([(sBar, sBar.n)],
       lambda v: v.tensor_scalar(scal[0:1, 2:3], scal[0:1, 2:3], EPS_Q, None, Alu.max), [(sBar, 1)])
    V_([(sCCD, nCCD2)],
       lambda v: v.scalar_tensor_tensor(scal[0:1, 3:4], scal[0:1, 11:12], 1.0 / QMAX, scal[0:1, 0:1], Alu.mult, Alu.mult),
       [(sBar, 1)])
    V_([(sBar, sBar.n)],
       lambda v: v.tensor_scalar(scal[0:1, 3:4], scal[0:1, 3:4], EPS_Q, None, Alu.max), [(sBar, 1)])
    V_([(sBar, sBar.n)],
       lambda v: v.reciprocal(scal[0:1, 4:5], scal[0:1, 3:4]), [(sBar, 1)])
    V_([(sBar, sBar.n)],
       lambda v: v.scalar_tensor_tensor(scal[0:1, 5:6], scal[0:1, 0:1], 1.0, scal[0:1, 4:5], Alu.mult, Alu.mult),
       [(sBar, 1)])
    V_([(sBar, sBar.n)],
       lambda v: v.tensor_scalar(scal[0:1, 6:7], scal[0:1, 3:4], 0.5, None, Alu.mult),
       [(sBar, 1)])
    bcast_scalar(5, 1, [(sBar, sBar.n)])
    bcast_scalar(6, 2, [(sBar, sBar.n)])
    nBar2 = sBar.n
    if upto <= 70:
        return _finish([(sBar, nBar2)])

    tgA, tgA_ids = AF.alloc2()
    tgB, tgB_ids = AF.alloc2()
    (nfs_s, nfs_ids) = AB.alloc(2)
    sTg = sem("sTg")
    sNfg = sem("sNfg")
    sTnh = sem("sTnh")
    sT2 = sem("sT2")
    sNfs = sem("sNfs")
    sClip = sem("sClip")
    sG2 = sem("sG2")
    sG2V = sem("sG2V")
    sG2M = sem("sG2M")
    sQs = sem("sQs")
    NH = max(S // 1024, 1)
    HW_ = min(1024, S)

    def qpass(src_fn, dst_fn, scale_col, extra_w_first=(), extra_inc_last=(),
              pre=None):
        for hh in range(NH):
            lo, hi = hh * HW_, (hh + 1) * HW_
            w = [(sQs, sQs.n)]
            if hh == 0:
                w += list(extra_w_first)
            V_(w, (lambda lo_=lo, hi_=hi, sf=src_fn, sc_=scale_col: (lambda v: v.tensor_tensor(
                qscr[:, 0:hi_ - lo_], sf(lo_, hi_),
                bsc[:, sc_:sc_ + 1].broadcast_to([128, hi_ - lo_]), Alu.mult)))(),
               [(sQs, 1)])
            V_([(sQs, sQs.n)],
               (lambda lo_=lo, hi_=hi, df=dst_fn: (lambda v: v.tensor_copy(
                   df(lo_, hi_), qscr[:, 0:hi_ - lo_])))(),
               [(sQs, 1)] + (list(extra_inc_last) if hh == NH - 1 else []))

    qscr2A, qscr2A_ids = AB.alloc2()
    qscr2B, qscr2B_ids = AB.alloc2()
    qscr2 = [qscr2A.bitcast(I32), qscr2B.bitcast(I32)]
    sQ2 = sem("sQ2")
    q2_tok = {0: 0, 1: 0}

    def qpass_sg(src_fn, dst_fn, scale_col, par2, extra_w_first=(), extra_inc_last=()):
        w = [(sQ2, q2_tok[par2])] + list(extra_w_first)
        S_(w, (lambda sf=src_fn, sc_=scale_col, p_=par2: (lambda s: s.activation(
            qscr2[p_][:, 0:S], sf(0, S), Act.Identity,
            bias=0.0, scale=bsc[:, sc_:sc_ + 1])))(),
           [(sQ2, 1)])
        V_([(sQ2, sQ2.n)],
           (lambda df=dst_fn, p_=par2: (lambda v: v.tensor_copy(
               df(0, S), qscr2[p_][:, 0:S])))(),
           [(sQ2, 1)] + list(extra_inc_last))
        q2_tok[par2] = sQ2.n

    def p8_stage1(ic):
        par = ic % 2
        qpass_sg((lambda lo, hi, ic_=ic: gate_sl(ic_)[:, lo:hi]),
                 (lambda lo, hi, ic_=ic: gate_sl(ic_)[:, lo:hi]),
                 1, par, extra_w_first=[(sGE, IC), (sBar, nBar2)],
                 extra_inc_last=[(sNfg, 1)])

    def p8_stage2(ic):
        par = ic % 2
        tg = tgA if par == 0 else tgB
        wTn = [(sNfg, ic + 1)]
        if ic >= 2:
            wTn.append((sT2, ic - 1))
        S_(wTn,
           (lambda ic_=ic, t_=tg: (lambda s: s.activation(
               t_[:, 0:S], gate_sl(ic_), Act.Tanh, bias=zeroc[:, 0:1], scale=bsc[:, 2:3])))(),
           [(sTnh, 1)])
        wSg = [(sTnh, ic + 1), (sQ2, q2_tok[par])]
        if ic == 0:
            wSg.append((sem("sConst"), 2))
        S_(wSg,
           (lambda t_=tg, p_=par: (lambda s: s.activation(
               qscr2[p_][:, 0:S], t_[:, 0:S], Act.Identity,
               bias=bsc[:, 7:8], scale=32768.0)))(),
           [(sQ2, 1), (sT2, 1)])
        w = [(sQ2, sQ2.n)]
        if ic >= 2:
            w.append((sG2, ic - 1))
        V_(w, (lambda p_=par: (lambda v: v.tensor_scalar(
            nfs_s[p_][:, 0:S], qscr2[p_][:, 0:S], 65535.0, None, Alu.min)))(),
           [(sQ2, 1), (sClip, 1)])
        q2_tok[par] = sQ2.n
        V_([(sClip, ic + 1)],
           (lambda ic_=ic, p_=par: (lambda v: v.tensor_tensor(
               gate_sl(ic_), gate_sl(ic_), nfs_s[p_][:, 0:S], Alu.mult)))(),
           [(sG2, 1)])
        wRm = [(sG2, ic + 1)]
        if ic >= 16:
            wRm.append((sG2M, ic - 15))
        V_(wRm,
           (lambda ic_=ic: (lambda v: v.tensor_reduce(
               bcolc[:, ic_ % 16:ic_ % 16 + 1], gate_sl(ic_), AX, Alu.max,
               apply_absolute_value=True)))(),
           [(sG2V, 1)])
        G_([(sG2V, ic + 1)],
           (lambda ic_=ic: (lambda g: g.tensor_reduce(
               redrow[0:1, 48 + ic_:49 + ic_], bcolc[:, ic_ % 16:ic_ % 16 + 1],
               mybir.AxisListType.XYZWC, Alu.max)))(),
           [(sG2M, 1)])

    for ic in range(IC + 1):
        if ic < IC:
            p8_stage1(ic)
        if ic >= 1:
            p8_stage2(ic - 1)
    AF.release(tgA_ids)
    AF.release(tgB_ids)
    cross_core_max([(48, 48 + IC, 12)], 2, [(sG2M, IC)])
    nCCD3 = sCCD.n
    V_([(sCCD, nCCD3)],
       lambda v: v.scalar_tensor_tensor(scal[0:1, 7:8], scal[0:1, 12:13], 2.0 ** -16 / QMAX, scal[0:1, 3:4], Alu.mult, Alu.mult),
       [(sBar, 1)])
    V_([(sBar, sBar.n)],
       lambda v: v.tensor_scalar(scal[0:1, 7:8], scal[0:1, 7:8], EPS_Q, None, Alu.max), [(sBar, 1)])
    V_([(sBar, sBar.n)],
       lambda v: v.reciprocal(scal[0:1, 16:17], scal[0:1, 7:8]), [(sBar, 1)])
    V_([(sBar, sBar.n)],
       lambda v: v.scalar_tensor_tensor(scal[0:1, 17:18], scal[0:1, 3:4], 2.0 ** -16, scal[0:1, 16:17], Alu.mult, Alu.mult),
       [(sBar, 1)])
    V_([(sBar, sBar.n)],
       lambda v: v.reciprocal(scal[0:1, 18:19], scal[0:1, 2:3]), [(sBar, 1)])
    V_([(sBar, sBar.n)],
       lambda v: v.scalar_tensor_tensor(scal[0:1, 19:20], scal[0:1, 0:1], 1.0, scal[0:1, 18:19], Alu.mult, Alu.mult),
       [(sBar, 1)])
    bcast_scalar(17, 3, [(sBar, sBar.n)])
    bcast_scalar(19, 4, [(sBar, sBar.n)])
    nBar3 = sBar.n
    if upto <= 80:
        return _finish([(sBar, nBar3)])

    AB.release(upc_ids)
    (upr_s, upr_ids) = AB.alloc(3)
    (nfu_s, nfu_ids) = AB.alloc(2)
    sTu = sem("sTu")
    sNfu = sem("sNfu")
    sTg2 = sem("sTg2")
    sNfg2 = sem("sNfg2")
    sO = sem("sO")
    sOV = sem("sOV")
    sOM = sem("sOM")
    upr_tok = []
    for ic in range(IC):
        b = ic % 3
        w = [(rUpSt[0], nUpSt0), (rUpSt[1], nUpSt1), (sMMg, IC)]
        if ic >= 3:
            w.append((sNfu, ic - 2))
        SY(w, (lambda ic_=ic, b_=b: (lambda h: h.dma_start(
            out=upr_s[b_][:, 0:S], in_=upraw_d[ic_, :, :])))(),
           [(rA[b], 16)])
        upr_tok.append(rA[b].n)
    def p9_stage1(ic):
        b = ic % 3
        par = ic % 2
        nld = upr_tok[ic]
        wA1 = [(rA[b], nld), (sBar, nBar3)]
        if ic >= 2:
            wA1.append((sO, ic - 1))
        qpass_sg((lambda lo, hi, b_=b: upr_s[b_][:, lo:hi]),
                 (lambda lo, hi, p_=par: nfu_s[p_][:, lo:hi]),
                 4, par, extra_w_first=wA1, extra_inc_last=[(sNfu, 1)])

    def p9_stage2(ic):
        par = ic % 2
        qpass_sg((lambda lo, hi, ic_=ic: gate_sl(ic_)[:, lo:hi]),
                 (lambda lo, hi, ic_=ic: gate_sl(ic_)[:, lo:hi]),
                 3, par, extra_w_first=[(sG2M, IC), (sBar, nBar3), (sNfu, ic + 1)],
                 extra_inc_last=[(sNfg2, 1)])
        V_([(sNfg2, ic + 1), (sNfu, ic + 1)],
           (lambda ic_=ic, p_=par: (lambda v: v.tensor_tensor(
               gate_sl(ic_), gate_sl(ic_), nfu_s[p_][:, 0:S], Alu.mult)))(),
           [(sO, 1)])
        wRm9 = [(sO, ic + 1)]
        if ic == 0:
            wRm9.append((sG2M, IC))
        if ic >= 16:
            wRm9.append((sOM, ic - 15))
        V_(wRm9,
           (lambda ic_=ic: (lambda v: v.tensor_reduce(
               bcolc[:, ic_ % 16:ic_ % 16 + 1], gate_sl(ic_), AX, Alu.max,
               apply_absolute_value=True)))(),
           [(sOV, 1)])
        G_([(sOV, ic + 1)],
           (lambda ic_=ic: (lambda g: g.tensor_reduce(
               redrow[0:1, 72 + ic_:73 + ic_], bcolc[:, ic_ % 16:ic_ % 16 + 1],
               mybir.AxisListType.XYZWC, Alu.max)))(),
           [(sOM, 1)])

    for ic in range(IC + 1):
        if ic < IC:
            p9_stage1(ic)
        if ic >= 1:
            p9_stage2(ic - 1)
    cross_core_max([(72, 72 + IC, 13)], 3, [(sOM, IC)])
    nCCD4 = sCCD.n
    V_([(sCCD, nCCD4)],
       lambda v: v.scalar_tensor_tensor(scal[0:1, 20:21], scal[0:1, 2:3], 1.0, scal[0:1, 7:8], Alu.mult, Alu.mult),
       [(sBar, 1)])
    V_([(sBar, sBar.n)],
       lambda v: v.scalar_tensor_tensor(scal[0:1, 21:22], scal[0:1, 13:14], 1.0 / QMAX, scal[0:1, 20:21], Alu.mult, Alu.mult),
       [(sBar, 1)])
    V_([(sBar, sBar.n)],
       lambda v: v.tensor_scalar(scal[0:1, 21:22], scal[0:1, 21:22], EPS_Q, None, Alu.max), [(sBar, 1)])
    V_([(sBar, sBar.n)],
       lambda v: v.reciprocal(scal[0:1, 22:23], scal[0:1, 21:22]), [(sBar, 1)])
    V_([(sBar, sBar.n)],
       lambda v: v.scalar_tensor_tensor(scal[0:1, 23:24], scal[0:1, 20:21], 1.0, scal[0:1, 22:23], Alu.mult, Alu.mult),
       [(sBar, 1)])
    bcast_scalar(23, 5, [(sBar, sBar.n)])
    bcast_scalar(21, 6, [(sBar, sBar.n)])
    nBar4 = sBar.n

    sTo = sem("sTo")
    sNfo = sem("sNfo")
    q3A, q3A_ids = AF.alloc2()
    q3B, q3B_ids = AF.alloc2()
    qscr3 = [q3A.bitcast(I32), q3B.bitcast(I32)]
    sQ3 = sem("sQ3")
    q3_tok = {0: 0, 1: 0}
    for ic in range(IC):
        par3 = ic % 2
        S_([(sQ3, q3_tok[par3]), (sBar, nBar4), (sOM, IC)],
           (lambda ic_=ic, p_=par3: (lambda s: s.activation(
               qscr3[p_][:, 0:S], gate_sl(ic_), Act.Identity,
               bias=0.0, scale=bsc[:, 5:6])))(),
           [(sQ3, 1)])
        V_([(sQ3, sQ3.n)],
           (lambda ic_=ic, p_=par3: (lambda v: v.tensor_copy(
               gate_sl(ic_), qscr3[p_][:, 0:S])))(),
           [(sQ3, 1), (sNfo, 1)])
        q3_tok[par3] = sQ3.n
    AB.release(upr_ids); AB.release(nfu_ids)
    AB.release(nfs_ids); AB.release(wfb_ids)
    AB.release(qscr2A_ids); AB.release(qscr2B_ids)
    (wdn_s, wdn_ids) = AB.alloc(12)

    def wdn_sl(ic):
        return wdn_s[ic // 2][:, (ic % 2) * 1024:(ic % 2 + 1) * 1024]

    for ic in range(IC):
        src = wdn_bd[ic * 128:(ic + 1) * 128, :]
        SY([(rStDn, 24 * 16), (sO, IC)],
           (lambda s_=src, ic_=ic: (lambda h: h.dma_start(out=wdn_sl(ic_), in_=s_)))(),
           [(rW[0], 16)])
    nWdnL = rW[0].n

    sMMd = sem("sMMd")
    sYE = sem("sYE")
    sYR = sem("sYR")
    (x1rr_s, x1rr_ids) = AF.alloc(2)
    (yv_s, yv_ids) = AF.alloc(2)
    x1rr_tok = [0] * T
    yst_tok = [0] * T

    def x1rr_load(sc):
        par = sc % 2
        w = [(rX1s[sc % 2], x1st_tok[sc])]
        if sc < 2:
            w.append((sT2, IC))
        if sc >= 2:
            w.append((sYR, 2 * (sc - 1)))
        SY(w, (lambda sc_=sc, p_=par: (lambda h: h.dma_start(
            out=x1rr_s[p_][:, :], in_=x1_d[sc_ * 128:(sc_ + 1) * 128, :])))(),
           [(rX[par], 16)])
        x1rr_tok[sc] = rX[par].n

    for sc in range(2):
        x1rr_load(sc)
    for sc in range(T):
        par = sc % 2
        pb = 4 + 2 * (sc % 2)
        for ic in range(IC):
            for ne in range(2):
                w = [(rW[0], nWdnL), (sNfo, ic + 1)]
                if ic == 0 and ne == 0 and sc >= 2:
                    w.append((sYE, sc - 1))
                T_(w, (lambda sc_=sc, ic_=ic, ne_=ne, pb_=pb: (lambda t: t.matmul(
                    bank(pb_ + ne_), gate_sl(ic_)[:, sc_ * 128:(sc_ + 1) * 128],
                    wdn_sl(ic_)[:, ne_ * 512:(ne_ + 1) * 512],
                    start=(ic_ == 0), stop=(ic_ == IC - 1))))(),
                   [(sMMd, 1)] if (ic == IC - 1 and ne == 1) else [])
        wYE = [(sMMd, sc + 1), (sBar, nBar4)]
        if sc >= 2:
            wYE.append((rYSt[par], yst_tok[sc - 2]))
        S_(wYE,
           (lambda p_=par, pb_=pb: (lambda s: s.activation(
               yv_s[p_][:, :], ps_all[:, pb_ * 512:pb_ * 512 + D], Act.Identity,
               bias=0.0, scale=bsc[:, 6:7])))(),
           [(sYE, 1)])
        w = [(sYE, sc + 1), (sModP, nModP)]
        V_(w, (lambda p_=par: (lambda v: v.tensor_tensor(yv_s[p_][:, :], yv_s[p_][:, :], gbb[:, :], Alu.mult)))(),
           [(sYR, 1)])
        V_([(sYR, 2 * sc + 1), (rX[par], x1rr_tok[sc])],
           (lambda p_=par: (lambda v: v.tensor_tensor(yv_s[p_][:, :], yv_s[p_][:, :], x1rr_s[p_][:, :], Alu.add)))(),
           [(sYR, 1)])
        SY([(sYR, 2 * (sc + 1))],
           (lambda sc_=sc, p_=par: (lambda h: h.dma_start(
               out=y_d[sc_ * 128:(sc_ + 1) * 128, :], in_=yv_s[p_][:, :])))(),
           [(rYSt[par], 16)])
        yst_tok[sc] = rYSt[par].n
        if sc + 2 < T:
            x1rr_load(sc + 2)
    rec.wait("sync", [(rYSt[0], rYSt[0].n), (rYSt[1], rYSt[1].n)])
    rec.wait("gpsimd", [(rYSt[0], rYSt[0].n), (rYSt[1], rYSt[1].n)])

    with nc.Block() as block:
        block.sync(lambda h: [f(h) for f in rec.ops["sync"]])
        block.tensor(lambda h: [f(h) for f in rec.ops["tensor"]])
        block.vector(lambda h: [f(h) for f in rec.ops["vector"]])
        block.scalar(lambda h: [f(h) for f in rec.ops["scalar"]])
        block.gpsimd(lambda h: [f(h) for f in rec.ops["gpsimd"]])
    st.close()
    return nc


def make_in_maps(inputs, S, ncores):
    x = np.ascontiguousarray(np.asarray(inputs["L0_attn_x"], dtype=np.float32))
    wqT = np.ascontiguousarray(np.asarray(inputs["L0_attn_Wq"]).T)
    wkT = np.ascontiguousarray(np.asarray(inputs["L0_attn_Wk"]).T)
    wvT = np.ascontiguousarray(np.asarray(inputs["L0_attn_Wv"]).T)
    woT = np.ascontiguousarray(np.asarray(inputs["L0_attn_Wo"]).T)
    wup_o = np.ascontiguousarray(np.asarray(inputs["L0_ffn_w_up"], dtype=np.float32))
    wg_o = np.ascontiguousarray(np.asarray(inputs["L0_ffn_w_gate"], dtype=np.float32))
    wdn_o = np.ascontiguousarray(np.asarray(inputs["L0_ffn_w_down"], dtype=np.float32))
    wmod = np.ascontiguousarray(np.asarray(inputs["L0_wmod"]))
    c = np.asarray(inputs["L0_c"], dtype=np.float32)
    bq = np.asarray(inputs["L0_attn_bq"], dtype=np.float32)
    bk = np.asarray(inputs["L0_attn_bk"], dtype=np.float32)
    bv = np.asarray(inputs["L0_attn_bv"], dtype=np.float32)
    bo = np.asarray(inputs["L0_attn_bo"], dtype=np.float32)
    bqcol = np.ascontiguousarray(bq.reshape(DC, 128).T)
    bkcol = np.ascontiguousarray(bk.reshape(DC, 128).T)
    maps = []
    for b in range(ncores):
        maps.append({
            "x": np.ascontiguousarray(x[b, :S]),
            "wqT": wqT, "wkT": wkT, "wvT": wvT, "woT": woT,
            "wup_o": wup_o, "wg_o": wg_o, "wdn_o": wdn_o, "wmod": wmod,
            "ccol": np.ascontiguousarray(c[b].reshape(DC, 128).T),
            "bqcol": bqcol, "bkcol": bkcol,
            "bvrow": np.ascontiguousarray(bv.reshape(1, D)),
            "borow": np.ascontiguousarray(bo.reshape(1, D)),
        })
    return maps


def kernel(**inputs):
    S = int(inputs["L0_attn_x"].shape[1])
    ncores = int(inputs["L0_attn_x"].shape[0])
    nc = build_program(S, ncores)
    in_maps = make_in_maps(inputs, S, ncores)
    res = run_bass_kernel_spmd(nc, in_maps, core_ids=list(range(ncores)))
    out = np.stack([np.asarray(res.results[b]["y"]) for b in range(ncores)], axis=0)
    return out.astype(np.float32)
